# version 03
# baseline (speedup 1.0000x reference)
import numpy as np
import ml_dtypes
import concourse.bass as bass
import concourse.mybir as mybir
from concourse.bass_utils import run_bass_kernel_spmd
from contextlib import ExitStack

F32 = mybir.dt.float32
BF16 = mybir.dt.bfloat16
I32 = mybir.dt.int32
AF = mybir.ActivationFunctionType
ALU = mybir.AluOpType
AX = mybir.AxisListType.X
BF = ml_dtypes.bfloat16

D, DIM, DFF, T = 4096, 512, 2048, 1024
NT, NV, NO, NF, ND, NL = 8, 32, 32, 16, 4, 4
DEBUG_DUMPS = False
MAGIC = 12582912.0
RHO_MAX = 0.99999994
PIH = float(np.pi / 2)
TWO_PI = float(2 * np.pi)


def build_program():
    nc = bass.Bass("TRN2", target_bir_lowering=False)
    es = ExitStack()

    dt = nc.dram_tensor
    x_tm_d = dt("x_tm", [T, D], F32, kind="ExternalInput").ap()
    x_fm_d = dt("x_fm", [D, T], F32, kind="ExternalInput").ap()
    winT_d = dt("winT", [2, 128, 16 * DIM], F32, kind="ExternalInput").ap()
    baseT_d = dt("baseT", [NO, 128, NV * 128], F32, kind="ExternalInput").ap()
    atT_d = dt("atT", [2, 128, 16 * 128], F32, kind="ExternalInput").ap()
    b2T_d = dt("b2T", [128, NO * 128], F32, kind="ExternalInput").ap()
    routerT_d = dt("routerT", [128, NV * 16], F32, kind="ExternalInput").ap()
    w1T_d = dt("w1T", [NL, NF, 128, ND * 128], F32, kind="ExternalInput").ap()
    w2T_d = dt("w2T", [NL, 128, NF * DIM], F32, kind="ExternalInput").ap()
    woutT_d = dt("woutT", [NO, 128, ND * 128], F32, kind="ExternalInput").ap()
    binr_d = dt("binr", [1, DIM], F32, kind="ExternalInput").ap()
    b1c_d = dt("b1c", [128, NL * NF], F32, kind="ExternalInput").ap()
    b2r_d = dt("b2r", [1, NL * DIM], F32, kind="ExternalInput").ap()
    basebc_d = dt("basebc", [128, NO], F32, kind="ExternalInput").ap()
    boutc_d = dt("boutc", [128, NO], F32, kind="ExternalInput").ap()
    routbr_d = dt("routbr", [1, 16], F32, kind="ExternalInput").ap()
    emr_d = dt("emr", [1, 16], F32, kind="ExternalInput").ap()
    chvec_d = dt("chvec", [128, 176], F32, kind="ExternalInput").ap()
    iota_d = dt("iota", [128, T], F32, kind="ExternalInput").ap()
    consts_d = dt("consts", [128, 8], F32, kind="ExternalInput").ap()
    meta_d = dt("meta", [128, 16], F32, kind="ExternalInput").ap()
    e16_d = dt("e16", [8, 128], BF16, kind="ExternalInput").ap()
    ident_d = dt("identity", [128, 128], BF16, kind="ExternalInput").ap()
    outT_d = dt("outT", [D, T], F32, kind="ExternalOutput").ap()
    contrib_d = dt("contrib", [128, 8], F32, kind="Internal")
    gathered_d = dt("gathered", [8 * 128, 8], F32, kind="Internal")
    dbg_d = {}
    if DEBUG_DUMPS:
        for nm, shp, dty in [("dxn", [128, NT * D], BF16), ("dh0", [128, NT * DIM], F32),
                             ("dlog", [128, NT * 16], F32), ("dw", [128, NT * 8], BF16),
                             ("dhw", [128, T], BF16), ("dg", [128, NO * T], BF16),
                             ("dpp", [128, 96], F32), ("dgath", [128, 64], F32),
                             ("dh1", [128, NT * DIM], F32), ("dh2", [128, NT * DIM], F32),
                             ("dh3", [128, NT * DIM], F32), ("dh4", [128, NT * DIM], F32),
                             ("du0", [128, 4 * T], F32), ("dyc", [128, T], F32),
                             ("dct", [128, T], F32), ("dst", [128, T], F32),
                             ("dmb", [128, 128], BF16), ("desc", [128, 16], F32),
                             ("dcnt", [128, 16], F32),
                             ("doutk", [128, T], BF16), ("dz1s", [128, 8 * T], BF16)]:
            dbg_d[nm] = dt(nm, shp, dty, kind="ExternalOutput").ap()

    ops = {k: [] for k in ("sync", "scalar", "vector", "tensor", "gpsimd")}
    sems = {}

    def mksem(name):
        sems[name] = [es.enter_context(nc.semaphore(name)), 0]

    for s in ("sX", "sW", "sS", "sO", "sV", "sA", "sT", "sG", "sC", "sWf", "sW2f"):
        mksem(s)
    for i in range(8):
        mksem(f"bd{i}")
        mksem(f"bf{i}")

    def incd(name, n=1):
        sems[name][1] += n
        return (sems[name][0], n)

    def val(name):
        return sems[name][1]

    def E(eng, fn):
        ops[eng].append(fn)

    def W(eng, name, v=None):
        h = sems[name][0]
        vv = sems[name][1] if v is None else v
        if vv > 0:
            E(eng, lambda e, h=h, vv=vv: e.wait_ge(h, vv))

    def dma(eng, out, in_, sem="sW"):
        tgt = incd(sem, 16)
        E(eng, lambda e, out=out, in_=in_, tgt=tgt:
          e.dma_start(out=out, in_=in_).then_inc(*tgt))

    def mk(engine_letter):
        def emitter(fn, sem=None, sem2=None):
            tgt = incd(sem) if sem else None
            def run(e, fn=fn, tgt=tgt):
                ins = fn(e)
                if tgt:
                    ins = ins.then_inc(*tgt)
                return ins
            E(engine_letter, run)
            if sem2:
                tgt2 = incd(sem2)
                E(engine_letter, lambda e, tgt2=tgt2: e.nop().then_inc(*tgt2))
        return emitter

    _mkV = mk("vector")
    AUTOJ = [False]

    def junk():
        E("vector", lambda v: v.tensor_scalar(out=vjunk[:], in0=vjunk[:],
                                              scalar1=1.0, scalar2=None,
                                              op0=ALU.mult))

    def V(fn, sem=None, sem2=None):
        _mkV(fn, sem, sem2)
        if AUTOJ[0]:
            junk()

    A = mk("scalar")
    TE = mk("tensor")
    GP = mk("gpsimd")

    def dump(nm, src_ap, waits=()):
        if not DEBUG_DUMPS:
            return
        for s, v in waits:
            W("scalar", s, v)
        dma("scalar", dbg_d[nm], src_ap, "sS")
        W("vector", "sS", val("sS"))
        W("tensor", "sS", val("sS"))

    sb = nc.alloc_sbuf_tensor
    consts = sb("consts_t", [128, 8], F32)
    chvec = sb("chvec_t", [128, 176], F32)
    meta = sb("meta_t", [128, 16], F32)
    e16 = sb("e16_t", [8, 128], BF16)
    onesr = sb("onesr_t", [1, 512], BF16)
    ones128f = sb("ones128f_t", [1, 128], F32)
    ident = sb("ident_t", [128, 128], BF16)
    binr = sb("binr_t", [1, DIM], BF16)
    b1c = sb("b1c_t", [128, NL * NF], F32)
    b2r = sb("b2r_t", [1, DIM], BF16)
    basebc = sb("basebc_t", [128, NO], F32)
    boutc = sb("boutc_t", [128, NO], F32)
    routbr = sb("routbr_t", [1, 16], BF16)
    emr = sb("emr_t", [1, 16], F32)
    mb = sb("mb_t", [128, 8 * 16], BF16)
    pp = sb("pp_t", [128, 96], F32)
    h = sb("h_t", [128, NT * DIM], F32)
    g = sb("g_t", [128, NO * T], BF16)
    finals = sb("finals_t", [128, 8], F32)
    gath = sb("gath_t", [128, 64], F32)
    gcoef = sb("gcoef_t", [128, 8], F32)
    acar = sb("acar_t", [128, 2], F32)
    statc = sb("statc_t", [128, 16], F32)
    zero128 = sb("zero128_t", [128, 1], F32)
    vjunk = sb("vjunk_t", [128, 16], F32)
    msk8 = sb("msk8_t", [128, 8], F32)
    mskc = sb("mskc_t", [128, 8], F32)
    beta2bf = sb("beta2bf_t", [128, ND], BF16)
    lnlbbf = sb("lnlbbf_t", [128, ND], BF16)
    b1eff = sb("b1eff_t", [128, NF], F32)
    bouteff = sb("bouteff_t", [128, NO], F32)

    pbank = [nc.alloc_psum_tensor(f"pb{i}", [128, 512], F32) for i in range(8)]
    pb = [b.ap() for b in pbank]
    pbbf = [b.ap().bitcast(BF16) for b in pbank]

    hv = h.ap().rearrange("p (i r) -> p i r", i=NT)

    dma("sync", consts[:], consts_d, "sX")
    dma("sync", chvec[:], chvec_d, "sX")
    dma("sync", meta[:], meta_d, "sX")
    dma("sync", b1c[:], b1c_d, "sX")
    dma("sync", basebc[:], basebc_d, "sX")
    dma("sync", boutc[:], boutc_d, "sX")
    dma("sync", emr[:], emr_d, "sX")
    dma("sync", ident[:], ident_d, "sX")
    dma("sync", e16[:], e16_d, "sX")
    dma("gpsimd", binr[:], binr_d, "sW")
    dma("gpsimd", routbr[:], routbr_d, "sW")
    sx_small = val("sX")

    ctxXN = ExitStack()
    xn = ctxXN.enter_context(nc.sbuf_tensor("xn", [128, NT * D], BF16))
    xnv = xn.ap().rearrange("p (i r) -> p i r", i=NT)

    ctxA = ExitStack()
    sbA = lambda *a: ctxA.enter_context(nc.sbuf_tensor(*a))
    prW = sbA("prW", [128, 16], F32)
    prX = sbA("prX", [128, 16], F32)
    prY = sbA("prY", [128, 16], F32)
    prZ = sbA("prZ", [128, 16], F32)
    prS = sbA("prS", [128, 16], F32)
    prM = sbA("prM", [128, 16], I32)
    ones16 = sbA("ones16", [128, 16], F32)
    zeros16 = sbA("zeros16", [128, 16], F32)
    rcol = sbA("rcol", [128, 16], F32)
    prT = sbA("prT", [128, 16], F32)
    prU = sbA("prU", [128, 16], F32)

    W("vector", "sX", sx_small)
    V(lambda v: v.memset(vjunk[:], 1.0))
    AUTOJ[0] = True
    V(lambda v: v.memset(onesr[:], 1.0))
    V(lambda v: v.memset(ones128f[:], 1.0))
    V(lambda v: v.memset(zero128[:], 0.0))
    V(lambda v: v.memset(ones16[:], 1.0))
    V(lambda v: v.memset(zeros16[:], 0.0))

    def chv(q, l):
        off = {"ln1w": 0, "ln1b": 4, "ln2w": 8, "ln2b": 12, "scl": 16,
               "pre": 20, "pim": 24, "lcr": 28, "lci": 32}[q] + l * 40
        return chvec.ap()[:, off:off + 4]

    for l in range(NL):
        V(lambda v, l=l: v.tensor_copy(out=prW[:, l * 4:l * 4 + 4], in_=chv("pre", l)))
        V(lambda v, l=l: v.tensor_copy(out=prX[:, l * 4:l * 4 + 4], in_=chv("pim", l)))
    V(lambda v: v.tensor_mul(out=prY[:], in0=prW[:], in1=prW[:]))
    V(lambda v: v.tensor_mul(out=prZ[:], in0=prX[:], in1=prX[:]))
    V(lambda v: v.tensor_add(out=prY[:], in0=prY[:], in1=prZ[:]))
    V(lambda v: v.tensor_scalar(out=prM[:], in0=prY[:], scalar1=1e-30,
                                scalar2=None, op0=ALU.is_gt), "sV")
    W("scalar", "sV")
    A(lambda a: a.activation(out=rcol[:], in_=prY[:], func=AF.Sqrt), "sA")
    W("vector", "sA")
    V(lambda v: v.tensor_scalar(out=prZ[:], in0=rcol[:], scalar1=1e-30,
                                scalar2=None, op0=ALU.max))
    V(lambda v: v.reciprocal(out=prZ[:], in_=prZ[:]))
    V(lambda v: v.tensor_mul(out=prZ[:], in0=prY[:], in1=prZ[:]))
    V(lambda v: v.tensor_add(out=rcol[:], in0=rcol[:], in1=prZ[:]))
    V(lambda v: v.tensor_scalar(out=rcol[:], in0=rcol[:], scalar1=0.5,
                                scalar2=None, op0=ALU.mult), "sV")
    W("scalar", "sV")
    A(lambda a: a.activation(out=prS[:], in_=rcol[:], func=AF.Tanh), "sA")
    W("vector", "sA")
    V(lambda v: v.tensor_scalar(out=pp[:, 0:16], in0=prS[:], scalar1=RHO_MAX,
                                scalar2=None, op0=ALU.min))
    V(lambda v: v.tensor_scalar(out=prZ[:], in0=pp.ap()[:, 0:16], scalar1=1e-38,
                                scalar2=None, op0=ALU.max), "sV")
    W("scalar", "sV")
    A(lambda a: a.activation(out=pp[:, 16:32], in_=prZ[:], func=AF.Ln), "sA")
    W("vector", "sA")
    V(lambda v: v.tensor_scalar(out=prZ[:], in0=rcol[:], scalar1=1e-30,
                                scalar2=None, op0=ALU.max))
    V(lambda v: v.reciprocal(out=prZ[:], in_=prZ[:]))
    cos1, sin1 = prY, prS
    V(lambda v: v.tensor_mul(out=cos1[:], in0=prW[:], in1=prZ[:]))
    V(lambda v: v.tensor_mul(out=sin1[:], in0=prX[:], in1=prZ[:]))
    V(lambda v: v.select(out=cos1[:], mask=prM[:], on_true=cos1[:], on_false=ones16[:]))
    V(lambda v: v.select(out=sin1[:], mask=prM[:], on_true=sin1[:], on_false=zeros16[:]))
    V(lambda v: v.tensor_scalar(out=prZ[:], in0=prW[:], scalar1=-1.0,
                                scalar2=None, op0=ALU.mult))
    V(lambda v: v.tensor_max(out=prZ[:], in0=prZ[:], in1=prW[:]))
    V(lambda v: v.tensor_add(out=prZ[:], in0=rcol[:], in1=prZ[:]))
    V(lambda v: v.tensor_scalar(out=prZ[:], in0=prZ[:], scalar1=1e-30,
                                scalar2=None, op0=ALU.max))
    V(lambda v: v.reciprocal(out=prZ[:], in_=prZ[:]))
    V(lambda v: v.tensor_mul(out=prZ[:], in0=prX[:], in1=prZ[:]), "sV")
    W("scalar", "sV")
    A(lambda a: a.activation(out=prZ[:], in_=prZ[:], func=AF.Arctan), "sA")
    W("vector", "sA")
    V(lambda v: v.tensor_scalar(out=prT[:], in0=prZ[:],
                                scalar1=float(1.0 / np.pi), scalar2=None,
                                op0=ALU.mult))
    V(lambda v: v.tensor_scalar(out=prU[:], in0=prX[:], scalar1=0.0,
                                scalar2=None, op0=ALU.is_ge))
    V(lambda v: v.tensor_scalar(out=prU[:], in0=prU[:], scalar1=1.0,
                                scalar2=-0.5, op0=ALU.mult, op1=ALU.add))
    V(lambda v: v.tensor_sub(out=prU[:], in0=prU[:], in1=prT[:]))
    V(lambda v: v.tensor_scalar(out=prM[:], in0=prW[:], scalar1=0.0,
                                scalar2=None, op0=ALU.is_lt))
    V(lambda v: v.select(out=pp[:, 32:48], mask=prM[:], on_true=prU[:],
                         on_false=prT[:]))
    for l in range(NL):
        V(lambda v, l=l: v.tensor_copy(out=prW[:, l * 4:l * 4 + 4], in_=chv("lcr", l)))
        V(lambda v, l=l: v.tensor_copy(out=prX[:, l * 4:l * 4 + 4], in_=chv("lci", l)))
    V(lambda v: v.tensor_mul(out=pp[:, 48:64], in0=prW[:], in1=cos1[:]))
    V(lambda v: v.tensor_mul(out=prZ[:], in0=prX[:], in1=sin1[:]))
    V(lambda v: v.tensor_sub(out=pp[:, 48:64], in0=pp.ap()[:, 48:64], in1=prZ[:]))
    V(lambda v: v.tensor_mul(out=pp[:, 64:80], in0=prW[:], in1=sin1[:]))
    V(lambda v: v.tensor_mul(out=prZ[:], in0=prX[:], in1=cos1[:]))
    V(lambda v: v.tensor_add(out=pp[:, 64:80], in0=pp.ap()[:, 64:80], in1=prZ[:]))
    V(lambda v: v.tensor_copy(out=lnlbbf[:], in_=chvec.ap()[:, 164:168]), "sV")

    W("tensor", "sX", sx_small)
    W("tensor", "sV")
    TE(lambda t: t.matmul(pb[7][0:128, 0:16], ones128f.ap()[:], emr.ap()[:],
                          start=True, stop=True), "bd7")
    W("vector", "bd7")
    for e in range(8):
        V(lambda v, e=e: v.tensor_scalar(out=mb[:, e * 16:e * 16 + 16],
                                         in0=pb[7][0:128, 0:16],
                                         scalar1=float(e), scalar2=None,
                                         op0=ALU.is_equal),
          "bf7" if e == 7 else None)
    AUTOJ[0] = False

    xt = [sbA(f"xt{i}", [128, D], F32) for i in range(2)]
    sqs = sbA("sqs", [128, D], F32)
    xnt = sbA("xnt", [128, D], BF16)

    for i in range(NT):
        if i >= 2:
            W("sync", "sV", val("sV"))
        dma("sync", xt[i % 2][:], x_tm_d[i * 128:(i + 1) * 128, :], "sX")
        W("vector", "sX", sx_small + 16 * (i + 1))
        V(lambda v, i=i: v.reduce_sum(out=statc[:, 0:1], in_=xt[i % 2][:], axis=AX), "sV")
        W("scalar", "sV")
        A(lambda a, i=i: a.activation(out=sqs[:], in_=xt[i % 2][:], func=AF.Square,
                                      accum_out=statc[:, 1:2]), "sA")
        W("vector", "sA")
        V(lambda v: v.tensor_scalar(out=statc[:, 2:3], in0=statc[:, 0:1],
                                    scalar1=float(1.0 / D), scalar2=None, op0=ALU.mult))
        V(lambda v: v.tensor_scalar(out=statc[:, 3:4], in0=statc[:, 1:2],
                                    scalar1=float(1.0 / D), scalar2=None, op0=ALU.mult))
        junk()
        V(lambda v: v.scalar_tensor_tensor(out=statc[:, 4:5], in0=statc[:, 2:3],
                                           scalar=statc[:, 2:3], in1=statc[:, 3:4],
                                           op0=ALU.mult, op1=ALU.subtract), "sV")
        W("scalar", "sV")
        A(lambda a: a.activation(out=statc[:, 5:6], in_=statc[:, 4:5], func=AF.Ln,
                                 scale=-1.0, bias=consts.ap()[:, 0:1]))
        A(lambda a: a.activation(out=statc[:, 6:7], in_=statc[:, 5:6], func=AF.Exp,
                                 scale=-0.5), "sA")
        W("vector", "sA")
        if i >= 1:
            W("vector", "bd4", val("bd4"))
        V(lambda v, i=i: v.tensor_scalar(out=xnt[:], in0=xt[i % 2][:],
                                         scalar1=statc[:, 2:3], scalar2=statc[:, 6:7],
                                         op0=ALU.subtract, op1=ALU.mult), "sV")
        W("tensor", "sV")
        for gi in range(4):
            bank = 3 + (gi % 2)
            W("tensor", f"bf{bank}", val(f"bf{bank}"))
            for c8 in range(8):
                c = gi * 8 + c8
                TE(lambda t, c=c, c8=c8, bank=bank:
                   t.transpose(pbbf[bank][:, c8 * 128:(c8 + 1) * 128],
                               xnt.ap()[:, c * 128:(c + 1) * 128], ident.ap()[:]),
                   f"bd{bank}" if c8 == 7 else None)
            W("scalar", f"bd{bank}", val(f"bd{bank}"))
            A(lambda a, i=i, gi=gi, bank=bank:
              a.copy(out=xn.ap()[:, i * D + gi * 1024:i * D + (gi + 1) * 1024],
                     in_=pbbf[bank][:]), f"bf{bank}")
    xn_bf3, xn_bf4 = val("bf3"), val("bf4")
    dump("dxn", xn.ap()[:], [("bf3", xn_bf3), ("bf4", xn_bf4)])
    dump("dpp", pp.ap()[:], [("sV", val("sV"))])
    xn_apply_sv = val("sV")

    def xn_lhs(i, c):
        return xn.ap()[:, i * D + c * 128:i * D + c * 128 + 128]

    def xn_rhs(c, tc):
        return xnv[:, 4 * tc:4 * tc + 4, c * 128:(c + 1) * 128]

    ctxA.close()
    ctxB = ExitStack()
    sbB = lambda *a: ctxB.enter_context(nc.sbuf_tensor(*a))
    winT_sb = sbB("winT_sb", [128, 16 * DIM], BF16)
    atT_sb = sbB("atT_sb", [128, 16 * 128], BF16)
    b2T_sb = sbB("b2T_sb", [128, NO * 128], BF16)
    routT_sb = sbB("routT_sb", [128, NV * 16], BF16)
    bstream = [sbB(f"bstream{i}", [128, NV * 128], BF16) for i in range(2)]
    logits = sbB("logits", [128, NT * 16], F32)
    wsel = sbB("wsel", [128, NT * 8], BF16)
    cnt = sbB("cnt", [128, 16], F32)
    mskB = sbB("mskB", [128, 16], F32)
    cmpa = sbB("cmpa", [128, 256], F32)
    escb = sbB("escb", [128, 16], F32)
    wfm = sbB("wfm", [8, T], BF16)
    wexp_sb = sbB("wexp_sb", [128, T], BF16)
    hwb = sbB("hwb", [128, T], BF16)

    W("gpsimd", "sV", xn_apply_sv)
    W("gpsimd", "sA", val("sA"))
    dma("gpsimd", routT_sb[:], routerT_d, "sW")
    dma("gpsimd", b2T_sb[:], b2T_d, "sW")
    dma("gpsimd", atT_sb[:], atT_d[0], "sW")
    sw_rt = val("sW")

    W("tensor", "sW", sw_rt)
    W("tensor", "bf3", xn_bf3)
    W("tensor", "bf4", xn_bf4)
    for i in range(NT):
        W("tensor", "bf7", val("bf7"))
        for c in range(NV):
            TE(lambda t, i=i, c=c: t.matmul(pb[7][:, 0:16], xn_lhs(i, c),
                                            routT_sb.ap()[:, c * 16:(c + 1) * 16],
                                            start=(c == 0), stop=False))
        TE(lambda t: t.matmul(pb[7][:, 0:16], onesr.ap()[0:1, 0:128], routbr.ap()[:],
                              start=False, stop=True), "bd7")
        W("vector", "bd7", val("bd7"))
        V(lambda v, i=i: v.tensor_copy(out=logits[:, i * 16:(i + 1) * 16],
                                       in_=pb[7][:, 0:16]), "bf7")
    dump("dmb", mb.ap()[:], [("sV", val("sV"))])
    for i in range(NT):
        lg = logits.ap()[:, i * 16:(i + 1) * 16]
        V(lambda v, lg=lg: v.reduce_max(out=statc[:, 8:9], in_=lg, axis=AX,
                                        negate=True), "sV")
        W("scalar", "sV")
        A(lambda a, lg=lg: a.activation(out=escb[:], in_=lg, func=AF.Exp,
                                        bias=statc[:, 8:9], accum_out=statc[:, 9:10]), "sA")
        W("vector", "sA")
        V(lambda v: v.reciprocal(out=statc[:, 10:11], in_=statc[:, 9:10]))
        for j in range(15):
            V(lambda v, j=j: v.tensor_scalar(out=cmpa[:, j * 16:(j + 1) * 16],
                                             in0=escb[:],
                                             scalar1=escb[:, j:j + 1],
                                             scalar2=None, op0=ALU.is_gt))
        junk()
        for j in range(15):
            V(lambda v, j=j: v.reduce_sum(out=cnt[:, j:j + 1],
                                          in_=cmpa.ap()[:, j * 16:(j + 1) * 16],
                                          axis=AX))
        V(lambda v: v.memset(cnt[:, 15:16], 100.0))
        junk()
        if i == 0:
            V(lambda v: v.engine_nop(), "sV")
            dump("desc", escb.ap()[:], [("sV", val("sV"))])
            dump("dcnt", cnt.ap()[:], [("sV", val("sV"))])
        V(lambda v: v.tensor_scalar(out=mskB[:], in0=cnt[:], scalar1=7.5,
                                    scalar2=None, op0=ALU.is_lt))
        junk()
        V(lambda v: v.tensor_mul(out=escb[:], in0=escb[:], in1=mskB[:]))
        junk()
        for e in range(8):
            V(lambda v, e=e: v.tensor_mul(out=cmpa[:, e * 16:(e + 1) * 16],
                                          in0=escb[:],
                                          in1=mb.ap()[:, e * 16:e * 16 + 16]))
        junk()
        for e in range(8):
            V(lambda v, e=e: v.reduce_sum(out=msk8[:, e:e + 1],
                                          in_=cmpa.ap()[:, e * 16:(e + 1) * 16],
                                          axis=AX))
        junk()
        V(lambda v, i=i: v.tensor_scalar(out=wsel[:, i * 8:i * 8 + 8],
                                         in0=msk8.ap()[:],
                                         scalar1=statc[:, 10:11], scalar2=None,
                                         op0=ALU.mult), "sV")
    w_done = val("sV")
    dump("dlog", logits.ap()[:], [("sV", w_done)])
    dump("dw", wsel.ap()[:], [("sV", w_done)])
    W("tensor", "sV", w_done)
    for i in range(NT):
        W("tensor", "bf2", val("bf2"))
        TE(lambda t, i=i: t.transpose(pbbf[2][0:8, 0:128],
                                      wsel.ap()[:, i * 8:(i + 1) * 8], ident.ap()[:]),
           "bd2")
        W("vector", "bd2", val("bd2"))
        V(lambda v, i=i: v.tensor_copy(out=wfm[:, i * 128:(i + 1) * 128],
                                       in_=pbbf[2][0:8, 0:128]), "bf2")
    wfm_sv = val("sV")
    wfm_bf2 = val("bf2")
    for tc in range(2):
        W("tensor", "bf6", val("bf6"))
        for half in range(2):
            if not (tc == 0 and half == 0):
                W("gpsimd", "sT", val("sT"))
                dma("gpsimd", atT_sb[:], atT_d[half], "sW")
                W("tensor", "sW", val("sW"))
            for c16 in range(16):
                cc = half * 16 + c16
                TE(lambda t, c16=c16, cc=cc, tc=tc:
                   t.matmul(pb[6][:], atT_sb.ap()[:, c16 * 128:(c16 + 1) * 128],
                            xn_rhs(cc, tc), start=(cc == 0), stop=(cc == NV - 1)),
                   "sT" if c16 == 15 else None)
        hA_done = val("sT")
        W("tensor", "sV", wfm_sv)
        W("tensor", "bf2", wfm_bf2)
        W("tensor", "bf5", val("bf5"))
        TE(lambda t, tc=tc: t.matmul(pb[5][:], e16.ap()[:],
                                     wfm.ap()[0:8, tc * 512:(tc + 1) * 512],
                                     start=True, stop=True), "bd5")
        W("scalar", "bd5", val("bd5"))
        A(lambda a, tc=tc: a.copy(out=wexp_sb[:, tc * 512:(tc + 1) * 512],
                                  in_=pb[5][:]), "bf5")
        W("vector", "sT", hA_done)
        W("vector", "bf5", val("bf5"))
        V(lambda v, tc=tc: v.tensor_mul(out=hwb[:, tc * 512:(tc + 1) * 512],
                                        in0=pb[6][:],
                                        in1=wexp_sb.ap()[:, tc * 512:(tc + 1) * 512]),
          "bf6")
    hw_sv = val("sV")
    hw_bf6 = val("bf6")
    dump("dhw", hwb.ap()[:], [("bf6", hw_bf6)])

    win_done = [("sT", 0)]
    for half in range(2):
        if half == 1:
            W("gpsimd", *win_done[0])
        dma("gpsimd", winT_sb[:], winT_d[half], "sW")
        W("tensor", "sW", val("sW"))
        for i in range(NT):
            W("tensor", "bf2", val("bf2"))
            for c16 in range(16):
                last = (half == 1 and c16 == 15)
                TE(lambda t, i=i, c16=c16, half=half, last=last:
                   t.matmul(pb[2][:], xn_lhs(i, half * 16 + c16),
                            winT_sb.ap()[:, c16 * DIM:(c16 + 1) * DIM],
                            start=(c16 == 0), stop=last),
                   "bd2" if last else None)
            if half == 0:
                TE(lambda t: t.matmul(pb[2][:], onesr.ap()[0:1, 0:128], binr.ap()[:],
                                      start=False, stop=True),
                   "bd2")
                if i == NT - 1:
                    win_done[0] = ("bd2", val("bd2"))
                W("vector", "bd2", val("bd2"))
                V(lambda v, i=i: v.tensor_copy(out=hv[:, i, :], in_=pb[2][:]), "bf2")
            else:
                W("vector", "bd2", val("bd2"))
                V(lambda v, i=i: v.tensor_add(out=hv[:, i, :], in0=hv[:, i, :],
                                              in1=pb[2][:]), "bf2")
    h_sv = val("sV")
    dump("dh0", h.ap()[:], [("bf2", val("bf2"))])

    W("tensor", "bf6", hw_bf6)
    sw_base0 = val("sW")
    base_done = []
    for oi in range(NO):
        buf = oi % 2
        if oi >= 2:
            W("gpsimd", *base_done[oi - 2])
        dma("gpsimd", bstream[buf][:], baseT_d[oi], "sW")
        W("tensor", "sW", sw_base0 + 16 * (oi + 1))
        for tc in range(2):
            bank = tc
            W("tensor", f"bf{bank}", val(f"bf{bank}"))
            for c in range(NV):
                TE(lambda t, buf=buf, c=c, tc=tc, bank=bank:
                   t.matmul(pb[bank][:], bstream[buf].ap()[:, c * 128:(c + 1) * 128],
                            xn_rhs(c, tc), start=(c == 0), stop=False))
            TE(lambda t, oi=oi, tc=tc, bank=bank:
               t.matmul(pb[bank][:], b2T_sb.ap()[:, oi * 128:(oi + 1) * 128],
                        hwb.ap()[:, tc * 512:(tc + 1) * 512], start=False, stop=True),
               f"bd{bank}")
            if tc == 1:
                base_done.append((f"bd{bank}", val(f"bd{bank}")))
            W("scalar", f"bd{bank}", val(f"bd{bank}"))
            A(lambda a, oi=oi, tc=tc, bank=bank:
              a.activation(out=g[:, oi * T + tc * 512:oi * T + (tc + 1) * 512],
                           in_=pb[bank][:], func=AF.Silu,
                           bias=basebc.ap()[:, oi:oi + 1]), f"bf{bank}")
    g_bf0, g_bf1 = val("bf0"), val("bf1")
    dump("dg", g.ap()[:], [("bf0", g_bf0), ("bf1", g_bf1)])

    ctxB.close()
    ctxXN.close()
    ctxC = ExitStack()
    sbC = lambda *a: ctxC.enter_context(nc.sbuf_tensor(*a))
    iota = sbC("iota_sb", [128, T], F32)
    zt = sbC("zt", [128, DIM], BF16)
    zfm = sbC("zfm", [128, NT * DIM], BF16)
    z2fm = sbC("z2fm", [128, NT * DIM], BF16)
    yc = sbC("yc", [128, T], F32)
    scr = [sbC(f"scr{i}", [128, T], F32) for i in range(3)]
    u0 = sbC("u0", [128, 4 * T], F32)
    v0 = sbC("v0", [128, 4 * T], F32)
    ct = sbC("ct", [128, T], F32)
    st = sbC("st", [128, T], F32)
    rp = sbC("rp", [128, T], BF16)
    z1s = sbC("z1s", [128, 8 * T], BF16)
    w1sb = [sbC(f"w1sb{i}", [128, ND * 128], BF16) for i in range(4)]
    w2sb = sbC("w2sb", [128, NF * DIM], BF16)
    outk = sbC("outk", [128, T], BF16)

    sqs2ap = scr[1].ap()[:, 0:DIM]
    zfmv = zfm.ap().rearrange("p (i r) -> p i r", i=NT)
    z2fmv = z2fm.ap().rearrange("p (i r) -> p i r", i=NT)
    gathv = gath.ap().rearrange("p (cc j) -> p cc j", cc=8)

    W("gpsimd", "bf0", g_bf0)
    W("gpsimd", "bf1", g_bf1)
    W("gpsimd", "sV", val("sV"))
    dma("gpsimd", iota[:], iota_d, "sW")
    dma("gpsimd", w2sb[:], w2T_d[0], "sW")
    sw_w2 = val("sW")

    tlb, wre, wim = scr[0], scr[1], scr[2]
    w1_done = []
    w2_done = [None]

    def emit_ln_stats(src_ap):
        inv_n = float(1.0 / DIM)
        V(lambda v, src_ap=src_ap: v.reduce_sum(out=statc[:, 0:1], in_=src_ap,
                                                axis=AX), "sV")
        W("scalar", "sV")
        A(lambda a, src_ap=src_ap: a.activation(out=sqs2ap, in_=src_ap, func=AF.Square,
                                                accum_out=statc[:, 1:2]), "sA")
        W("vector", "sA")
        V(lambda v: v.tensor_scalar(out=statc[:, 2:3], in0=statc[:, 0:1],
                                    scalar1=inv_n, scalar2=None, op0=ALU.mult))
        V(lambda v: v.tensor_scalar(out=statc[:, 3:4], in0=statc[:, 1:2],
                                    scalar1=inv_n, scalar2=None, op0=ALU.mult))
        junk()
        V(lambda v: v.scalar_tensor_tensor(out=statc[:, 4:5], in0=statc[:, 2:3],
                                           scalar=statc[:, 2:3], in1=statc[:, 3:4],
                                           op0=ALU.mult, op1=ALU.subtract), "sV")
        W("scalar", "sV")
        A(lambda a: a.activation(out=statc[:, 5:6], in_=statc[:, 4:5], func=AF.Ln,
                                 scale=-1.0, bias=consts.ap()[:, 0:1]))
        A(lambda a: a.activation(out=statc[:, 6:7], in_=statc[:, 5:6], func=AF.Exp,
                                 scale=-0.5), "sA")
        W("vector", "sA")

    def emit_ln_to_fm(dstv):
        for i in range(NT):
            emit_ln_stats(hv[:, i, :])
            W("vector", f"bd{3 + ((i + 1) % 2)}", val(f"bd{3 + ((i + 1) % 2)}"))
            V(lambda v, i=i: v.tensor_scalar(out=zt[:], in0=hv[:, i, :],
                                             scalar1=statc[:, 2:3], scalar2=statc[:, 6:7],
                                             op0=ALU.subtract, op1=ALU.mult), "sV")
            bank = 3 + (i % 2)
            W("tensor", "sV")
            W("tensor", f"bf{bank}", val(f"bf{bank}"))
            for c in range(ND):
                TE(lambda t, c=c, bank=bank:
                   t.transpose(pbbf[bank][:, c * 128:(c + 1) * 128],
                               zt.ap()[:, c * 128:(c + 1) * 128], ident.ap()[:]),
                   f"bd{bank}" if c == ND - 1 else None)
            W("scalar", f"bd{bank}", val(f"bd{bank}"))
            A(lambda a, i=i, bank=bank, dstv=dstv:
              a.copy(out=dstv[:, i, :], in_=pbbf[bank][:, 0:512]), f"bf{bank}")

    def emit_tables(pc, with_carrybits=False):
        V(lambda v, pc=pc: v.tensor_scalar(out=statc[:, 11:12],
                                           in0=pp.ap()[:, 32 + pc:33 + pc],
                                           scalar1=meta.ap()[:, 0:1], scalar2=None,
                                           op0=ALU.mult))
        junk()
        V(lambda v, pc=pc: v.tensor_scalar(out=tlb[:], in0=iota[:],
                                           scalar1=pp.ap()[:, 32 + pc:33 + pc],
                                           scalar2=statc[:, 11:12],
                                           op0=ALU.mult, op1=ALU.add))
        V(lambda v: v.tensor_scalar(out=wre[:], in0=tlb[:], scalar1=MAGIC,
                                    scalar2=-MAGIC, op0=ALU.add, op1=ALU.add))
        V(lambda v: v.tensor_sub(out=wim[:], in0=tlb[:], in1=wre[:]), "sV")
        W("scalar", "sV")
        A(lambda a: a.activation(out=st[:], in_=wim[:], func=AF.Sin, scale=TWO_PI), "sA")
        V(lambda v: v.tensor_scalar(out=wre[:], in0=tlb[:], scalar1=0.25,
                                    scalar2=MAGIC, op0=ALU.add, op1=ALU.add))
        V(lambda v: v.tensor_scalar(out=wre[:], in0=wre[:], scalar1=-MAGIC,
                                    scalar2=None, op0=ALU.add))
        V(lambda v: v.tensor_sub(out=wim[:], in0=tlb[:], in1=wre[:]), "sV")
        W("scalar", "sV")
        A(lambda a: a.activation(out=ct[:], in_=wim[:], func=AF.Sin,
                                 scale=TWO_PI, bias=consts.ap()[:, 1:2]), "sA")
        W("vector", "sA")

    sw_w2_cur = [sw_w2]
    for l in range(NL):
        lb = l * 40
        emit_ln_to_fm(zfmv)

        W("vector", "bf3", val("bf3"))
        W("vector", "bf4", val("bf4"))
        for c in range(4):
            pc = l * 4 + c
            V(lambda v, c=c, lb=lb: v.tensor_scalar(
                out=yc[:], in0=zfmv[:, 0:NT, c * 128:(c + 1) * 128],
                scalar1=chvec.ap()[:, lb + 0 + c:lb + 1 + c],
                scalar2=chvec.ap()[:, lb + 4 + c:lb + 5 + c],
                op0=ALU.mult, op1=ALU.add), "sV")
            W("scalar", "sV")
            A(lambda a: a.activation(out=yc[:], in_=yc[:], func=AF.Silu), "sA")
            if l == 0 and c == 0:
                dump("dyc", yc.ap()[:], [("sA", val("sA"))])
            emit_tables(pc)
            if l == 0 and c == 0:
                dump("dct", ct.ap()[:], [("sA", val("sA"))])
                dump("dst", st.ap()[:], [("sA", val("sA"))])
            V(lambda v: v.tensor_mul(out=wre[:], in0=ct[:], in1=yc[:]))
            V(lambda v: v.tensor_mul(out=wim[:], in0=st[:], in1=yc[:]))
            V(lambda v: v.tensor_scalar(out=wim[:], in0=wim[:], scalar1=-1.0,
                                        scalar2=None, op0=ALU.mult))
            rhoap = pp.ap()[:, 0 + pc:1 + pc]
            for seg in range(2):
                s0, s1 = seg * 512, (seg + 1) * 512
                V(lambda v, c=c, s0=s0, s1=s1, seg=seg, rhoap=rhoap:
                  v.tensor_tensor_scan(out=u0.ap()[:, c * T + s0:c * T + s1],
                                       data0=rhoap.broadcast_to([128, 512]),
                                       data1=wre.ap()[:, s0:s1],
                                       initial=(zero128.ap()[:] if seg == 0 else
                                                u0.ap()[:, c * T + 511:c * T + 512]),
                                       op0=ALU.mult, op1=ALU.add))
                V(lambda v, c=c, s0=s0, s1=s1, seg=seg, rhoap=rhoap:
                  v.tensor_tensor_scan(out=v0.ap()[:, c * T + s0:c * T + s1],
                                       data0=rhoap.broadcast_to([128, 512]),
                                       data1=wim.ap()[:, s0:s1],
                                       initial=(zero128.ap()[:] if seg == 0 else
                                                v0.ap()[:, c * T + 511:c * T + 512]),
                                       op0=ALU.mult, op1=ALU.add))
            V(lambda v, c=c: v.tensor_copy(out=finals[:, c:c + 1],
                                           in_=u0.ap()[:, c * T + 1023:c * T + 1024]))
            V(lambda v, c=c: v.tensor_copy(out=finals[:, 4 + c:5 + c],
                                           in_=v0.ap()[:, c * T + 1023:c * T + 1024]),
              "sV")
        scans_sv = val("sV")
        if l == 0:
            dump("du0", u0.ap()[:], [("sV", scans_sv)])

        W("gpsimd", "sV", scans_sv)
        dma("gpsimd", contrib_d.ap(), finals[:], "sW")
        W("gpsimd", "sW", val("sW"))
        tgt = incd("sC")
        E("gpsimd", lambda gp, tgt=tgt: gp.collective_compute(
            "AllGather", ALU.bypass, replica_groups=[[0, 1, 2, 3, 4, 5, 6, 7]],
            ins=[contrib_d.ap().opt()], outs=[gathered_d.ap().opt()]).then_inc(*tgt))
        W("gpsimd", "sC", val("sC"))
        dma("gpsimd", gath.ap().rearrange("p (c j) -> p c j", c=8),
            gathered_d.ap().rearrange("(c p) j -> p c j", p=128), "sW")
        gath_w = val("sW")
        if l == 0:
            dump("dgath", gath.ap()[:], [("sW", gath_w)])

        W("vector", "sW", gath_w)
        W("scalar", "sW", gath_w)
        for c in range(4):
            pc = l * 4 + c
            lnr = pp.ap()[:, 16 + pc:17 + pc]
            for j in range(8):
                A(lambda a, j=j, lnr=lnr: a.activation(out=gcoef[:, j:j + 1], in_=lnr,
                                                       func=AF.Exp,
                                                       scale=meta.ap()[:, 1 + j:2 + j]))
            A(lambda a, lnr=lnr: a.activation(out=statc[:, 12:13], in_=lnr, func=AF.Exp,
                                              scale=meta.ap()[:, 9:10]), "sA")
            W("vector", "sA")
            V(lambda v, c=c: v.tensor_mul(out=msk8[:], in0=gcoef[:],
                                          in1=gathv[:, :, c]))
            V(lambda v, c=c: v.tensor_mul(out=mskc[:], in0=gcoef[:],
                                          in1=gathv[:, :, 4 + c]))
            junk()
            V(lambda v: v.reduce_sum(out=acar[:, 0:1], in_=msk8.ap()[:], axis=AX))
            V(lambda v: v.reduce_sum(out=acar[:, 1:2], in_=mskc.ap()[:], axis=AX))
            junk()
            V(lambda v, pc=pc: v.scalar_tensor_tensor(out=acar[:, 0:1],
                                                      in0=statc[:, 12:13],
                                                      scalar=pp.ap()[:, 48 + pc:49 + pc],
                                                      in1=acar.ap()[:, 0:1],
                                                      op0=ALU.mult, op1=ALU.add))
            V(lambda v, pc=pc: v.scalar_tensor_tensor(out=acar[:, 1:2],
                                                      in0=statc[:, 12:13],
                                                      scalar=pp.ap()[:, 64 + pc:65 + pc],
                                                      in1=acar.ap()[:, 1:2],
                                                      op0=ALU.mult, op1=ALU.add))
            junk()
            V(lambda v, lnr=lnr: v.tensor_scalar(out=tlb[:], in0=iota[:], scalar1=lnr,
                                                 scalar2=lnr, op0=ALU.mult, op1=ALU.add),
              "sV")
            W("scalar", "sV")
            A(lambda a: a.activation(out=rp[:], in_=tlb[:], func=AF.Exp), "sA")
            W("vector", "sA")
            emit_tables(pc)
            V(lambda v, c=c: v.scalar_tensor_tensor(out=wre[:], in0=rp[:],
                                                    scalar=acar[:, 0:1],
                                                    in1=u0.ap()[:, c * T:(c + 1) * T],
                                                    op0=ALU.mult, op1=ALU.add))
            V(lambda v, c=c: v.scalar_tensor_tensor(out=wim[:], in0=rp[:],
                                                    scalar=acar[:, 1:2],
                                                    in1=v0.ap()[:, c * T:(c + 1) * T],
                                                    op0=ALU.mult, op1=ALU.add))
            V(lambda v: v.tensor_mul(out=wre[:], in0=ct[:], in1=wre[:]))
            V(lambda v: v.tensor_mul(out=wim[:], in0=st[:], in1=wim[:]))
            V(lambda v: v.tensor_sub(out=tlb[:], in0=wre[:], in1=wim[:]))
            V(lambda v, lb=lb, c=c: v.tensor_scalar(
                out=outk[:], in0=tlb[:],
                scalar1=chvec.ap()[:, lb + 16 + c:lb + 17 + c],
                scalar2=None, op0=ALU.mult), "sV")
            if l == 0 and c == 0:
                dump("doutk", outk.ap()[:], [("sV", val("sV"))])
            W("tensor", "sV", val("sV"))
            for grp in range(2):
                bank = 3 + grp
                W("tensor", f"bf{bank}", val(f"bf{bank}"))
                for i4 in range(4):
                    i = grp * 4 + i4
                    TE(lambda t, c=c, i=i, i4=i4, bank=bank:
                       t.transpose(pbbf[bank][:, i4 * 128:(i4 + 1) * 128],
                                   outk.ap()[:, i * 128:(i + 1) * 128],
                                   ident.ap()[:]), f"bd{bank}" if i4 == 3 else None)
                W("vector", f"bd{bank}", val(f"bd{bank}"))
                V(lambda v, c=c, grp=grp, bank=bank:
                  v.tensor_add(out=hv[:, grp * 4:grp * 4 + 4, c * 128:(c + 1) * 128],
                               in0=hv[:, grp * 4:grp * 4 + 4, c * 128:(c + 1) * 128],
                               in1=pbbf[bank].rearrange("p (i j) -> p i j", i=8)[:, 0:4, :]),
                  f"bf{bank}")
        hupd_sv = val("sV")
        dump(f"dh{l + 1}", h.ap()[:], [("sV", hupd_sv), ("bf3", val("bf3")),
                                       ("bf4", val("bf4"))])

        if w2_done[0] is not None:
            W("gpsimd", *w2_done[0])
        dma("gpsimd", b2r[:], b2r_d[:, l * DIM:(l + 1) * DIM], "sW")
        sw_b2r = val("sW")
        W("vector", "sV", hupd_sv)
        emit_ln_to_fm(z2fmv)
        z2_bf3, z2_bf4 = val("bf3"), val("bf4")
        W("tensor", "bf3", z2_bf3)
        W("tensor", "bf4", z2_bf4)
        V(lambda v, l=l: v.tensor_copy(out=beta2bf[:], in_=chv("ln2b", l)), "sV")
        b2bf_sv = val("sV")

        for half in range(2):
            for f8 in range(8):
                fi = half * 8 + f8
                slot = fi % 4
                kglob = l * NF + fi
                if kglob >= 4:
                    W("gpsimd", *w1_done[kglob - 4])
                dma("gpsimd", w1sb[slot][:], w1T_d[l, fi], "sW")
                sw_f = val("sW")
                W("tensor", "sW", sw_f)
                W("tensor", "sV", b2bf_sv)
                if fi == 0:
                    W("tensor", "bf7", val("bf7"))
                for d in range(ND):
                    TE(lambda t, slot=slot, d=d, fi=fi:
                       t.matmul(pb[7][:, fi:fi + 1],
                                w1sb[slot].ap()[:, d * 128:(d + 1) * 128],
                                beta2bf.ap()[:, d:d + 1],
                                start=(d == 0), stop=(d == ND - 1),
                                skip_group_check=True),
                       "bd7" if d == ND - 1 else None)
                W("vector", "bd7", val("bd7"))
                V(lambda v, fi=fi, l=l: v.tensor_add(
                    out=b1eff[:, fi:fi + 1],
                    in0=b1c.ap()[:, l * NF + fi:l * NF + fi + 1],
                    in1=pb[7][:, fi:fi + 1]), "bf7" if fi == NF - 1 else None)
                for d in range(ND):
                    V(lambda v, slot=slot, d=d, lb=lb: v.tensor_scalar(
                        out=w1sb[slot].ap()[:, d * 128:(d + 1) * 128],
                        in0=w1sb[slot].ap()[:, d * 128:(d + 1) * 128],
                        scalar1=chvec.ap()[:, lb + 8 + d:lb + 9 + d],
                        scalar2=None, op0=ALU.mult),
                      "sV" if d == ND - 1 else None)
                fold_sv = val("sV")
                W("tensor", "sV", fold_sv)
                for tc in range(2):
                    bank = 5 + tc
                    W("tensor", f"bf{bank}", val(f"bf{bank}"))
                    for d in range(ND):
                        TE(lambda t, slot=slot, d=d, tc=tc, bank=bank:
                           t.matmul(pb[bank][:],
                                    w1sb[slot].ap()[:, d * 128:(d + 1) * 128],
                                    z2fmv[:, 4 * tc:4 * tc + 4, d * 128:(d + 1) * 128],
                                    start=(d == 0), stop=(d == ND - 1)),
                           f"bd{bank}" if d == ND - 1 else None)
                    W("scalar", f"bd{bank}", val(f"bd{bank}"))
                    W("scalar", "sV", fold_sv)
                    A(lambda a, f8=f8, tc=tc, bank=bank, fi=fi:
                      a.activation(out=z1s[:, f8 * T + tc * 512:f8 * T + (tc + 1) * 512],
                                   in_=pb[bank][:], func=AF.Silu,
                                   bias=b1eff.ap()[:, fi:fi + 1]), f"bf{bank}")
                w1_done.append(("bd6", val("bd6")))
            W("tensor", "sW", sw_w2_cur[0])
            W("tensor", "sW", sw_b2r)
            W("tensor", "bf5", val("bf5"))
            W("tensor", "bf6", val("bf6"))
            for i in range(NT):
                W("tensor", "bf2", val("bf2"))
                for f8 in range(8):
                    last = (half == 0 and f8 == 7)
                    TE(lambda t, i=i, f8=f8, half=half, last=last:
                       t.matmul(pb[2][:],
                                z1s.ap()[:, f8 * T + i * 128:f8 * T + (i + 1) * 128],
                                w2sb.ap()[:, (half * 8 + f8) * DIM:(half * 8 + f8 + 1) * DIM],
                                start=(f8 == 0), stop=last),
                       "bd2" if last else None)
                if half == 1:
                    TE(lambda t: t.matmul(pb[2][:], onesr.ap()[0:1, 0:128], b2r.ap()[:],
                                          start=False, stop=True),
                       "bd2")
                    if i == NT - 1:
                        w2_done[0] = ("bd2", val("bd2"))
                W("vector", "bd2", val("bd2"))
                V(lambda v, i=i: v.tensor_add(out=hv[:, i, :], in0=hv[:, i, :],
                                              in1=pb[2][:]), "bf2")
        if l + 1 < NL:
            W("gpsimd", *w2_done[0])
            dma("gpsimd", w2sb[:], w2T_d[l + 1], "sW")
            sw_w2_cur[0] = val("sW")

    z3fmv = z2fmv
    emit_ln_to_fm(z3fmv)
    W("tensor", "bf3", val("bf3"))
    W("tensor", "bf4", val("bf4"))
    xbuf = yc
    obuf = scr[0]

    wout_done = []
    for vi in range(NO):
        slot = vi % 4
        if vi < 4:
            W("gpsimd", *w1_done[(NL - 1) * NF + 12 + vi])
        else:
            W("gpsimd", *wout_done[vi - 4])
        dma("gpsimd", w1sb[slot][:], woutT_d[vi], "sW")
        sw_v = val("sW")
        W("tensor", "sW", sw_v)
        if vi == 0:
            W("tensor", "bf7", val("bf7"))
        for d in range(ND):
            TE(lambda t, slot=slot, d=d, vi=vi:
               t.matmul(pb[7][:, vi:vi + 1], w1sb[slot].ap()[:, d * 128:(d + 1) * 128],
                        lnlbbf.ap()[:, d:d + 1], start=(d == 0), stop=(d == ND - 1),
                        skip_group_check=True),
               "bd7" if d == ND - 1 else None)
        W("vector", "bd7", val("bd7"))
        V(lambda v, vi=vi: v.tensor_add(out=bouteff[:, vi:vi + 1],
                                        in0=boutc.ap()[:, vi:vi + 1],
                                        in1=pb[7][:, vi:vi + 1]))
        for d in range(ND):
            V(lambda v, slot=slot, d=d: v.tensor_scalar(
                out=w1sb[slot].ap()[:, d * 128:(d + 1) * 128],
                in0=w1sb[slot].ap()[:, d * 128:(d + 1) * 128],
                scalar1=chvec.ap()[:, 160 + d:161 + d],
                scalar2=None, op0=ALU.mult),
              "sV" if d == ND - 1 else None)
        fold_sv = val("sV")
        W("tensor", "sV", fold_sv)
        for tc in range(2):
            bank = tc
            W("scalar", "sV", val("sV"))
            dma("scalar", xbuf.ap()[:, tc * 512:(tc + 1) * 512],
                x_fm_d[vi * 128:(vi + 1) * 128, tc * 512:(tc + 1) * 512], "sS")
            sw_x = val("sS")
            W("tensor", f"bf{bank}", val(f"bf{bank}"))
            for d in range(ND):
                TE(lambda t, slot=slot, d=d, tc=tc, bank=bank:
                   t.matmul(pb[bank][:], w1sb[slot].ap()[:, d * 128:(d + 1) * 128],
                            z3fmv[:, 4 * tc:4 * tc + 4, d * 128:(d + 1) * 128],
                            start=(d == 0), stop=(d == ND - 1)),
                   f"bd{bank}" if d == ND - 1 else None)
            if tc == 1:
                wout_done.append((f"bd{bank}", val(f"bd{bank}")))
            W("vector", f"bd{bank}", val(f"bd{bank}"))
            W("vector", "sS", sw_x)
            W("vector", "sO", max(0, val("sO") - 16))
            V(lambda v, vi=vi, tc=tc, bank=bank:
              v.scalar_tensor_tensor(out=obuf.ap()[:, tc * 512:(tc + 1) * 512],
                                     in0=pb[bank][:], scalar=bouteff[:, vi:vi + 1],
                                     in1=g.ap()[:, vi * T + tc * 512:vi * T + (tc + 1) * 512],
                                     op0=ALU.add, op1=ALU.mult), f"bf{bank}")
            V(lambda v, tc=tc: v.tensor_add(out=obuf.ap()[:, tc * 512:(tc + 1) * 512],
                                            in0=obuf.ap()[:, tc * 512:(tc + 1) * 512],
                                            in1=xbuf.ap()[:, tc * 512:(tc + 1) * 512]),
              "sV")
            W("scalar", "sV", val("sV"))
            dma("scalar", outT_d[vi * 128:(vi + 1) * 128, tc * 512:(tc + 1) * 512],
                obuf.ap()[:, tc * 512:(tc + 1) * 512], "sO")

    W("gpsimd", "sO", val("sO"))

    with nc.Block() as block:
        @block.sync
        def _(sync):
            for f in ops["sync"]:
                f(sync)

        @block.scalar
        def _(scalar):
            for f in ops["scalar"]:
                f(scalar)

        @block.vector
        def _(vector):
            for f in ops["vector"]:
                f(vector)

        @block.tensor
        def _(tensor):
            for f in ops["tensor"]:
                f(tensor)

        @block.gpsimd
        def _(gpsimd):
            for f in ops["gpsimd"]:
                f(gpsimd)

    es.close()
    return nc


_NC_CACHE = {}


def _host_layouts(inputs):
    f32 = np.float32
    W1 = inputs["L0_attn_W1"].astype(f32)
    W2 = inputs["L0_attn_W2"].astype(f32)
    Win = inputs["L0_attn_Win"].astype(f32)
    Wout = inputs["L0_attn_Wout"].astype(f32)
    baseW = inputs["L0_ffn_base_W"].astype(f32)
    routW = inputs["L0_ffn_router_W"].astype(f32)
    Am = inputs["L0_ffn_A"].astype(f32)
    Bm = inputs["L0_ffn_B"].astype(f32)

    c = np.ascontiguousarray
    out = {}
    WinT = Win.T.reshape(2, 16, 128, 512)
    out["winT"] = c(WinT.transpose(0, 2, 1, 3).reshape(2, 128, 16 * 512))
    BT = baseW.T.reshape(NV, 128, NO, 128)
    out["baseT"] = c(BT.transpose(2, 1, 0, 3).reshape(NO, 128, NV * 128))
    AT = Am.reshape(128, D).T.reshape(2, 16, 128, 128)
    out["atT"] = c(AT.transpose(0, 2, 1, 3).reshape(2, 128, 16 * 128))
    out["b2T"] = c(Bm.transpose(0, 2, 1).reshape(128, D))
    RT = np.zeros((D, 16), f32)
    RT[:, :15] = routW.T
    out["routerT"] = c(RT.reshape(NV, 128, 16).transpose(1, 0, 2).reshape(128, NV * 16))
    W1T = W1.transpose(0, 2, 1).reshape(NL, ND, 128, NF, 128)
    out["w1T"] = c(W1T.transpose(0, 3, 2, 1, 4).reshape(NL, NF, 128, ND * 128))
    W2T = W2.transpose(0, 2, 1).reshape(NL, NF, 128, DIM)
    out["w2T"] = c(W2T.transpose(0, 2, 1, 3).reshape(NL, 128, NF * DIM))
    WoT = Wout.T.reshape(ND, 128, NO, 128)
    out["woutT"] = c(WoT.transpose(2, 1, 0, 3).reshape(NO, 128, ND * 128))
    out["binr"] = c(inputs["L0_attn_bin"].astype(f32).reshape(1, DIM))
    out["b1c"] = c(inputs["L0_attn_b1"].astype(f32).reshape(NL, NF, 128)
                   .transpose(2, 0, 1).reshape(128, NL * NF))
    out["b2r"] = c(inputs["L0_attn_b2"].astype(f32).reshape(1, NL * DIM))
    out["basebc"] = c(inputs["L0_ffn_base_b"].astype(f32).reshape(NO, 128).T)
    out["boutc"] = c(inputs["L0_attn_bout"].astype(f32).reshape(NO, 128).T)
    rb = np.full((1, 16), -1e30, f32)
    rb[0, :15] = inputs["L0_ffn_router_b"].astype(f32)
    out["routbr"] = rb
    em = np.full((1, 16), -1.0, f32)
    em[0, :15] = inputs["L0_ffn_expert_map"].astype(f32)
    out["emr"] = em
    ch = np.zeros((128, 176), f32)
    packs = [("ln1w", inputs["L0_attn_ln1_w"]), ("ln1b", inputs["L0_attn_ln1_b"]),
             ("ln2w", inputs["L0_attn_ln2_w"]), ("ln2b", inputs["L0_attn_ln2_b"]),
             ("scl", inputs["L0_attn_sc_lin"]), ("pre", inputs["L0_attn_phz_re"]),
             ("pim", inputs["L0_attn_phz_im"]), ("lcr", inputs["L0_attn_lc_re"]),
             ("lci", inputs["L0_attn_lc_im"])]
    offs = {"ln1w": 0, "ln1b": 4, "ln2w": 8, "ln2b": 12, "scl": 16,
            "pre": 20, "pim": 24, "lcr": 28, "lci": 32}
    for l in range(NL):
        for name, arr in packs:
            a4 = np.asarray(arr, f32)[l].reshape(4, 128).T
            ch[:, l * 40 + offs[name]:l * 40 + offs[name] + 4] = a4
    ch[:, 160:164] = np.asarray(inputs["L0_attn_lnl_w"], f32).reshape(4, 128).T
    ch[:, 164:168] = np.asarray(inputs["L0_attn_lnl_b"], f32).reshape(4, 128).T
    out["chvec"] = c(ch)
    out["iota"] = c(np.broadcast_to(np.arange(T, dtype=f32)[None, :], (128, T)))
    cst = np.zeros((128, 8), f32)
    cst[:, 0] = 1e-5
    cst[:, 1] = PIH
    out["consts"] = cst
    out["e16"] = (2.0 * (np.arange(128)[None, :] // 16 ==
                         np.arange(8)[:, None])).astype(BF)
    out["identity"] = np.eye(128).astype(BF)
    return out


def kernel(**inputs):
    shared = _host_layouts(inputs)
    x = np.asarray(inputs["L0_attn_x"], np.float32)
    in_maps = []
    for core in range(8):
        b, q = core // 4, core % 4
        xs = np.ascontiguousarray(x[b, q * T:(q + 1) * T, :])
        xfs = np.ascontiguousarray(xs.T)
        mt = np.zeros((128, 16), np.float32)
        mt[:, 0] = 1024.0 * q
        for j in range(8):
            jb, jq = j // 4, j % 4
            if jb == b and jq < q:
                mt[:, 1 + j] = 1024.0 * (q - 1 - jq)
            else:
                mt[:, 1 + j] = 1e30
        mt[:, 9] = 1024.0 * q
        m = dict(shared)
        m["x_tm"] = xs
        m["x_fm"] = xfs
        m["meta"] = mt
        in_maps.append(m)

    if "nc" not in _NC_CACHE:
        _NC_CACHE["nc"] = build_program()
    nc = _NC_CACHE["nc"]
    res = run_bass_kernel_spmd(nc, in_maps, core_ids=list(range(8)))
    _NC_CACHE["res"] = res
    out = np.empty((2, 4096, D), np.float32)
    for core in range(8):
        b, q = core // 4, core % 4
        out[b, q * T:(q + 1) * T, :] = np.asarray(res.results[core]["outT"]).T
    return out


if __name__ == "__main__":
    nc = build_program()
    print("build ok:", nc.n_instructions(), "instructions")


# version 04
# speedup vs baseline: 1.0487x; 1.0487x over previous
import numpy as np
import ml_dtypes
import concourse.bass as bass
import concourse.mybir as mybir
from concourse.bass_utils import run_bass_kernel_spmd
from contextlib import ExitStack

F32 = mybir.dt.float32
BF16 = mybir.dt.bfloat16
I32 = mybir.dt.int32
AF = mybir.ActivationFunctionType
ALU = mybir.AluOpType
AX = mybir.AxisListType.X
BF = ml_dtypes.bfloat16

D, DIM, DFF, T = 4096, 512, 2048, 1024
NT, NV, NO, NF, ND, NL = 8, 32, 32, 16, 4, 4
DEBUG_DUMPS = False
MAGIC = 12582912.0
RHO_MAX = 0.99999994
PIH = float(np.pi / 2)
TWO_PI = float(2 * np.pi)


def build_program():
    nc = bass.Bass("TRN2", target_bir_lowering=False)
    es = ExitStack()

    dt = nc.dram_tensor
    x_tm_d = dt("x_tm", [T, D], F32, kind="ExternalInput").ap()
    x_fm_d = dt("x_fm", [D, T], F32, kind="ExternalInput").ap()
    winT_d = dt("winT", [2, 128, 16 * DIM], F32, kind="ExternalInput").ap()
    baseT_d = dt("baseT", [NO, 128, NV * 128], F32, kind="ExternalInput").ap()
    atT_d = dt("atT", [2, 128, 16 * 128], F32, kind="ExternalInput").ap()
    b2T_d = dt("b2T", [128, NO * 128], F32, kind="ExternalInput").ap()
    routerT_d = dt("routerT", [128, NV * 16], F32, kind="ExternalInput").ap()
    w1T_d = dt("w1T", [NL, NF, 128, ND * 128], F32, kind="ExternalInput").ap()
    w2T_d = dt("w2T", [NL, 128, NF * DIM], F32, kind="ExternalInput").ap()
    woutT_d = dt("woutT", [NO, 128, ND * 128], F32, kind="ExternalInput").ap()
    binr_d = dt("binr", [1, DIM], F32, kind="ExternalInput").ap()
    b1c_d = dt("b1c", [128, NL * NF], F32, kind="ExternalInput").ap()
    b2r_d = dt("b2r", [1, NL * DIM], F32, kind="ExternalInput").ap()
    basebc_d = dt("basebc", [128, NO], F32, kind="ExternalInput").ap()
    boutc_d = dt("boutc", [128, NO], F32, kind="ExternalInput").ap()
    routbr_d = dt("routbr", [1, 16], F32, kind="ExternalInput").ap()
    emr_d = dt("emr", [1, 16], F32, kind="ExternalInput").ap()
    chvec_d = dt("chvec", [128, 176], F32, kind="ExternalInput").ap()
    iota_d = dt("iota", [128, T], F32, kind="ExternalInput").ap()
    consts_d = dt("consts", [128, 8], F32, kind="ExternalInput").ap()
    meta_d = dt("meta", [128, 16], F32, kind="ExternalInput").ap()
    e16_d = dt("e16", [8, 128], BF16, kind="ExternalInput").ap()
    ident_d = dt("identity", [128, 128], BF16, kind="ExternalInput").ap()
    outT_d = dt("outT", [D, T], F32, kind="ExternalOutput").ap()
    contrib_d = dt("contrib", [128, 8], F32, kind="Internal")
    gathered_d = dt("gathered", [8 * 128, 8], F32, kind="Internal")
    dbg_d = {}
    if DEBUG_DUMPS:
        for nm, shp, dty in [("dxn", [128, NT * D], BF16), ("dh0", [128, NT * DIM], F32),
                             ("dlog", [128, NT * 16], F32), ("dw", [128, NT * 8], BF16),
                             ("dhw", [128, T], BF16), ("dg", [128, NO * T], BF16),
                             ("dpp", [128, 96], F32), ("dgath", [128, 64], F32),
                             ("dh1", [128, NT * DIM], F32), ("dh2", [128, NT * DIM], F32),
                             ("dh3", [128, NT * DIM], F32), ("dh4", [128, NT * DIM], F32),
                             ("du0", [128, 4 * T], F32), ("dyc", [128, T], F32),
                             ("dct", [128, T], F32), ("dst", [128, T], F32),
                             ("dmb", [128, 128], BF16), ("desc", [128, 16], F32),
                             ("dcnt", [128, 16], F32),
                             ("doutk", [128, T], BF16), ("dz1s", [128, 8 * T], BF16)]:
            dbg_d[nm] = dt(nm, shp, dty, kind="ExternalOutput").ap()

    ops = {k: [] for k in ("sync", "scalar", "vector", "tensor", "gpsimd")}
    sems = {}

    def mksem(name):
        sems[name] = [es.enter_context(nc.semaphore(name)), 0]

    for s in ("sX", "sW", "sS", "sO", "sV", "sA", "sT", "sG", "sC", "sWf", "sW2f"):
        mksem(s)
    for i in range(8):
        mksem(f"bd{i}")
        mksem(f"bf{i}")

    def incd(name, n=1):
        sems[name][1] += n
        return (sems[name][0], n)

    def val(name):
        return sems[name][1]

    def E(eng, fn):
        ops[eng].append(fn)

    def W(eng, name, v=None):
        h = sems[name][0]
        vv = sems[name][1] if v is None else v
        if vv > 0:
            E(eng, lambda e, h=h, vv=vv: e.wait_ge(h, vv))

    def dma(eng, out, in_, sem="sW"):
        tgt = incd(sem, 16)
        E(eng, lambda e, out=out, in_=in_, tgt=tgt:
          e.dma_start(out=out, in_=in_).then_inc(*tgt))

    def mk(engine_letter):
        def emitter(fn, sem=None, sem2=None):
            tgt = incd(sem) if sem else None
            def run(e, fn=fn, tgt=tgt):
                ins = fn(e)
                if tgt:
                    ins = ins.then_inc(*tgt)
                return ins
            E(engine_letter, run)
            if sem2:
                tgt2 = incd(sem2)
                E(engine_letter, lambda e, tgt2=tgt2: e.nop().then_inc(*tgt2))
        return emitter

    _mkV = mk("vector")
    AUTOJ = [False]

    def junk():
        E("vector", lambda v: v.tensor_scalar(out=vjunk[:], in0=vjunk[:],
                                              scalar1=1.0, scalar2=None,
                                              op0=ALU.mult))

    def V(fn, sem=None, sem2=None):
        _mkV(fn, sem, sem2)
        if AUTOJ[0]:
            junk()

    A = mk("scalar")
    TE = mk("tensor")
    GP = mk("gpsimd")

    def dump(nm, src_ap, waits=()):
        if not DEBUG_DUMPS:
            return
        for s, v in waits:
            W("scalar", s, v)
        dma("scalar", dbg_d[nm], src_ap, "sS")
        W("vector", "sS", val("sS"))
        W("tensor", "sS", val("sS"))

    sb = nc.alloc_sbuf_tensor
    consts = sb("consts_t", [128, 8], F32)
    chvec = sb("chvec_t", [128, 176], F32)
    meta = sb("meta_t", [128, 16], F32)
    e16 = sb("e16_t", [8, 128], BF16)
    onesr = sb("onesr_t", [1, 512], BF16)
    ones128f = sb("ones128f_t", [1, 128], F32)
    ident = sb("ident_t", [128, 128], BF16)
    binr = sb("binr_t", [1, DIM], BF16)
    b1c = sb("b1c_t", [128, NL * NF], F32)
    b2r = sb("b2r_t", [1, DIM], BF16)
    basebc = sb("basebc_t", [128, NO], F32)
    boutc = sb("boutc_t", [128, NO], F32)
    routbr = sb("routbr_t", [1, 16], BF16)
    emr = sb("emr_t", [1, 16], F32)
    mb = sb("mb_t", [128, 8 * 16], BF16)
    pp = sb("pp_t", [128, 96], F32)
    h = sb("h_t", [128, NT * DIM], F32)
    g = sb("g_t", [128, NO * T], BF16)
    finals = sb("finals_t", [128, 8], F32)
    gath = sb("gath_t", [128, 64], F32)
    gcoef = sb("gcoef_t", [128, 8], F32)
    acar = sb("acar_t", [128, 2], F32)
    statc = sb("statc_t", [128, 16], F32)
    zero128 = sb("zero128_t", [128, 1], F32)
    vjunk = sb("vjunk_t", [128, 16], F32)
    msk8 = sb("msk8_t", [128, 8], F32)
    mskc = sb("mskc_t", [128, 8], F32)
    beta2bf = sb("beta2bf_t", [128, ND], BF16)
    lnlbbf = sb("lnlbbf_t", [128, ND], BF16)
    b1eff = sb("b1eff_t", [128, NF], F32)
    lnst = sb("lnst_t", [128, 32], F32)
    bouteff = sb("bouteff_t", [128, NO], F32)

    pbank = [nc.alloc_psum_tensor(f"pb{i}", [128, 512], F32) for i in range(8)]
    pb = [b.ap() for b in pbank]
    pbbf = [b.ap().bitcast(BF16) for b in pbank]

    hv = h.ap().rearrange("p (i r) -> p i r", i=NT)

    dma("sync", consts[:], consts_d, "sX")
    dma("sync", chvec[:], chvec_d, "sX")
    dma("sync", meta[:], meta_d, "sX")
    dma("sync", b1c[:], b1c_d, "sX")
    dma("sync", basebc[:], basebc_d, "sX")
    dma("sync", boutc[:], boutc_d, "sX")
    dma("sync", emr[:], emr_d, "sX")
    dma("sync", ident[:], ident_d, "sX")
    dma("sync", e16[:], e16_d, "sX")
    dma("gpsimd", binr[:], binr_d, "sW")
    dma("gpsimd", routbr[:], routbr_d, "sW")
    sx_small = val("sX")

    ctxXN = ExitStack()
    xn = ctxXN.enter_context(nc.sbuf_tensor("xn", [128, NT * D], BF16))
    xnv = xn.ap().rearrange("p (i r) -> p i r", i=NT)

    ctxA = ExitStack()
    sbA = lambda *a: ctxA.enter_context(nc.sbuf_tensor(*a))
    prW = sbA("prW", [128, 16], F32)
    prX = sbA("prX", [128, 16], F32)
    prY = sbA("prY", [128, 16], F32)
    prZ = sbA("prZ", [128, 16], F32)
    prS = sbA("prS", [128, 16], F32)
    prM = sbA("prM", [128, 16], I32)
    ones16 = sbA("ones16", [128, 16], F32)
    zeros16 = sbA("zeros16", [128, 16], F32)
    rcol = sbA("rcol", [128, 16], F32)
    prT = sbA("prT", [128, 16], F32)
    prU = sbA("prU", [128, 16], F32)

    W("vector", "sX", sx_small)
    V(lambda v: v.memset(vjunk[:], 1.0))
    AUTOJ[0] = True
    V(lambda v: v.memset(onesr[:], 1.0))
    V(lambda v: v.memset(ones128f[:], 1.0))
    V(lambda v: v.memset(zero128[:], 0.0))
    V(lambda v: v.memset(ones16[:], 1.0))
    V(lambda v: v.memset(zeros16[:], 0.0))

    def chv(q, l):
        off = {"ln1w": 0, "ln1b": 4, "ln2w": 8, "ln2b": 12, "scl": 16,
               "pre": 20, "pim": 24, "lcr": 28, "lci": 32}[q] + l * 40
        return chvec.ap()[:, off:off + 4]

    for l in range(NL):
        V(lambda v, l=l: v.tensor_copy(out=prW[:, l * 4:l * 4 + 4], in_=chv("pre", l)))
        V(lambda v, l=l: v.tensor_copy(out=prX[:, l * 4:l * 4 + 4], in_=chv("pim", l)))
    V(lambda v: v.tensor_mul(out=prY[:], in0=prW[:], in1=prW[:]))
    V(lambda v: v.tensor_mul(out=prZ[:], in0=prX[:], in1=prX[:]))
    V(lambda v: v.tensor_add(out=prY[:], in0=prY[:], in1=prZ[:]))
    V(lambda v: v.tensor_scalar(out=prM[:], in0=prY[:], scalar1=1e-30,
                                scalar2=None, op0=ALU.is_gt), "sV")
    W("scalar", "sV")
    A(lambda a: a.activation(out=rcol[:], in_=prY[:], func=AF.Sqrt), "sA")
    W("vector", "sA")
    V(lambda v: v.tensor_scalar(out=prZ[:], in0=rcol[:], scalar1=1e-30,
                                scalar2=None, op0=ALU.max))
    V(lambda v: v.reciprocal(out=prZ[:], in_=prZ[:]))
    V(lambda v: v.tensor_mul(out=prZ[:], in0=prY[:], in1=prZ[:]))
    V(lambda v: v.tensor_add(out=rcol[:], in0=rcol[:], in1=prZ[:]))
    V(lambda v: v.tensor_scalar(out=rcol[:], in0=rcol[:], scalar1=0.5,
                                scalar2=None, op0=ALU.mult), "sV")
    W("scalar", "sV")
    A(lambda a: a.activation(out=prS[:], in_=rcol[:], func=AF.Tanh), "sA")
    W("vector", "sA")
    V(lambda v: v.tensor_scalar(out=pp[:, 0:16], in0=prS[:], scalar1=RHO_MAX,
                                scalar2=None, op0=ALU.min))
    V(lambda v: v.tensor_scalar(out=prZ[:], in0=pp.ap()[:, 0:16], scalar1=1e-38,
                                scalar2=None, op0=ALU.max), "sV")
    W("scalar", "sV")
    A(lambda a: a.activation(out=pp[:, 16:32], in_=prZ[:], func=AF.Ln), "sA")
    W("vector", "sA")
    V(lambda v: v.tensor_scalar(out=prZ[:], in0=rcol[:], scalar1=1e-30,
                                scalar2=None, op0=ALU.max))
    V(lambda v: v.reciprocal(out=prZ[:], in_=prZ[:]))
    cos1, sin1 = prY, prS
    V(lambda v: v.tensor_mul(out=cos1[:], in0=prW[:], in1=prZ[:]))
    V(lambda v: v.tensor_mul(out=sin1[:], in0=prX[:], in1=prZ[:]))
    V(lambda v: v.select(out=cos1[:], mask=prM[:], on_true=cos1[:], on_false=ones16[:]))
    V(lambda v: v.select(out=sin1[:], mask=prM[:], on_true=sin1[:], on_false=zeros16[:]))
    V(lambda v: v.tensor_scalar(out=prZ[:], in0=prW[:], scalar1=-1.0,
                                scalar2=None, op0=ALU.mult))
    V(lambda v: v.tensor_max(out=prZ[:], in0=prZ[:], in1=prW[:]))
    V(lambda v: v.tensor_add(out=prZ[:], in0=rcol[:], in1=prZ[:]))
    V(lambda v: v.tensor_scalar(out=prZ[:], in0=prZ[:], scalar1=1e-30,
                                scalar2=None, op0=ALU.max))
    V(lambda v: v.reciprocal(out=prZ[:], in_=prZ[:]))
    V(lambda v: v.tensor_mul(out=prZ[:], in0=prX[:], in1=prZ[:]), "sV")
    W("scalar", "sV")
    A(lambda a: a.activation(out=prZ[:], in_=prZ[:], func=AF.Arctan), "sA")
    W("vector", "sA")
    V(lambda v: v.tensor_scalar(out=prT[:], in0=prZ[:],
                                scalar1=float(1.0 / np.pi), scalar2=None,
                                op0=ALU.mult))
    V(lambda v: v.tensor_scalar(out=prU[:], in0=prX[:], scalar1=0.0,
                                scalar2=None, op0=ALU.is_ge))
    V(lambda v: v.tensor_scalar(out=prU[:], in0=prU[:], scalar1=1.0,
                                scalar2=-0.5, op0=ALU.mult, op1=ALU.add))
    V(lambda v: v.tensor_sub(out=prU[:], in0=prU[:], in1=prT[:]))
    V(lambda v: v.tensor_scalar(out=prM[:], in0=prW[:], scalar1=0.0,
                                scalar2=None, op0=ALU.is_lt))
    V(lambda v: v.select(out=pp[:, 32:48], mask=prM[:], on_true=prU[:],
                         on_false=prT[:]))
    for l in range(NL):
        V(lambda v, l=l: v.tensor_copy(out=prW[:, l * 4:l * 4 + 4], in_=chv("lcr", l)))
        V(lambda v, l=l: v.tensor_copy(out=prX[:, l * 4:l * 4 + 4], in_=chv("lci", l)))
    V(lambda v: v.tensor_mul(out=pp[:, 48:64], in0=prW[:], in1=cos1[:]))
    V(lambda v: v.tensor_mul(out=prZ[:], in0=prX[:], in1=sin1[:]))
    V(lambda v: v.tensor_sub(out=pp[:, 48:64], in0=pp.ap()[:, 48:64], in1=prZ[:]))
    V(lambda v: v.tensor_mul(out=pp[:, 64:80], in0=prW[:], in1=sin1[:]))
    V(lambda v: v.tensor_mul(out=prZ[:], in0=prX[:], in1=cos1[:]))
    V(lambda v: v.tensor_add(out=pp[:, 64:80], in0=pp.ap()[:, 64:80], in1=prZ[:]))
    V(lambda v: v.tensor_copy(out=lnlbbf[:], in_=chvec.ap()[:, 164:168]), "sV")

    W("tensor", "sX", sx_small)
    W("tensor", "sV")
    TE(lambda t: t.matmul(pb[7][0:128, 0:16], ones128f.ap()[:], emr.ap()[:],
                          start=True, stop=True), "bd7")
    W("vector", "bd7")
    for e in range(8):
        V(lambda v, e=e: v.tensor_scalar(out=mb[:, e * 16:e * 16 + 16],
                                         in0=pb[7][0:128, 0:16],
                                         scalar1=float(e), scalar2=None,
                                         op0=ALU.is_equal),
          "bf7" if e == 7 else None)
    AUTOJ[0] = False

    xt = [sbA(f"xt{i}", [128, D], F32) for i in range(2)]
    sqs = sbA("sqs", [128, D], F32)
    xnt = sbA("xnt", [128, D], BF16)

    for i in range(NT):
        if i >= 2:
            W("sync", "sV", val("sV"))
        dma("sync", xt[i % 2][:], x_tm_d[i * 128:(i + 1) * 128, :], "sX")
        W("vector", "sX", sx_small + 16 * (i + 1))
        V(lambda v, i=i: v.reduce_sum(out=statc[:, 0:1], in_=xt[i % 2][:], axis=AX), "sV")
        W("scalar", "sV")
        A(lambda a, i=i: a.activation(out=sqs[:], in_=xt[i % 2][:], func=AF.Square,
                                      accum_out=statc[:, 1:2]), "sA")
        W("vector", "sA")
        V(lambda v: v.tensor_scalar(out=statc[:, 2:3], in0=statc[:, 0:1],
                                    scalar1=float(1.0 / D), scalar2=None, op0=ALU.mult))
        V(lambda v: v.tensor_scalar(out=statc[:, 3:4], in0=statc[:, 1:2],
                                    scalar1=float(1.0 / D), scalar2=None, op0=ALU.mult))
        junk()
        V(lambda v: v.scalar_tensor_tensor(out=statc[:, 4:5], in0=statc[:, 2:3],
                                           scalar=statc[:, 2:3], in1=statc[:, 3:4],
                                           op0=ALU.mult, op1=ALU.subtract), "sV")
        W("scalar", "sV")
        A(lambda a: a.activation(out=statc[:, 5:6], in_=statc[:, 4:5], func=AF.Ln,
                                 scale=-1.0, bias=consts.ap()[:, 0:1]))
        A(lambda a: a.activation(out=statc[:, 6:7], in_=statc[:, 5:6], func=AF.Exp,
                                 scale=-0.5), "sA")
        W("vector", "sA")
        if i >= 1:
            W("vector", "bd4", val("bd4"))
        V(lambda v, i=i: v.tensor_scalar(out=xnt[:], in0=xt[i % 2][:],
                                         scalar1=statc[:, 2:3], scalar2=statc[:, 6:7],
                                         op0=ALU.subtract, op1=ALU.mult), "sV")
        W("tensor", "sV")
        for gi in range(4):
            bank = 3 + (gi % 2)
            W("tensor", f"bf{bank}", val(f"bf{bank}"))
            for c8 in range(8):
                c = gi * 8 + c8
                TE(lambda t, c=c, c8=c8, bank=bank:
                   t.transpose(pbbf[bank][:, c8 * 128:(c8 + 1) * 128],
                               xnt.ap()[:, c * 128:(c + 1) * 128], ident.ap()[:]),
                   f"bd{bank}" if c8 == 7 else None)
            W("scalar", f"bd{bank}", val(f"bd{bank}"))
            A(lambda a, i=i, gi=gi, bank=bank:
              a.copy(out=xn.ap()[:, i * D + gi * 1024:i * D + (gi + 1) * 1024],
                     in_=pbbf[bank][:]), f"bf{bank}")
    xn_bf3, xn_bf4 = val("bf3"), val("bf4")
    dump("dxn", xn.ap()[:], [("bf3", xn_bf3), ("bf4", xn_bf4)])
    dump("dpp", pp.ap()[:], [("sV", val("sV"))])
    xn_apply_sv = val("sV")

    def xn_lhs(i, c):
        return xn.ap()[:, i * D + c * 128:i * D + c * 128 + 128]

    def xn_rhs(c, tc):
        return xnv[:, 4 * tc:4 * tc + 4, c * 128:(c + 1) * 128]

    ctxA.close()
    ctxB = ExitStack()
    sbB = lambda *a: ctxB.enter_context(nc.sbuf_tensor(*a))
    winT_sb = sbB("winT_sb", [128, 16 * DIM], BF16)
    atT_sb = sbB("atT_sb", [128, 16 * 128], BF16)
    b2T_sb = sbB("b2T_sb", [128, NO * 128], BF16)
    routT_sb = sbB("routT_sb", [128, NV * 16], BF16)
    bstream = [sbB(f"bstream{i}", [128, NV * 128], BF16) for i in range(2)]
    logits = sbB("logits", [128, NT * 16], F32)
    wsel = sbB("wsel", [128, NT * 8], BF16)
    cnt = sbB("cnt", [128, 16], F32)
    mskB = sbB("mskB", [128, 16], F32)
    cmpa = sbB("cmpa", [128, 256], F32)
    escb = sbB("escb", [128, 16], F32)
    wfm = sbB("wfm", [8, T], BF16)
    wexp_sb = sbB("wexp_sb", [128, T], BF16)
    hwb = sbB("hwb", [128, T], BF16)

    W("gpsimd", "sV", xn_apply_sv)
    W("gpsimd", "sA", val("sA"))
    dma("gpsimd", routT_sb[:], routerT_d, "sW")
    dma("gpsimd", b2T_sb[:], b2T_d, "sW")
    dma("gpsimd", atT_sb[:], atT_d[0], "sW")
    sw_rt = val("sW")

    W("tensor", "sW", sw_rt)
    W("tensor", "bf3", xn_bf3)
    W("tensor", "bf4", xn_bf4)
    for i in range(NT):
        W("tensor", "bf7", val("bf7"))
        for c in range(NV):
            TE(lambda t, i=i, c=c: t.matmul(pb[7][:, 0:16], xn_lhs(i, c),
                                            routT_sb.ap()[:, c * 16:(c + 1) * 16],
                                            start=(c == 0), stop=False))
        TE(lambda t: t.matmul(pb[7][:, 0:16], onesr.ap()[0:1, 0:128], routbr.ap()[:],
                              start=False, stop=True), "bd7")
        W("vector", "bd7", val("bd7"))
        V(lambda v, i=i: v.tensor_copy(out=logits[:, i * 16:(i + 1) * 16],
                                       in_=pb[7][:, 0:16]), "bf7")
    dump("dmb", mb.ap()[:], [("sV", val("sV"))])
    for i in range(NT):
        lg = logits.ap()[:, i * 16:(i + 1) * 16]
        V(lambda v, lg=lg: v.reduce_max(out=statc[:, 8:9], in_=lg, axis=AX,
                                        negate=True), "sV")
        W("scalar", "sV")
        A(lambda a, lg=lg: a.activation(out=escb[:], in_=lg, func=AF.Exp,
                                        bias=statc[:, 8:9], accum_out=statc[:, 9:10]), "sA")
        W("vector", "sA")
        V(lambda v: v.reciprocal(out=statc[:, 10:11], in_=statc[:, 9:10]))
        for j in range(15):
            V(lambda v, j=j: v.tensor_scalar(out=cmpa[:, j * 16:(j + 1) * 16],
                                             in0=escb[:],
                                             scalar1=escb[:, j:j + 1],
                                             scalar2=None, op0=ALU.is_gt))
        junk()
        for j in range(15):
            V(lambda v, j=j: v.reduce_sum(out=cnt[:, j:j + 1],
                                          in_=cmpa.ap()[:, j * 16:(j + 1) * 16],
                                          axis=AX))
        V(lambda v: v.memset(cnt[:, 15:16], 100.0))
        junk()
        if i == 0:
            V(lambda v: v.engine_nop(), "sV")
            dump("desc", escb.ap()[:], [("sV", val("sV"))])
            dump("dcnt", cnt.ap()[:], [("sV", val("sV"))])
        V(lambda v: v.tensor_scalar(out=mskB[:], in0=cnt[:], scalar1=7.5,
                                    scalar2=None, op0=ALU.is_lt))
        junk()
        V(lambda v: v.tensor_mul(out=escb[:], in0=escb[:], in1=mskB[:]))
        junk()
        for e in range(8):
            V(lambda v, e=e: v.tensor_mul(out=cmpa[:, e * 16:(e + 1) * 16],
                                          in0=escb[:],
                                          in1=mb.ap()[:, e * 16:e * 16 + 16]))
        junk()
        for e in range(8):
            V(lambda v, e=e: v.reduce_sum(out=msk8[:, e:e + 1],
                                          in_=cmpa.ap()[:, e * 16:(e + 1) * 16],
                                          axis=AX))
        junk()
        V(lambda v, i=i: v.tensor_scalar(out=wsel[:, i * 8:i * 8 + 8],
                                         in0=msk8.ap()[:],
                                         scalar1=statc[:, 10:11], scalar2=None,
                                         op0=ALU.mult), "sV")
    w_done = val("sV")
    dump("dlog", logits.ap()[:], [("sV", w_done)])
    dump("dw", wsel.ap()[:], [("sV", w_done)])
    W("tensor", "sV", w_done)
    for i in range(NT):
        W("tensor", "bf2", val("bf2"))
        TE(lambda t, i=i: t.transpose(pbbf[2][0:8, 0:128],
                                      wsel.ap()[:, i * 8:(i + 1) * 8], ident.ap()[:]),
           "bd2")
        W("vector", "bd2", val("bd2"))
        V(lambda v, i=i: v.tensor_copy(out=wfm[:, i * 128:(i + 1) * 128],
                                       in_=pbbf[2][0:8, 0:128]), "bf2")
    wfm_sv = val("sV")
    wfm_bf2 = val("bf2")
    for tc in range(2):
        W("tensor", "bf6", val("bf6"))
        for half in range(2):
            if not (tc == 0 and half == 0):
                W("gpsimd", "sT", val("sT"))
                dma("gpsimd", atT_sb[:], atT_d[half], "sW")
                W("tensor", "sW", val("sW"))
            for c16 in range(16):
                cc = half * 16 + c16
                TE(lambda t, c16=c16, cc=cc, tc=tc:
                   t.matmul(pb[6][:], atT_sb.ap()[:, c16 * 128:(c16 + 1) * 128],
                            xn_rhs(cc, tc), start=(cc == 0), stop=(cc == NV - 1)),
                   "sT" if c16 == 15 else None)
        hA_done = val("sT")
        W("tensor", "sV", wfm_sv)
        W("tensor", "bf2", wfm_bf2)
        W("tensor", "bf5", val("bf5"))
        TE(lambda t, tc=tc: t.matmul(pb[5][:], e16.ap()[:],
                                     wfm.ap()[0:8, tc * 512:(tc + 1) * 512],
                                     start=True, stop=True), "bd5")
        W("scalar", "bd5", val("bd5"))
        A(lambda a, tc=tc: a.copy(out=wexp_sb[:, tc * 512:(tc + 1) * 512],
                                  in_=pb[5][:]), "bf5")
        W("vector", "sT", hA_done)
        W("vector", "bf5", val("bf5"))
        V(lambda v, tc=tc: v.tensor_mul(out=hwb[:, tc * 512:(tc + 1) * 512],
                                        in0=pb[6][:],
                                        in1=wexp_sb.ap()[:, tc * 512:(tc + 1) * 512]),
          "bf6")
    hw_sv = val("sV")
    hw_bf6 = val("bf6")
    dump("dhw", hwb.ap()[:], [("bf6", hw_bf6)])

    win_done = [("sT", 0)]
    for half in range(2):
        if half == 1:
            W("gpsimd", *win_done[0])
        dma("gpsimd", winT_sb[:], winT_d[half], "sW")
        W("tensor", "sW", val("sW"))
        for i in range(NT):
            W("tensor", "bf2", val("bf2"))
            for c16 in range(16):
                last = (half == 1 and c16 == 15)
                TE(lambda t, i=i, c16=c16, half=half, last=last:
                   t.matmul(pb[2][:], xn_lhs(i, half * 16 + c16),
                            winT_sb.ap()[:, c16 * DIM:(c16 + 1) * DIM],
                            start=(c16 == 0), stop=last),
                   "bd2" if last else None)
            if half == 0:
                TE(lambda t: t.matmul(pb[2][:], onesr.ap()[0:1, 0:128], binr.ap()[:],
                                      start=False, stop=True),
                   "bd2")
                if i == NT - 1:
                    win_done[0] = ("bd2", val("bd2"))
                W("vector", "bd2", val("bd2"))
                V(lambda v, i=i: v.tensor_copy(out=hv[:, i, :], in_=pb[2][:]), "bf2")
            else:
                W("vector", "bd2", val("bd2"))
                V(lambda v, i=i: v.tensor_add(out=hv[:, i, :], in0=hv[:, i, :],
                                              in1=pb[2][:]), "bf2")
    h_sv = val("sV")
    dump("dh0", h.ap()[:], [("bf2", val("bf2"))])

    W("tensor", "bf6", hw_bf6)
    sw_base0 = val("sW")
    base_done = []
    for oi in range(NO):
        buf = oi % 2
        if oi >= 2:
            W("gpsimd", *base_done[oi - 2])
        dma("gpsimd", bstream[buf][:], baseT_d[oi], "sW")
        W("tensor", "sW", sw_base0 + 16 * (oi + 1))
        for tc in range(2):
            bank = tc
            W("tensor", f"bf{bank}", val(f"bf{bank}"))
            for c in range(NV):
                TE(lambda t, buf=buf, c=c, tc=tc, bank=bank:
                   t.matmul(pb[bank][:], bstream[buf].ap()[:, c * 128:(c + 1) * 128],
                            xn_rhs(c, tc), start=(c == 0), stop=False))
            TE(lambda t, oi=oi, tc=tc, bank=bank:
               t.matmul(pb[bank][:], b2T_sb.ap()[:, oi * 128:(oi + 1) * 128],
                        hwb.ap()[:, tc * 512:(tc + 1) * 512], start=False, stop=True),
               f"bd{bank}")
            if tc == 1:
                base_done.append((f"bd{bank}", val(f"bd{bank}")))
            W("scalar", f"bd{bank}", val(f"bd{bank}"))
            A(lambda a, oi=oi, tc=tc, bank=bank:
              a.activation(out=g[:, oi * T + tc * 512:oi * T + (tc + 1) * 512],
                           in_=pb[bank][:], func=AF.Silu,
                           bias=basebc.ap()[:, oi:oi + 1]), f"bf{bank}")
    g_bf0, g_bf1 = val("bf0"), val("bf1")
    dump("dg", g.ap()[:], [("bf0", g_bf0), ("bf1", g_bf1)])

    ctxB.close()
    ctxXN.close()
    ctxC = ExitStack()
    sbC = lambda *a: ctxC.enter_context(nc.sbuf_tensor(*a))
    iota = sbC("iota_sb", [128, T], F32)
    zt = sbC("zt", [128, DIM], BF16)
    zfm = sbC("zfm", [128, NT * DIM], BF16)
    z2fm = sbC("z2fm", [128, NT * DIM], BF16)
    yc = sbC("yc", [128, T], F32)
    scr = [sbC(f"scr{i}", [128, T], F32) for i in range(3)]
    u0 = sbC("u0", [128, 4 * T], F32)
    v0 = sbC("v0", [128, 4 * T], F32)
    ct = sbC("ct", [128, T], F32)
    st = sbC("st", [128, T], F32)
    rp = sbC("rp", [128, T], BF16)
    z1s = sbC("z1s", [128, 8 * T], BF16)
    w1sb = [sbC(f"w1sb{i}", [128, ND * 128], BF16) for i in range(4)]
    w2sb = sbC("w2sb", [128, NF * DIM], BF16)
    outk = sbC("outk", [128, T], BF16)

    sqs2ap = scr[1].ap()[:, 0:DIM]
    zfmv = zfm.ap().rearrange("p (i r) -> p i r", i=NT)
    z2fmv = z2fm.ap().rearrange("p (i r) -> p i r", i=NT)
    gathv = gath.ap().rearrange("p (cc j) -> p cc j", cc=8)

    W("gpsimd", "bf0", g_bf0)
    W("gpsimd", "bf1", g_bf1)
    W("gpsimd", "sV", val("sV"))
    dma("gpsimd", iota[:], iota_d, "sW")
    dma("gpsimd", w2sb[:], w2T_d[0], "sW")
    sw_w2 = val("sW")

    tlb, wre, wim = scr[0], scr[1], scr[2]
    w1_done = []
    w2_done = [None]

    def emit_ln_to_fm(dstv):
        inv_n = float(1.0 / DIM)
        for i in range(NT):
            V(lambda v, i=i: v.reduce_sum(out=lnst[:, i:i + 1], in_=hv[:, i, :],
                                          axis=AX), "sV" if i == NT - 1 else None)
        W("scalar", "sV")
        for i in range(NT):
            A(lambda a, i=i: a.activation(out=sqs2ap, in_=hv[:, i, :], func=AF.Square,
                                          accum_out=lnst[:, 8 + i:9 + i]),
              "sA" if i == NT - 1 else None)
        W("vector", "sA")
        V(lambda v: v.tensor_scalar(out=lnst[:, 16:24], in0=lnst.ap()[:, 0:8],
                                    scalar1=inv_n, scalar2=None, op0=ALU.mult))
        V(lambda v: v.tensor_scalar(out=lnst[:, 8:16], in0=lnst.ap()[:, 8:16],
                                    scalar1=inv_n, scalar2=None, op0=ALU.mult))
        junk()
        V(lambda v: v.tensor_mul(out=lnst[:, 0:8], in0=lnst.ap()[:, 16:24],
                                 in1=lnst.ap()[:, 16:24]))
        junk()
        V(lambda v: v.tensor_sub(out=lnst[:, 0:8], in0=lnst.ap()[:, 0:8],
                                 in1=lnst.ap()[:, 8:16]), "sV")
        W("scalar", "sV")
        A(lambda a: a.activation(out=lnst[:, 8:16], in_=lnst.ap()[:, 0:8], func=AF.Ln,
                                 scale=-1.0, bias=consts.ap()[:, 0:1]))
        A(lambda a: a.activation(out=lnst[:, 24:32], in_=lnst.ap()[:, 8:16],
                                 func=AF.Exp, scale=-0.5), "sA")
        W("vector", "sA")
        for i in range(NT):
            W("vector", f"bd{3 + ((i + 1) % 2)}", val(f"bd{3 + ((i + 1) % 2)}"))
            V(lambda v, i=i: v.tensor_scalar(out=zt[:], in0=hv[:, i, :],
                                             scalar1=lnst[:, 16 + i:17 + i],
                                             scalar2=lnst[:, 24 + i:25 + i],
                                             op0=ALU.subtract, op1=ALU.mult), "sV")
            bank = 3 + (i % 2)
            W("tensor", "sV")
            W("tensor", f"bf{bank}", val(f"bf{bank}"))
            for c in range(ND):
                TE(lambda t, c=c, bank=bank:
                   t.transpose(pbbf[bank][:, c * 128:(c + 1) * 128],
                               zt.ap()[:, c * 128:(c + 1) * 128], ident.ap()[:]),
                   f"bd{bank}" if c == ND - 1 else None)
            W("scalar", f"bd{bank}", val(f"bd{bank}"))
            A(lambda a, i=i, bank=bank, dstv=dstv:
              a.copy(out=dstv[:, i, :], in_=pbbf[bank][:, 0:512]), f"bf{bank}")

    def emit_table_fracs(pc, sem=None):
        V(lambda v, pc=pc: v.tensor_scalar(out=statc[:, 11:12],
                                           in0=pp.ap()[:, 32 + pc:33 + pc],
                                           scalar1=meta.ap()[:, 0:1], scalar2=None,
                                           op0=ALU.mult))
        junk()
        V(lambda v, pc=pc: v.tensor_scalar(out=tlb[:], in0=iota[:],
                                           scalar1=pp.ap()[:, 32 + pc:33 + pc],
                                           scalar2=statc[:, 11:12],
                                           op0=ALU.mult, op1=ALU.add))
        V(lambda v: v.tensor_scalar(out=wre[:], in0=tlb[:], scalar1=MAGIC,
                                    scalar2=-MAGIC, op0=ALU.add, op1=ALU.add))
        V(lambda v: v.tensor_sub(out=wre[:], in0=tlb[:], in1=wre[:]))
        V(lambda v: v.tensor_scalar(out=wim[:], in0=tlb[:], scalar1=0.25,
                                    scalar2=MAGIC, op0=ALU.add, op1=ALU.add))
        V(lambda v: v.tensor_scalar(out=wim[:], in0=wim[:], scalar1=-MAGIC,
                                    scalar2=None, op0=ALU.add))
        V(lambda v: v.tensor_sub(out=wim[:], in0=tlb[:], in1=wim[:]), sem)

    def emit_table_sins(sem=None):
        A(lambda a: a.activation(out=st[:], in_=wre[:], func=AF.Sin, scale=TWO_PI))
        A(lambda a: a.activation(out=ct[:], in_=wim[:], func=AF.Sin,
                                 scale=TWO_PI, bias=consts.ap()[:, 1:2]), sem)

    sw_w2_cur = [sw_w2]
    for l in range(NL):
        lb = l * 40
        emit_ln_to_fm(zfmv)

        W("vector", "bf3", val("bf3"))
        W("vector", "bf4", val("bf4"))
        for c in range(4):
            pc = l * 4 + c
            V(lambda v, c=c, lb=lb: v.tensor_scalar(
                out=yc[:], in0=zfmv[:, 0:NT, c * 128:(c + 1) * 128],
                scalar1=chvec.ap()[:, lb + 0 + c:lb + 1 + c],
                scalar2=chvec.ap()[:, lb + 4 + c:lb + 5 + c],
                op0=ALU.mult, op1=ALU.add))
            emit_table_fracs(pc, "sV")
            W("scalar", "sV")
            A(lambda a: a.activation(out=yc[:], in_=yc[:], func=AF.Silu))
            emit_table_sins("sA")
            if l == 0 and c == 0:
                dump("dyc", yc.ap()[:], [("sA", val("sA"))])
                dump("dct", ct.ap()[:], [("sA", val("sA"))])
                dump("dst", st.ap()[:], [("sA", val("sA"))])
            W("vector", "sA")
            V(lambda v: v.tensor_mul(out=wre[:], in0=ct[:], in1=yc[:]))
            V(lambda v: v.tensor_mul(out=wim[:], in0=st[:], in1=yc[:]))
            V(lambda v: v.tensor_scalar(out=wim[:], in0=wim[:], scalar1=-1.0,
                                        scalar2=None, op0=ALU.mult))
            rhoap = pp.ap()[:, 0 + pc:1 + pc]
            for seg in range(2):
                s0, s1 = seg * 512, (seg + 1) * 512
                V(lambda v, c=c, s0=s0, s1=s1, seg=seg, rhoap=rhoap:
                  v.tensor_tensor_scan(out=u0.ap()[:, c * T + s0:c * T + s1],
                                       data0=rhoap.broadcast_to([128, 512]),
                                       data1=wre.ap()[:, s0:s1],
                                       initial=(zero128.ap()[:] if seg == 0 else
                                                u0.ap()[:, c * T + 511:c * T + 512]),
                                       op0=ALU.mult, op1=ALU.add))
                V(lambda v, c=c, s0=s0, s1=s1, seg=seg, rhoap=rhoap:
                  v.tensor_tensor_scan(out=v0.ap()[:, c * T + s0:c * T + s1],
                                       data0=rhoap.broadcast_to([128, 512]),
                                       data1=wim.ap()[:, s0:s1],
                                       initial=(zero128.ap()[:] if seg == 0 else
                                                v0.ap()[:, c * T + 511:c * T + 512]),
                                       op0=ALU.mult, op1=ALU.add))
            V(lambda v, c=c: v.tensor_copy(out=finals[:, c:c + 1],
                                           in_=u0.ap()[:, c * T + 1023:c * T + 1024]))
            V(lambda v, c=c: v.tensor_copy(out=finals[:, 4 + c:5 + c],
                                           in_=v0.ap()[:, c * T + 1023:c * T + 1024]),
              "sV")
        scans_sv = val("sV")
        if l == 0:
            dump("du0", u0.ap()[:], [("sV", scans_sv)])

        W("gpsimd", "sV", scans_sv)
        dma("gpsimd", contrib_d.ap(), finals[:], "sW")
        W("gpsimd", "sW", val("sW"))
        tgt = incd("sC")
        E("gpsimd", lambda gp, tgt=tgt: gp.collective_compute(
            "AllGather", ALU.bypass, replica_groups=[[0, 1, 2, 3, 4, 5, 6, 7]],
            ins=[contrib_d.ap().opt()], outs=[gathered_d.ap().opt()]).then_inc(*tgt))
        W("gpsimd", "sC", val("sC"))
        dma("gpsimd", gath.ap().rearrange("p (c j) -> p c j", c=8),
            gathered_d.ap().rearrange("(c p) j -> p c j", p=128), "sW")
        gath_w = val("sW")
        if l == 0:
            dump("dgath", gath.ap()[:], [("sW", gath_w)])

        W("vector", "sW", gath_w)
        W("scalar", "sW", gath_w)
        for c in range(4):
            pc = l * 4 + c
            lnr = pp.ap()[:, 16 + pc:17 + pc]
            emit_table_fracs(pc)
            V(lambda v, lnr=lnr: v.tensor_scalar(out=tlb[:], in0=iota[:], scalar1=lnr,
                                                 scalar2=lnr, op0=ALU.mult, op1=ALU.add),
              "sV")
            W("scalar", "sV")
            for j in range(8):
                A(lambda a, j=j, lnr=lnr: a.activation(out=gcoef[:, j:j + 1], in_=lnr,
                                                       func=AF.Exp,
                                                       scale=meta.ap()[:, 1 + j:2 + j]))
            A(lambda a, lnr=lnr: a.activation(out=statc[:, 12:13], in_=lnr, func=AF.Exp,
                                              scale=meta.ap()[:, 9:10]))
            emit_table_sins()
            A(lambda a: a.activation(out=rp[:], in_=tlb[:], func=AF.Exp), "sA")
            W("vector", "sA")
            V(lambda v, c=c: v.tensor_mul(out=msk8[:], in0=gcoef[:],
                                          in1=gathv[:, :, c]))
            V(lambda v, c=c: v.tensor_mul(out=mskc[:], in0=gcoef[:],
                                          in1=gathv[:, :, 4 + c]))
            junk()
            V(lambda v: v.reduce_sum(out=acar[:, 0:1], in_=msk8.ap()[:], axis=AX))
            V(lambda v: v.reduce_sum(out=acar[:, 1:2], in_=mskc.ap()[:], axis=AX))
            junk()
            V(lambda v, pc=pc: v.scalar_tensor_tensor(out=acar[:, 0:1],
                                                      in0=statc[:, 12:13],
                                                      scalar=pp.ap()[:, 48 + pc:49 + pc],
                                                      in1=acar.ap()[:, 0:1],
                                                      op0=ALU.mult, op1=ALU.add))
            V(lambda v, pc=pc: v.scalar_tensor_tensor(out=acar[:, 1:2],
                                                      in0=statc[:, 12:13],
                                                      scalar=pp.ap()[:, 64 + pc:65 + pc],
                                                      in1=acar.ap()[:, 1:2],
                                                      op0=ALU.mult, op1=ALU.add))
            junk()
            V(lambda v, c=c: v.scalar_tensor_tensor(out=wre[:], in0=rp[:],
                                                    scalar=acar[:, 0:1],
                                                    in1=u0.ap()[:, c * T:(c + 1) * T],
                                                    op0=ALU.mult, op1=ALU.add))
            V(lambda v, c=c: v.scalar_tensor_tensor(out=wim[:], in0=rp[:],
                                                    scalar=acar[:, 1:2],
                                                    in1=v0.ap()[:, c * T:(c + 1) * T],
                                                    op0=ALU.mult, op1=ALU.add))
            V(lambda v: v.tensor_mul(out=wre[:], in0=ct[:], in1=wre[:]))
            V(lambda v: v.tensor_mul(out=wim[:], in0=st[:], in1=wim[:]))
            V(lambda v: v.tensor_sub(out=tlb[:], in0=wre[:], in1=wim[:]))
            V(lambda v, lb=lb, c=c: v.tensor_scalar(
                out=outk[:], in0=tlb[:],
                scalar1=chvec.ap()[:, lb + 16 + c:lb + 17 + c],
                scalar2=None, op0=ALU.mult), "sV")
            if l == 0 and c == 0:
                dump("doutk", outk.ap()[:], [("sV", val("sV"))])
            W("tensor", "sV", val("sV"))
            for grp in range(2):
                bank = 3 + grp
                W("tensor", f"bf{bank}", val(f"bf{bank}"))
                for i4 in range(4):
                    i = grp * 4 + i4
                    TE(lambda t, c=c, i=i, i4=i4, bank=bank:
                       t.transpose(pbbf[bank][:, i4 * 128:(i4 + 1) * 128],
                                   outk.ap()[:, i * 128:(i + 1) * 128],
                                   ident.ap()[:]), f"bd{bank}" if i4 == 3 else None)
                W("vector", f"bd{bank}", val(f"bd{bank}"))
                V(lambda v, c=c, grp=grp, bank=bank:
                  v.tensor_add(out=hv[:, grp * 4:grp * 4 + 4, c * 128:(c + 1) * 128],
                               in0=hv[:, grp * 4:grp * 4 + 4, c * 128:(c + 1) * 128],
                               in1=pbbf[bank].rearrange("p (i j) -> p i j", i=8)[:, 0:4, :]),
                  f"bf{bank}")
        hupd_sv = val("sV")
        dump(f"dh{l + 1}", h.ap()[:], [("sV", hupd_sv), ("bf3", val("bf3")),
                                       ("bf4", val("bf4"))])

        if w2_done[0] is not None:
            W("gpsimd", *w2_done[0])
        dma("gpsimd", b2r[:], b2r_d[:, l * DIM:(l + 1) * DIM], "sW")
        sw_b2r = val("sW")
        W("vector", "sV", hupd_sv)
        emit_ln_to_fm(z2fmv)
        z2_bf3, z2_bf4 = val("bf3"), val("bf4")
        W("vector", "bf3", z2_bf3)
        W("vector", "bf4", z2_bf4)
        for dd in range(ND):
            V(lambda v, dd=dd, lb=lb: v.tensor_scalar(
                out=z2fmv[:, 0:NT, dd * 128:(dd + 1) * 128],
                in0=z2fmv[:, 0:NT, dd * 128:(dd + 1) * 128],
                scalar1=chvec.ap()[:, lb + 8 + dd:lb + 9 + dd],
                scalar2=None, op0=ALU.mult),
              "sV" if dd == ND - 1 else None)
        z2g_sv = val("sV")
        W("tensor", "sV", z2g_sv)
        V(lambda v, l=l: v.tensor_copy(out=beta2bf[:], in_=chv("ln2b", l)), "sV")
        b2bf_sv = val("sV")

        for half in range(2):
            for f8 in range(8):
                fi = half * 8 + f8
                slot = fi % 4
                kglob = l * NF + fi
                if kglob >= 4:
                    W("gpsimd", *w1_done[kglob - 4])
                dma("gpsimd", w1sb[slot][:], w1T_d[l, fi], "sW")
                sw_f = val("sW")
                W("tensor", "sW", sw_f)
                W("tensor", "sV", b2bf_sv)
                bb = 7 if fi % 2 == 0 else 2
                W("tensor", f"bf{bb}", val(f"bf{bb}"))
                for d in range(ND):
                    TE(lambda t, slot=slot, d=d, fi=fi, bb=bb:
                       t.matmul(pb[bb][:, fi:fi + 1],
                                w1sb[slot].ap()[:, d * 128:(d + 1) * 128],
                                beta2bf.ap()[:, d:d + 1],
                                start=(d == 0), stop=(d == ND - 1),
                                skip_group_check=True),
                       f"bd{bb}" if d == ND - 1 else None)
                W("vector", f"bd{bb}", val(f"bd{bb}"))
                V(lambda v, fi=fi, l=l, bb=bb: v.tensor_add(
                    out=b1eff[:, fi:fi + 1],
                    in0=b1c.ap()[:, l * NF + fi:l * NF + fi + 1],
                    in1=pb[bb][:, fi:fi + 1]), f"bf{bb}")
                V(lambda v: v.engine_nop(), "sV")
                b1eff_sv = val("sV")
                for tc in range(2):
                    bank = 5 + tc
                    W("tensor", f"bf{bank}", val(f"bf{bank}"))
                    for d in range(ND):
                        TE(lambda t, slot=slot, d=d, tc=tc, bank=bank:
                           t.matmul(pb[bank][:],
                                    w1sb[slot].ap()[:, d * 128:(d + 1) * 128],
                                    z2fmv[:, 4 * tc:4 * tc + 4, d * 128:(d + 1) * 128],
                                    start=(d == 0), stop=(d == ND - 1)),
                           f"bd{bank}" if d == ND - 1 else None)
                    W("scalar", f"bd{bank}", val(f"bd{bank}"))
                    W("scalar", "sV", b1eff_sv)
                    A(lambda a, f8=f8, tc=tc, bank=bank, fi=fi:
                      a.activation(out=z1s[:, f8 * T + tc * 512:f8 * T + (tc + 1) * 512],
                                   in_=pb[bank][:], func=AF.Silu,
                                   bias=b1eff.ap()[:, fi:fi + 1]), f"bf{bank}")
                w1_done.append(("bd6", val("bd6")))
            W("tensor", "sW", sw_w2_cur[0])
            W("tensor", "sW", sw_b2r)
            W("tensor", "bf5", val("bf5"))
            W("tensor", "bf6", val("bf6"))
            for i in range(NT):
                wb = 2 if i % 2 == 0 else 7
                W("tensor", f"bf{wb}", val(f"bf{wb}"))
                for f8 in range(8):
                    last = (half == 0 and f8 == 7)
                    TE(lambda t, i=i, f8=f8, half=half, last=last, wb=wb:
                       t.matmul(pb[wb][:],
                                z1s.ap()[:, f8 * T + i * 128:f8 * T + (i + 1) * 128],
                                w2sb.ap()[:, (half * 8 + f8) * DIM:(half * 8 + f8 + 1) * DIM],
                                start=(f8 == 0), stop=last),
                       f"bd{wb}" if last else None)
                if half == 1:
                    TE(lambda t, wb=wb: t.matmul(pb[wb][:], onesr.ap()[0:1, 0:128],
                                                 b2r.ap()[:], start=False, stop=True),
                       f"bd{wb}")
                    if i == NT - 1:
                        w2_done[0] = (f"bd{wb}", val(f"bd{wb}"))
                W("vector", f"bd{wb}", val(f"bd{wb}"))
                V(lambda v, i=i, wb=wb: v.tensor_add(out=hv[:, i, :], in0=hv[:, i, :],
                                                     in1=pb[wb][:]), f"bf{wb}")
        if l + 1 < NL:
            W("gpsimd", *w2_done[0])
            dma("gpsimd", w2sb[:], w2T_d[l + 1], "sW")
            sw_w2_cur[0] = val("sW")

    z3fmv = z2fmv
    emit_ln_to_fm(z3fmv)
    W("vector", "bf3", val("bf3"))
    W("vector", "bf4", val("bf4"))
    for dd in range(ND):
        V(lambda v, dd=dd: v.tensor_scalar(
            out=z3fmv[:, 0:NT, dd * 128:(dd + 1) * 128],
            in0=z3fmv[:, 0:NT, dd * 128:(dd + 1) * 128],
            scalar1=chvec.ap()[:, 160 + dd:161 + dd],
            scalar2=None, op0=ALU.mult),
          "sV" if dd == ND - 1 else None)
    z3g_sv = val("sV")
    W("tensor", "sV", z3g_sv)
    xbuf = yc
    obuf = scr[0]

    wout_done = []
    for vi in range(NO):
        slot = vi % 4
        if vi < 4:
            W("gpsimd", *w1_done[(NL - 1) * NF + 12 + vi])
        else:
            W("gpsimd", *wout_done[vi - 4])
        dma("gpsimd", w1sb[slot][:], woutT_d[vi], "sW")
        sw_v = val("sW")
        W("tensor", "sW", sw_v)
        bb = 7 if vi % 2 == 0 else 2
        W("tensor", f"bf{bb}", val(f"bf{bb}"))
        for d in range(ND):
            TE(lambda t, slot=slot, d=d, vi=vi, bb=bb:
               t.matmul(pb[bb][:, vi:vi + 1], w1sb[slot].ap()[:, d * 128:(d + 1) * 128],
                        lnlbbf.ap()[:, d:d + 1], start=(d == 0), stop=(d == ND - 1),
                        skip_group_check=True),
               f"bd{bb}" if d == ND - 1 else None)
        W("vector", f"bd{bb}", val(f"bd{bb}"))
        V(lambda v, vi=vi, bb=bb: v.tensor_add(out=bouteff[:, vi:vi + 1],
                                               in0=boutc.ap()[:, vi:vi + 1],
                                               in1=pb[bb][:, vi:vi + 1]), f"bf{bb}")
        junk()
        for tc in range(2):
            bank = tc
            W("scalar", "sV", val("sV"))
            dma("scalar", xbuf.ap()[:, tc * 512:(tc + 1) * 512],
                x_fm_d[vi * 128:(vi + 1) * 128, tc * 512:(tc + 1) * 512], "sS")
            sw_x = val("sS")
            W("tensor", f"bf{bank}", val(f"bf{bank}"))
            for d in range(ND):
                TE(lambda t, slot=slot, d=d, tc=tc, bank=bank:
                   t.matmul(pb[bank][:], w1sb[slot].ap()[:, d * 128:(d + 1) * 128],
                            z3fmv[:, 4 * tc:4 * tc + 4, d * 128:(d + 1) * 128],
                            start=(d == 0), stop=(d == ND - 1)),
                   f"bd{bank}" if d == ND - 1 else None)
            if tc == 1:
                wout_done.append((f"bd{bank}", val(f"bd{bank}")))
            W("vector", f"bd{bank}", val(f"bd{bank}"))
            W("vector", "sS", sw_x)
            W("vector", "sO", max(0, val("sO") - 16))
            V(lambda v, vi=vi, tc=tc, bank=bank:
              v.scalar_tensor_tensor(out=obuf.ap()[:, tc * 512:(tc + 1) * 512],
                                     in0=pb[bank][:], scalar=bouteff[:, vi:vi + 1],
                                     in1=g.ap()[:, vi * T + tc * 512:vi * T + (tc + 1) * 512],
                                     op0=ALU.add, op1=ALU.mult), f"bf{bank}")
            V(lambda v, tc=tc: v.tensor_add(out=obuf.ap()[:, tc * 512:(tc + 1) * 512],
                                            in0=obuf.ap()[:, tc * 512:(tc + 1) * 512],
                                            in1=xbuf.ap()[:, tc * 512:(tc + 1) * 512]),
              "sV")
            W("scalar", "sV", val("sV"))
            dma("scalar", outT_d[vi * 128:(vi + 1) * 128, tc * 512:(tc + 1) * 512],
                obuf.ap()[:, tc * 512:(tc + 1) * 512], "sO")

    W("gpsimd", "sO", val("sO"))

    with nc.Block() as block:
        @block.sync
        def _(sync):
            for f in ops["sync"]:
                f(sync)

        @block.scalar
        def _(scalar):
            for f in ops["scalar"]:
                f(scalar)

        @block.vector
        def _(vector):
            for f in ops["vector"]:
                f(vector)

        @block.tensor
        def _(tensor):
            for f in ops["tensor"]:
                f(tensor)

        @block.gpsimd
        def _(gpsimd):
            for f in ops["gpsimd"]:
                f(gpsimd)

    es.close()
    return nc


_NC_CACHE = {}


def _host_layouts(inputs):
    f32 = np.float32
    W1 = inputs["L0_attn_W1"].astype(f32)
    W2 = inputs["L0_attn_W2"].astype(f32)
    Win = inputs["L0_attn_Win"].astype(f32)
    Wout = inputs["L0_attn_Wout"].astype(f32)
    baseW = inputs["L0_ffn_base_W"].astype(f32)
    routW = inputs["L0_ffn_router_W"].astype(f32)
    Am = inputs["L0_ffn_A"].astype(f32)
    Bm = inputs["L0_ffn_B"].astype(f32)

    c = np.ascontiguousarray
    out = {}
    WinT = Win.T.reshape(2, 16, 128, 512)
    out["winT"] = c(WinT.transpose(0, 2, 1, 3).reshape(2, 128, 16 * 512))
    BT = baseW.T.reshape(NV, 128, NO, 128)
    out["baseT"] = c(BT.transpose(2, 1, 0, 3).reshape(NO, 128, NV * 128))
    AT = Am.reshape(128, D).T.reshape(2, 16, 128, 128)
    out["atT"] = c(AT.transpose(0, 2, 1, 3).reshape(2, 128, 16 * 128))
    out["b2T"] = c(Bm.transpose(0, 2, 1).reshape(128, D))
    RT = np.zeros((D, 16), f32)
    RT[:, :15] = routW.T
    out["routerT"] = c(RT.reshape(NV, 128, 16).transpose(1, 0, 2).reshape(128, NV * 16))
    W1T = W1.transpose(0, 2, 1).reshape(NL, ND, 128, NF, 128)
    out["w1T"] = c(W1T.transpose(0, 3, 2, 1, 4).reshape(NL, NF, 128, ND * 128))
    W2T = W2.transpose(0, 2, 1).reshape(NL, NF, 128, DIM)
    out["w2T"] = c(W2T.transpose(0, 2, 1, 3).reshape(NL, 128, NF * DIM))
    WoT = Wout.T.reshape(ND, 128, NO, 128)
    out["woutT"] = c(WoT.transpose(2, 1, 0, 3).reshape(NO, 128, ND * 128))
    out["binr"] = c(inputs["L0_attn_bin"].astype(f32).reshape(1, DIM))
    out["b1c"] = c(inputs["L0_attn_b1"].astype(f32).reshape(NL, NF, 128)
                   .transpose(2, 0, 1).reshape(128, NL * NF))
    out["b2r"] = c(inputs["L0_attn_b2"].astype(f32).reshape(1, NL * DIM))
    out["basebc"] = c(inputs["L0_ffn_base_b"].astype(f32).reshape(NO, 128).T)
    out["boutc"] = c(inputs["L0_attn_bout"].astype(f32).reshape(NO, 128).T)
    rb = np.full((1, 16), -1e30, f32)
    rb[0, :15] = inputs["L0_ffn_router_b"].astype(f32)
    out["routbr"] = rb
    em = np.full((1, 16), -1.0, f32)
    em[0, :15] = inputs["L0_ffn_expert_map"].astype(f32)
    out["emr"] = em
    ch = np.zeros((128, 176), f32)
    packs = [("ln1w", inputs["L0_attn_ln1_w"]), ("ln1b", inputs["L0_attn_ln1_b"]),
             ("ln2w", inputs["L0_attn_ln2_w"]), ("ln2b", inputs["L0_attn_ln2_b"]),
             ("scl", inputs["L0_attn_sc_lin"]), ("pre", inputs["L0_attn_phz_re"]),
             ("pim", inputs["L0_attn_phz_im"]), ("lcr", inputs["L0_attn_lc_re"]),
             ("lci", inputs["L0_attn_lc_im"])]
    offs = {"ln1w": 0, "ln1b": 4, "ln2w": 8, "ln2b": 12, "scl": 16,
            "pre": 20, "pim": 24, "lcr": 28, "lci": 32}
    for l in range(NL):
        for name, arr in packs:
            a4 = np.asarray(arr, f32)[l].reshape(4, 128).T
            ch[:, l * 40 + offs[name]:l * 40 + offs[name] + 4] = a4
    ch[:, 160:164] = np.asarray(inputs["L0_attn_lnl_w"], f32).reshape(4, 128).T
    ch[:, 164:168] = np.asarray(inputs["L0_attn_lnl_b"], f32).reshape(4, 128).T
    out["chvec"] = c(ch)
    out["iota"] = c(np.broadcast_to(np.arange(T, dtype=f32)[None, :], (128, T)))
    cst = np.zeros((128, 8), f32)
    cst[:, 0] = 1e-5
    cst[:, 1] = PIH
    out["consts"] = cst
    out["e16"] = (2.0 * (np.arange(128)[None, :] // 16 ==
                         np.arange(8)[:, None])).astype(BF)
    out["identity"] = np.eye(128).astype(BF)
    return out


def kernel(**inputs):
    shared = _host_layouts(inputs)
    x = np.asarray(inputs["L0_attn_x"], np.float32)
    in_maps = []
    for core in range(8):
        b, q = core // 4, core % 4
        xs = np.ascontiguousarray(x[b, q * T:(q + 1) * T, :])
        xfs = np.ascontiguousarray(xs.T)
        mt = np.zeros((128, 16), np.float32)
        mt[:, 0] = 1024.0 * q
        for j in range(8):
            jb, jq = j // 4, j % 4
            if jb == b and jq < q:
                mt[:, 1 + j] = 1024.0 * (q - 1 - jq)
            else:
                mt[:, 1 + j] = 1e30
        mt[:, 9] = 1024.0 * q
        m = dict(shared)
        m["x_tm"] = xs
        m["x_fm"] = xfs
        m["meta"] = mt
        in_maps.append(m)

    if "nc" not in _NC_CACHE:
        _NC_CACHE["nc"] = build_program()
    nc = _NC_CACHE["nc"]
    res = run_bass_kernel_spmd(nc, in_maps, core_ids=list(range(8)))
    _NC_CACHE["res"] = res
    out = np.empty((2, 4096, D), np.float32)
    for core in range(8):
        b, q = core // 4, core % 4
        out[b, q * T:(q + 1) * T, :] = np.asarray(res.results[core]["outT"]).T
    return out


if __name__ == "__main__":
    nc = build_program()
    print("build ok:", nc.n_instructions(), "instructions")


# version 09
# speedup vs baseline: 1.0735x; 1.0237x over previous
import numpy as np
import ml_dtypes
import concourse.bass as bass
import concourse.mybir as mybir
from concourse.bass_utils import run_bass_kernel_spmd
from contextlib import ExitStack

F32 = mybir.dt.float32
BF16 = mybir.dt.bfloat16
I32 = mybir.dt.int32
AF = mybir.ActivationFunctionType
ALU = mybir.AluOpType
AX = mybir.AxisListType.X
BF = ml_dtypes.bfloat16

D, DIM, DFF, T = 4096, 512, 2048, 1024
NT, NV, NO, NF, ND, NL = 8, 32, 32, 16, 4, 4
DEBUG_DUMPS = False
MAGIC = 12582912.0
RHO_MAX = 0.99999994
PIH = float(np.pi / 2)
TWO_PI = float(2 * np.pi)


def build_program():
    nc = bass.Bass("TRN2", target_bir_lowering=False)
    es = ExitStack()

    dt = nc.dram_tensor
    x_tm_d = dt("x_tm", [T, D], F32, kind="ExternalInput").ap()
    x_fm_d = dt("x_fm", [D, T], F32, kind="ExternalInput").ap()
    winT_d = dt("winT", [2, 128, 16 * DIM], F32, kind="ExternalInput").ap()
    baseT_d = dt("baseT", [NO, 128, NV * 128], F32, kind="ExternalInput").ap()
    atT_d = dt("atT", [2, 128, 16 * 128], F32, kind="ExternalInput").ap()
    b2T_d = dt("b2T", [128, NO * 128], F32, kind="ExternalInput").ap()
    routerT_d = dt("routerT", [128, NV * 16], F32, kind="ExternalInput").ap()
    w1T_d = dt("w1T", [NL, NF, 128, ND * 128], F32, kind="ExternalInput").ap()
    w2T_d = dt("w2T", [NL, 128, NF * DIM], F32, kind="ExternalInput").ap()
    woutT_d = dt("woutT", [NO, 128, ND * 128], F32, kind="ExternalInput").ap()
    binr_d = dt("binr", [1, DIM], F32, kind="ExternalInput").ap()
    b1c_d = dt("b1c", [128, NL * NF], F32, kind="ExternalInput").ap()
    b2r_d = dt("b2r", [1, NL * DIM], F32, kind="ExternalInput").ap()
    basebc_d = dt("basebc", [128, NO], F32, kind="ExternalInput").ap()
    boutc_d = dt("boutc", [128, NO], F32, kind="ExternalInput").ap()
    routbr_d = dt("routbr", [1, 16], F32, kind="ExternalInput").ap()
    emr_d = dt("emr", [1, 16], F32, kind="ExternalInput").ap()
    chvec_d = dt("chvec", [128, 176], F32, kind="ExternalInput").ap()
    iota_d = dt("iota", [128, T], F32, kind="ExternalInput").ap()
    consts_d = dt("consts", [128, 8], F32, kind="ExternalInput").ap()
    meta_d = dt("meta", [128, 16], F32, kind="ExternalInput").ap()
    e16_d = dt("e16", [8, 128], BF16, kind="ExternalInput").ap()
    ident_d = dt("identity", [128, 128], BF16, kind="ExternalInput").ap()
    outT_d = dt("outT", [D, T], F32, kind="ExternalOutput").ap()
    contrib_d = dt("contrib", [128, 8], F32, kind="Internal")
    gathered_d = dt("gathered", [8 * 128, 8], F32, kind="Internal")
    dbg_d = {}
    if DEBUG_DUMPS:
        for nm, shp, dty in [("dxn", [128, NT * D], BF16), ("dh0", [128, NT * DIM], F32),
                             ("dlog", [128, NT * 16], F32), ("dw", [128, NT * 8], BF16),
                             ("dhw", [128, T], BF16), ("dg", [128, NO * T], BF16),
                             ("dpp", [128, 96], F32), ("dgath", [128, 64], F32),
                             ("dh1", [128, NT * DIM], F32), ("dh2", [128, NT * DIM], F32),
                             ("dh3", [128, NT * DIM], F32), ("dh4", [128, NT * DIM], F32),
                             ("du0", [128, 4 * T], F32), ("dyc", [128, T], F32),
                             ("dct", [128, T], F32), ("dst", [128, T], F32),
                             ("dmb", [128, 128], BF16), ("desc", [128, 16], F32),
                             ("dcnt", [128, 16], F32),
                             ("doutk", [128, T], BF16), ("dz1s", [128, 8 * T], BF16)]:
            dbg_d[nm] = dt(nm, shp, dty, kind="ExternalOutput").ap()

    ops = {k: [] for k in ("sync", "scalar", "vector", "tensor", "gpsimd")}
    sems = {}

    def mksem(name):
        sems[name] = [es.enter_context(nc.semaphore(name)), 0]

    for s in ("sX", "sW", "sS", "sO", "sV", "sA", "sT", "sG", "sC", "sWf", "sW2f"):
        mksem(s)
    for i in range(8):
        mksem(f"bd{i}")
        mksem(f"bf{i}")

    def incd(name, n=1):
        sems[name][1] += n
        return (sems[name][0], n)

    def val(name):
        return sems[name][1]

    def E(eng, fn):
        ops[eng].append(fn)

    def W(eng, name, v=None):
        h = sems[name][0]
        vv = sems[name][1] if v is None else v
        if vv > 0:
            E(eng, lambda e, h=h, vv=vv: e.wait_ge(h, vv))

    def dma(eng, out, in_, sem="sW"):
        tgt = incd(sem, 16)
        E(eng, lambda e, out=out, in_=in_, tgt=tgt:
          e.dma_start(out=out, in_=in_).then_inc(*tgt))

    def mk(engine_letter):
        def emitter(fn, sem=None, sem2=None):
            tgt = incd(sem) if sem else None
            def run(e, fn=fn, tgt=tgt):
                ins = fn(e)
                if tgt:
                    ins = ins.then_inc(*tgt)
                return ins
            E(engine_letter, run)
            if sem2:
                tgt2 = incd(sem2)
                E(engine_letter, lambda e, tgt2=tgt2: e.nop().then_inc(*tgt2))
        return emitter

    _mkV = mk("vector")
    AUTOJ = [False]

    def junk():
        E("vector", lambda v: v.tensor_scalar(out=vjunk[:], in0=vjunk[:],
                                              scalar1=1.0, scalar2=None,
                                              op0=ALU.mult))

    def V(fn, sem=None, sem2=None):
        _mkV(fn, sem, sem2)
        if AUTOJ[0]:
            junk()

    A = mk("scalar")
    TE = mk("tensor")
    GP = mk("gpsimd")

    def dump(nm, src_ap, waits=()):
        if not DEBUG_DUMPS:
            return
        for s, v in waits:
            W("scalar", s, v)
        dma("scalar", dbg_d[nm], src_ap, "sS")
        W("vector", "sS", val("sS"))
        W("tensor", "sS", val("sS"))

    sb = nc.alloc_sbuf_tensor
    consts = sb("consts_t", [128, 8], F32)
    chvec = sb("chvec_t", [128, 176], F32)
    meta = sb("meta_t", [128, 16], F32)
    e16 = sb("e16_t", [8, 128], BF16)
    onesr = sb("onesr_t", [1, 512], BF16)
    ones128f = sb("ones128f_t", [1, 128], F32)
    ident = sb("ident_t", [128, 128], BF16)
    binr = sb("binr_t", [1, DIM], BF16)
    b1c = sb("b1c_t", [128, NL * NF], F32)
    b2r = sb("b2r_t", [1, DIM], BF16)
    basebc = sb("basebc_t", [128, NO], F32)
    boutc = sb("boutc_t", [128, NO], F32)
    routbr = sb("routbr_t", [1, 16], BF16)
    emr = sb("emr_t", [1, 16], F32)
    mb = sb("mb_t", [128, 8 * 16], BF16)
    pp = sb("pp_t", [128, 96], F32)
    h = sb("h_t", [128, NT * DIM], F32)
    g = sb("g_t", [128, NO * T], BF16)
    finals = sb("finals_t", [128, 8], F32)
    gath = sb("gath_t", [128, 64], F32)
    gcoef = sb("gcoef_t", [128, 8], F32)
    acar = sb("acar_t", [128, 2], F32)
    statc = sb("statc_t", [128, 16], F32)
    zero128 = sb("zero128_t", [128, 1], F32)
    vjunk = sb("vjunk_t", [128, 16], F32)
    msk8 = sb("msk8_t", [128, 8], F32)
    mskc = sb("mskc_t", [128, 8], F32)
    lnst = sb("lnst_t", [128, 32], F32)

    pbank = [nc.alloc_psum_tensor(f"pb{i}", [128, 512], F32) for i in range(8)]
    pb = [b.ap() for b in pbank]
    pbbf = [b.ap().bitcast(BF16) for b in pbank]

    hv = h.ap().rearrange("p (i r) -> p i r", i=NT)

    dma("sync", consts[:], consts_d, "sX")
    dma("sync", chvec[:], chvec_d, "sX")
    dma("sync", meta[:], meta_d, "sX")
    dma("sync", b1c[:], b1c_d, "sX")
    dma("sync", basebc[:], basebc_d, "sX")
    dma("sync", boutc[:], boutc_d, "sX")
    dma("sync", emr[:], emr_d, "sX")
    dma("sync", ident[:], ident_d, "sX")
    dma("sync", e16[:], e16_d, "sX")
    dma("gpsimd", binr[:], binr_d, "sW")
    dma("gpsimd", routbr[:], routbr_d, "sW")
    sx_small = val("sX")

    ctxXN = ExitStack()
    xn = ctxXN.enter_context(nc.sbuf_tensor("xn", [128, NT * D], BF16))
    xnv = xn.ap().rearrange("p (i r) -> p i r", i=NT)

    ctxA = ExitStack()
    sbA = lambda *a: ctxA.enter_context(nc.sbuf_tensor(*a))
    prW = sbA("prW", [128, 16], F32)
    prX = sbA("prX", [128, 16], F32)
    prY = sbA("prY", [128, 16], F32)
    prZ = sbA("prZ", [128, 16], F32)
    prS = sbA("prS", [128, 16], F32)
    prM = sbA("prM", [128, 16], I32)
    ones16 = sbA("ones16", [128, 16], F32)
    zeros16 = sbA("zeros16", [128, 16], F32)
    rcol = sbA("rcol", [128, 16], F32)
    prT = sbA("prT", [128, 16], F32)
    prU = sbA("prU", [128, 16], F32)

    W("vector", "sX", sx_small)
    V(lambda v: v.memset(vjunk[:], 1.0))
    AUTOJ[0] = True
    V(lambda v: v.memset(onesr[:], 1.0))
    V(lambda v: v.memset(ones128f[:], 1.0))
    V(lambda v: v.memset(zero128[:], 0.0))
    V(lambda v: v.memset(ones16[:], 1.0))
    V(lambda v: v.memset(zeros16[:], 0.0))

    def chv(q, l):
        off = {"ln1w": 0, "ln1b": 4, "ln2w": 8, "ln2b": 12, "scl": 16,
               "pre": 20, "pim": 24, "lcr": 28, "lci": 32}[q] + l * 40
        return chvec.ap()[:, off:off + 4]

    for l in range(NL):
        V(lambda v, l=l: v.tensor_copy(out=prW[:, l * 4:l * 4 + 4], in_=chv("pre", l)))
        V(lambda v, l=l: v.tensor_copy(out=prX[:, l * 4:l * 4 + 4], in_=chv("pim", l)))
    V(lambda v: v.tensor_mul(out=prY[:], in0=prW[:], in1=prW[:]))
    V(lambda v: v.tensor_mul(out=prZ[:], in0=prX[:], in1=prX[:]))
    V(lambda v: v.tensor_add(out=prY[:], in0=prY[:], in1=prZ[:]))
    V(lambda v: v.tensor_scalar(out=prM[:], in0=prY[:], scalar1=1e-30,
                                scalar2=None, op0=ALU.is_gt), "sV")
    W("scalar", "sV")
    A(lambda a: a.activation(out=rcol[:], in_=prY[:], func=AF.Sqrt), "sA")
    W("vector", "sA")
    V(lambda v: v.tensor_scalar(out=prZ[:], in0=rcol[:], scalar1=1e-30,
                                scalar2=None, op0=ALU.max))
    V(lambda v: v.reciprocal(out=prZ[:], in_=prZ[:]))
    V(lambda v: v.tensor_mul(out=prZ[:], in0=prY[:], in1=prZ[:]))
    V(lambda v: v.tensor_add(out=rcol[:], in0=rcol[:], in1=prZ[:]))
    V(lambda v: v.tensor_scalar(out=rcol[:], in0=rcol[:], scalar1=0.5,
                                scalar2=None, op0=ALU.mult), "sV")
    W("scalar", "sV")
    A(lambda a: a.activation(out=prS[:], in_=rcol[:], func=AF.Tanh), "sA")
    W("vector", "sA")
    V(lambda v: v.tensor_scalar(out=pp[:, 0:16], in0=prS[:], scalar1=RHO_MAX,
                                scalar2=None, op0=ALU.min))
    V(lambda v: v.tensor_scalar(out=prZ[:], in0=pp.ap()[:, 0:16], scalar1=1e-38,
                                scalar2=None, op0=ALU.max), "sV")
    W("scalar", "sV")
    A(lambda a: a.activation(out=pp[:, 16:32], in_=prZ[:], func=AF.Ln), "sA")
    W("vector", "sA")
    V(lambda v: v.tensor_scalar(out=prZ[:], in0=rcol[:], scalar1=1e-30,
                                scalar2=None, op0=ALU.max))
    V(lambda v: v.reciprocal(out=prZ[:], in_=prZ[:]))
    cos1, sin1 = prY, prS
    V(lambda v: v.tensor_mul(out=cos1[:], in0=prW[:], in1=prZ[:]))
    V(lambda v: v.tensor_mul(out=sin1[:], in0=prX[:], in1=prZ[:]))
    V(lambda v: v.select(out=cos1[:], mask=prM[:], on_true=cos1[:], on_false=ones16[:]))
    V(lambda v: v.select(out=sin1[:], mask=prM[:], on_true=sin1[:], on_false=zeros16[:]))
    V(lambda v: v.tensor_scalar(out=prZ[:], in0=prW[:], scalar1=-1.0,
                                scalar2=None, op0=ALU.mult))
    V(lambda v: v.tensor_max(out=prZ[:], in0=prZ[:], in1=prW[:]))
    V(lambda v: v.tensor_add(out=prZ[:], in0=rcol[:], in1=prZ[:]))
    V(lambda v: v.tensor_scalar(out=prZ[:], in0=prZ[:], scalar1=1e-30,
                                scalar2=None, op0=ALU.max))
    V(lambda v: v.reciprocal(out=prZ[:], in_=prZ[:]))
    V(lambda v: v.tensor_mul(out=prZ[:], in0=prX[:], in1=prZ[:]), "sV")
    W("scalar", "sV")
    A(lambda a: a.activation(out=prZ[:], in_=prZ[:], func=AF.Arctan), "sA")
    W("vector", "sA")
    V(lambda v: v.tensor_scalar(out=prT[:], in0=prZ[:],
                                scalar1=float(1.0 / np.pi), scalar2=None,
                                op0=ALU.mult))
    V(lambda v: v.tensor_scalar(out=prU[:], in0=prX[:], scalar1=0.0,
                                scalar2=None, op0=ALU.is_ge))
    V(lambda v: v.tensor_scalar(out=prU[:], in0=prU[:], scalar1=1.0,
                                scalar2=-0.5, op0=ALU.mult, op1=ALU.add))
    V(lambda v: v.tensor_sub(out=prU[:], in0=prU[:], in1=prT[:]))
    V(lambda v: v.tensor_scalar(out=prM[:], in0=prW[:], scalar1=0.0,
                                scalar2=None, op0=ALU.is_lt))
    V(lambda v: v.select(out=pp[:, 32:48], mask=prM[:], on_true=prU[:],
                         on_false=prT[:]))
    for l in range(NL):
        V(lambda v, l=l: v.tensor_copy(out=prW[:, l * 4:l * 4 + 4], in_=chv("lcr", l)))
        V(lambda v, l=l: v.tensor_copy(out=prX[:, l * 4:l * 4 + 4], in_=chv("lci", l)))
    V(lambda v: v.tensor_mul(out=pp[:, 48:64], in0=prW[:], in1=cos1[:]))
    V(lambda v: v.tensor_mul(out=prZ[:], in0=prX[:], in1=sin1[:]))
    V(lambda v: v.tensor_sub(out=pp[:, 48:64], in0=pp.ap()[:, 48:64], in1=prZ[:]))
    V(lambda v: v.tensor_mul(out=pp[:, 64:80], in0=prW[:], in1=sin1[:]))
    V(lambda v: v.tensor_mul(out=prZ[:], in0=prX[:], in1=cos1[:]))
    V(lambda v: v.tensor_add(out=pp[:, 64:80], in0=pp.ap()[:, 64:80], in1=prZ[:]), "sV")

    W("tensor", "sX", sx_small)
    W("tensor", "sV")
    TE(lambda t: t.matmul(pb[7][0:128, 0:16], ones128f.ap()[:], emr.ap()[:],
                          start=True, stop=True), "bd7")
    W("vector", "bd7")
    for e in range(8):
        V(lambda v, e=e: v.tensor_scalar(out=mb[:, e * 16:e * 16 + 16],
                                         in0=pb[7][0:128, 0:16],
                                         scalar1=float(e), scalar2=None,
                                         op0=ALU.is_equal),
          "bf7" if e == 7 else None)
    AUTOJ[0] = False

    xt = [sbA(f"xt{i}", [128, D], F32) for i in range(2)]
    sqs = sbA("sqs", [128, D], F32)
    xnt = sbA("xnt", [128, D], BF16)

    xload = [None] * NT
    xnsv = [None] * NT
    xq = ["sync", "scalar", "gpsimd", "sync", "scalar", "gpsimd", "sync", "scalar"]
    xqs = {"sync": "sX", "scalar": "sS", "gpsimd": "sW"}

    def emit_xload(j):
        q = xq[j]
        if j >= 2:
            W(q, "sV", xnsv[j - 2])
        dma(q, xt[j % 2][:], x_tm_d[j * 128:(j + 1) * 128, :], xqs[q])
        xload[j] = (xqs[q], val(xqs[q]))

    emit_xload(0)
    emit_xload(1)
    for i in range(NT):
        W("vector", *xload[i])
        V(lambda v, i=i: v.reduce_sum(out=statc[:, 0:1], in_=xt[i % 2][:], axis=AX), "sV")
        W("scalar", "sV")
        A(lambda a, i=i: a.activation(out=sqs[:], in_=xt[i % 2][:], func=AF.Square,
                                      accum_out=statc[:, 1:2]), "sA")
        W("vector", "sA")
        V(lambda v: v.tensor_scalar(out=statc[:, 2:3], in0=statc[:, 0:1],
                                    scalar1=float(1.0 / D), scalar2=None, op0=ALU.mult))
        V(lambda v: v.tensor_scalar(out=statc[:, 3:4], in0=statc[:, 1:2],
                                    scalar1=float(1.0 / D), scalar2=None, op0=ALU.mult))
        junk()
        V(lambda v: v.scalar_tensor_tensor(out=statc[:, 4:5], in0=statc[:, 2:3],
                                           scalar=statc[:, 2:3], in1=statc[:, 3:4],
                                           op0=ALU.mult, op1=ALU.subtract), "sV")
        W("scalar", "sV")
        A(lambda a: a.activation(out=statc[:, 5:6], in_=statc[:, 4:5], func=AF.Ln,
                                 scale=-1.0, bias=consts.ap()[:, 0:1]))
        A(lambda a: a.activation(out=statc[:, 6:7], in_=statc[:, 5:6], func=AF.Exp,
                                 scale=-0.5), "sA")
        W("vector", "sA")
        if i >= 1:
            W("vector", "bd4", val("bd4"))
        V(lambda v, i=i: v.tensor_scalar(out=xnt[:], in0=xt[i % 2][:],
                                         scalar1=statc[:, 2:3], scalar2=statc[:, 6:7],
                                         op0=ALU.subtract, op1=ALU.mult), "sV")
        xnsv[i] = val("sV")
        if i + 2 < NT:
            emit_xload(i + 2)
        W("tensor", "sV")
        for gi in range(4):
            bank = 3 + (gi % 2)
            W("tensor", f"bf{bank}", val(f"bf{bank}"))
            for c8 in range(8):
                c = gi * 8 + c8
                TE(lambda t, c=c, c8=c8, bank=bank:
                   t.transpose(pbbf[bank][:, c8 * 128:(c8 + 1) * 128],
                               xnt.ap()[:, c * 128:(c + 1) * 128], ident.ap()[:]),
                   f"bd{bank}" if c8 == 7 else None)
            W("scalar", f"bd{bank}", val(f"bd{bank}"))
            A(lambda a, i=i, gi=gi, bank=bank:
              a.copy(out=xn.ap()[:, i * D + gi * 1024:i * D + (gi + 1) * 1024],
                     in_=pbbf[bank][:]), f"bf{bank}")
    xn_bf3, xn_bf4 = val("bf3"), val("bf4")
    dump("dxn", xn.ap()[:], [("bf3", xn_bf3), ("bf4", xn_bf4)])
    dump("dpp", pp.ap()[:], [("sV", val("sV"))])
    xn_apply_sv = val("sV")

    def xn_lhs(i, c):
        return xn.ap()[:, i * D + c * 128:i * D + c * 128 + 128]

    def xn_rhs(c, tc):
        return xnv[:, 4 * tc:4 * tc + 4, c * 128:(c + 1) * 128]

    ctxA.close()
    ctxB = ExitStack()
    sbB = lambda *a: ctxB.enter_context(nc.sbuf_tensor(*a))
    winT_sb = sbB("winT_sb", [128, 16 * DIM], BF16)
    atT_sb = sbB("atT_sb", [128, 16 * 128], BF16)
    b2T_sb = sbB("b2T_sb", [128, NO * 128], BF16)
    routT_sb = sbB("routT_sb", [128, NV * 16], BF16)
    bstream = [sbB(f"bstream{i}", [128, NV * 128], BF16) for i in range(2)]
    logits = sbB("logits", [128, NT * 16], F32)
    wsel = sbB("wsel", [128, NT * 8], BF16)
    cnt = sbB("cnt", [128, 16], F32)
    mskB = sbB("mskB", [128, 16], F32)
    cmpa = sbB("cmpa", [128, 256], F32)
    escb = sbB("escb", [128, 16], F32)
    wfm = sbB("wfm", [8, T], BF16)
    wexp_sb = sbB("wexp_sb", [128, T], BF16)
    hwb = sbB("hwb", [128, T], BF16)

    W("gpsimd", "sV", xn_apply_sv)
    W("gpsimd", "sA", val("sA"))
    dma("gpsimd", routT_sb[:], routerT_d, "sW")
    dma("gpsimd", b2T_sb[:], b2T_d, "sW")
    dma("gpsimd", atT_sb[:], atT_d[0], "sW")
    sw_rt = val("sW")

    W("tensor", "sW", sw_rt)
    W("tensor", "bf3", xn_bf3)
    W("tensor", "bf4", xn_bf4)
    for i in range(NT):
        W("tensor", "bf7", val("bf7"))
        for c in range(NV):
            TE(lambda t, i=i, c=c: t.matmul(pb[7][:, 0:16], xn_lhs(i, c),
                                            routT_sb.ap()[:, c * 16:(c + 1) * 16],
                                            start=(c == 0), stop=False))
        TE(lambda t: t.matmul(pb[7][:, 0:16], onesr.ap()[0:1, 0:128], routbr.ap()[:],
                              start=False, stop=True), "bd7")
        W("vector", "bd7", val("bd7"))
        V(lambda v, i=i: v.tensor_copy(out=logits[:, i * 16:(i + 1) * 16],
                                       in_=pb[7][:, 0:16]), "bf7")
    dump("dmb", mb.ap()[:], [("sV", val("sV"))])
    for i in range(NT):
        lg = logits.ap()[:, i * 16:(i + 1) * 16]
        V(lambda v, lg=lg: v.reduce_max(out=statc[:, 8:9], in_=lg, axis=AX,
                                        negate=True), "sV")
        W("scalar", "sV")
        A(lambda a, lg=lg: a.activation(out=escb[:], in_=lg, func=AF.Exp,
                                        bias=statc[:, 8:9], accum_out=statc[:, 9:10]), "sA")
        W("vector", "sA")
        V(lambda v: v.reciprocal(out=statc[:, 10:11], in_=statc[:, 9:10]))
        for j in range(15):
            V(lambda v, j=j: v.tensor_scalar(out=cmpa[:, j * 16:(j + 1) * 16],
                                             in0=escb[:],
                                             scalar1=escb[:, j:j + 1],
                                             scalar2=None, op0=ALU.is_gt))
        junk()
        for j in range(15):
            V(lambda v, j=j: v.reduce_sum(out=cnt[:, j:j + 1],
                                          in_=cmpa.ap()[:, j * 16:(j + 1) * 16],
                                          axis=AX))
        V(lambda v: v.memset(cnt[:, 15:16], 100.0))
        junk()
        if i == 0:
            V(lambda v: v.engine_nop(), "sV")
            dump("desc", escb.ap()[:], [("sV", val("sV"))])
            dump("dcnt", cnt.ap()[:], [("sV", val("sV"))])
        V(lambda v: v.tensor_scalar(out=mskB[:], in0=cnt[:], scalar1=7.5,
                                    scalar2=None, op0=ALU.is_lt))
        junk()
        V(lambda v: v.tensor_mul(out=escb[:], in0=escb[:], in1=mskB[:]))
        junk()
        for e in range(8):
            V(lambda v, e=e: v.tensor_mul(out=cmpa[:, e * 16:(e + 1) * 16],
                                          in0=escb[:],
                                          in1=mb.ap()[:, e * 16:e * 16 + 16]))
        junk()
        for e in range(8):
            V(lambda v, e=e: v.reduce_sum(out=msk8[:, e:e + 1],
                                          in_=cmpa.ap()[:, e * 16:(e + 1) * 16],
                                          axis=AX))
        junk()
        V(lambda v, i=i: v.tensor_scalar(out=wsel[:, i * 8:i * 8 + 8],
                                         in0=msk8.ap()[:],
                                         scalar1=statc[:, 10:11], scalar2=None,
                                         op0=ALU.mult), "sV")
    w_done = val("sV")
    dump("dlog", logits.ap()[:], [("sV", w_done)])
    dump("dw", wsel.ap()[:], [("sV", w_done)])
    W("tensor", "sV", w_done)
    for i in range(NT):
        W("tensor", "bf2", val("bf2"))
        TE(lambda t, i=i: t.transpose(pbbf[2][0:8, 0:128],
                                      wsel.ap()[:, i * 8:(i + 1) * 8], ident.ap()[:]),
           "bd2")
        W("vector", "bd2", val("bd2"))
        V(lambda v, i=i: v.tensor_copy(out=wfm[:, i * 128:(i + 1) * 128],
                                       in_=pbbf[2][0:8, 0:128]), "bf2")
    wfm_sv = val("sV")
    wfm_bf2 = val("bf2")
    for tc in range(2):
        W("tensor", "bf6", val("bf6"))
        for half in range(2):
            if not (tc == 0 and half == 0):
                W("gpsimd", "sT", val("sT"))
                dma("gpsimd", atT_sb[:], atT_d[half], "sW")
                W("tensor", "sW", val("sW"))
            for c16 in range(16):
                cc = half * 16 + c16
                TE(lambda t, c16=c16, cc=cc, tc=tc:
                   t.matmul(pb[6][:], atT_sb.ap()[:, c16 * 128:(c16 + 1) * 128],
                            xn_rhs(cc, tc), start=(cc == 0), stop=(cc == NV - 1)),
                   "sT" if c16 == 15 else None)
        hA_done = val("sT")
        W("tensor", "sV", wfm_sv)
        W("tensor", "bf2", wfm_bf2)
        W("tensor", "bf5", val("bf5"))
        TE(lambda t, tc=tc: t.matmul(pb[5][:], e16.ap()[:],
                                     wfm.ap()[0:8, tc * 512:(tc + 1) * 512],
                                     start=True, stop=True), "bd5")
        W("scalar", "bd5", val("bd5"))
        A(lambda a, tc=tc: a.copy(out=wexp_sb[:, tc * 512:(tc + 1) * 512],
                                  in_=pb[5][:]), "bf5")
        W("vector", "sT", hA_done)
        W("vector", "bf5", val("bf5"))
        V(lambda v, tc=tc: v.tensor_mul(out=hwb[:, tc * 512:(tc + 1) * 512],
                                        in0=pb[6][:],
                                        in1=wexp_sb.ap()[:, tc * 512:(tc + 1) * 512]),
          "bf6")
    hw_sv = val("sV")
    hw_bf6 = val("bf6")
    dump("dhw", hwb.ap()[:], [("bf6", hw_bf6)])

    win_done = [("sT", 0)]
    for half in range(2):
        if half == 1:
            W("gpsimd", *win_done[0])
        dma("gpsimd", winT_sb[:], winT_d[half], "sW")
        W("tensor", "sW", val("sW"))
        for i in range(NT):
            W("tensor", "bf2", val("bf2"))
            for c16 in range(16):
                last = (half == 1 and c16 == 15)
                TE(lambda t, i=i, c16=c16, half=half, last=last:
                   t.matmul(pb[2][:], xn_lhs(i, half * 16 + c16),
                            winT_sb.ap()[:, c16 * DIM:(c16 + 1) * DIM],
                            start=(c16 == 0), stop=last),
                   "bd2" if last else None)
            if half == 0:
                TE(lambda t: t.matmul(pb[2][:], onesr.ap()[0:1, 0:128], binr.ap()[:],
                                      start=False, stop=True),
                   "bd2")
                if i == NT - 1:
                    win_done[0] = ("bd2", val("bd2"))
                W("vector", "bd2", val("bd2"))
                V(lambda v, i=i: v.tensor_copy(out=hv[:, i, :], in_=pb[2][:]), "bf2")
            else:
                W("vector", "bd2", val("bd2"))
                V(lambda v, i=i: v.tensor_add(out=hv[:, i, :], in0=hv[:, i, :],
                                              in1=pb[2][:]), "bf2")
    h_sv = val("sV")
    dump("dh0", h.ap()[:], [("bf2", val("bf2"))])

    W("tensor", "bf6", hw_bf6)
    sw_base0 = val("sW")
    base_done = []
    for oi in range(NO):
        buf = oi % 2
        if oi >= 2:
            W("gpsimd", *base_done[oi - 2])
        dma("gpsimd", bstream[buf][:], baseT_d[oi], "sW")
        W("tensor", "sW", sw_base0 + 16 * (oi + 1))
        for tc in range(2):
            bank = tc
            W("tensor", f"bf{bank}", val(f"bf{bank}"))
            for c in range(NV):
                TE(lambda t, buf=buf, c=c, tc=tc, bank=bank:
                   t.matmul(pb[bank][:], bstream[buf].ap()[:, c * 128:(c + 1) * 128],
                            xn_rhs(c, tc), start=(c == 0), stop=False))
            TE(lambda t, oi=oi, tc=tc, bank=bank:
               t.matmul(pb[bank][:], b2T_sb.ap()[:, oi * 128:(oi + 1) * 128],
                        hwb.ap()[:, tc * 512:(tc + 1) * 512], start=False, stop=True),
               f"bd{bank}")
            if tc == 1:
                base_done.append((f"bd{bank}", val(f"bd{bank}")))
            W("scalar", f"bd{bank}", val(f"bd{bank}"))
            A(lambda a, oi=oi, tc=tc, bank=bank:
              a.activation(out=g[:, oi * T + tc * 512:oi * T + (tc + 1) * 512],
                           in_=pb[bank][:], func=AF.Silu,
                           bias=basebc.ap()[:, oi:oi + 1]), f"bf{bank}")
    g_bf0, g_bf1 = val("bf0"), val("bf1")
    dump("dg", g.ap()[:], [("bf0", g_bf0), ("bf1", g_bf1)])

    ctxB.close()
    ctxXN.close()
    ctxC = ExitStack()
    sbC = lambda *a: ctxC.enter_context(nc.sbuf_tensor(*a))
    iota = sbC("iota_sb", [128, T], F32)
    zt = sbC("zt", [128, DIM], BF16)
    zfm = sbC("zfm", [128, NT * DIM], BF16)
    z2fm = sbC("z2fm", [128, NT * DIM], BF16)
    yc = sbC("yc", [128, T], F32)
    scr = [sbC(f"scr{i}", [128, T], F32) for i in range(3)]
    u0 = sbC("u0", [128, 4 * T], F32)
    v0 = sbC("v0", [128, 4 * T], F32)
    ct = sbC("ct", [128, T], F32)
    st = sbC("st", [128, T], F32)
    rp = sbC("rp", [128, T], BF16)
    z1s = sbC("z1s", [128, 8 * T], BF16)
    w1sb = [sbC(f"w1sb{i}", [128, ND * 128], BF16) for i in range(4)]
    w2sb = sbC("w2sb", [128, NF * DIM], BF16)
    outk = sbC("outk", [128, T], BF16)

    sqs2ap = scr[1].ap()[:, 0:DIM]
    zfmv = zfm.ap().rearrange("p (i r) -> p i r", i=NT)
    z2fmv = z2fm.ap().rearrange("p (i r) -> p i r", i=NT)
    gathv = gath.ap().rearrange("p (cc j) -> p cc j", cc=8)

    W("gpsimd", "bf0", g_bf0)
    W("gpsimd", "bf1", g_bf1)
    W("gpsimd", "sV", val("sV"))
    dma("gpsimd", iota[:], iota_d, "sW")
    dma("gpsimd", w2sb[:], w2T_d[0], "sW")
    sw_w2 = val("sW")

    tlb, wre, wim = scr[0], scr[1], scr[2]
    w1_done = []
    w2_done = [None]

    def emit_ln_to_fm(dstv):
        inv_n = float(1.0 / DIM)
        for i in range(NT):
            V(lambda v, i=i: v.reduce_sum(out=lnst[:, i:i + 1], in_=hv[:, i, :],
                                          axis=AX), "sV" if i == NT - 1 else None)
        W("scalar", "sV")
        for i in range(NT):
            A(lambda a, i=i: a.activation(out=sqs2ap, in_=hv[:, i, :], func=AF.Square,
                                          accum_out=lnst[:, 8 + i:9 + i]),
              "sA" if i == NT - 1 else None)
        W("vector", "sA")
        V(lambda v: v.tensor_scalar(out=lnst[:, 16:24], in0=lnst.ap()[:, 0:8],
                                    scalar1=inv_n, scalar2=None, op0=ALU.mult))
        V(lambda v: v.tensor_scalar(out=lnst[:, 8:16], in0=lnst.ap()[:, 8:16],
                                    scalar1=inv_n, scalar2=None, op0=ALU.mult))
        junk()
        V(lambda v: v.tensor_mul(out=lnst[:, 0:8], in0=lnst.ap()[:, 16:24],
                                 in1=lnst.ap()[:, 16:24]))
        junk()
        V(lambda v: v.tensor_sub(out=lnst[:, 0:8], in0=lnst.ap()[:, 0:8],
                                 in1=lnst.ap()[:, 8:16]), "sV")
        W("scalar", "sV")
        A(lambda a: a.activation(out=lnst[:, 8:16], in_=lnst.ap()[:, 0:8], func=AF.Ln,
                                 scale=-1.0, bias=consts.ap()[:, 0:1]))
        A(lambda a: a.activation(out=lnst[:, 24:32], in_=lnst.ap()[:, 8:16],
                                 func=AF.Exp, scale=-0.5), "sA")
        W("vector", "sA")
        for i in range(NT):
            W("vector", f"bd{3 + ((i + 1) % 2)}", val(f"bd{3 + ((i + 1) % 2)}"))
            V(lambda v, i=i: v.tensor_scalar(out=zt[:], in0=hv[:, i, :],
                                             scalar1=lnst[:, 16 + i:17 + i],
                                             scalar2=lnst[:, 24 + i:25 + i],
                                             op0=ALU.subtract, op1=ALU.mult), "sV")
            bank = 3 + (i % 2)
            W("tensor", "sV")
            W("tensor", f"bf{bank}", val(f"bf{bank}"))
            for c in range(ND):
                TE(lambda t, c=c, bank=bank:
                   t.transpose(pbbf[bank][:, c * 128:(c + 1) * 128],
                               zt.ap()[:, c * 128:(c + 1) * 128], ident.ap()[:]),
                   f"bd{bank}" if c == ND - 1 else None)
            W("scalar", f"bd{bank}", val(f"bd{bank}"))
            A(lambda a, i=i, bank=bank, dstv=dstv:
              a.copy(out=dstv[:, i, :], in_=pbbf[bank][:, 0:512]), f"bf{bank}")

    def emit_table_fracs(pc, sem=None):
        V(lambda v, pc=pc: v.tensor_scalar(out=statc[:, 11:12],
                                           in0=pp.ap()[:, 32 + pc:33 + pc],
                                           scalar1=meta.ap()[:, 0:1], scalar2=None,
                                           op0=ALU.mult))
        junk()
        V(lambda v, pc=pc: v.tensor_scalar(out=tlb[:], in0=iota[:],
                                           scalar1=pp.ap()[:, 32 + pc:33 + pc],
                                           scalar2=statc[:, 11:12],
                                           op0=ALU.mult, op1=ALU.add))
        V(lambda v: v.tensor_scalar(out=wre[:], in0=tlb[:], scalar1=MAGIC,
                                    scalar2=-MAGIC, op0=ALU.add, op1=ALU.add))
        V(lambda v: v.tensor_sub(out=wre[:], in0=tlb[:], in1=wre[:]))
        V(lambda v: v.tensor_scalar(out=wim[:], in0=tlb[:], scalar1=0.25,
                                    scalar2=MAGIC, op0=ALU.add, op1=ALU.add))
        V(lambda v: v.tensor_scalar(out=wim[:], in0=wim[:], scalar1=-MAGIC,
                                    scalar2=None, op0=ALU.add))
        V(lambda v: v.tensor_sub(out=wim[:], in0=tlb[:], in1=wim[:]), sem)

    def emit_table_sins(sem=None):
        A(lambda a: a.activation(out=st[:], in_=wre[:], func=AF.Sin, scale=TWO_PI))
        A(lambda a: a.activation(out=ct[:], in_=wim[:], func=AF.Sin,
                                 scale=TWO_PI, bias=consts.ap()[:, 1:2]), sem)

    sw_w2_cur = [sw_w2]
    for l in range(NL):
        lb = l * 40
        emit_ln_to_fm(zfmv)

        W("vector", "bf3", val("bf3"))
        W("vector", "bf4", val("bf4"))
        for c in range(4):
            pc = l * 4 + c
            V(lambda v, c=c, lb=lb: v.tensor_scalar(
                out=yc[:], in0=zfmv[:, 0:NT, c * 128:(c + 1) * 128],
                scalar1=chvec.ap()[:, lb + 0 + c:lb + 1 + c],
                scalar2=chvec.ap()[:, lb + 4 + c:lb + 5 + c],
                op0=ALU.mult, op1=ALU.add))
            emit_table_fracs(pc, "sV")
            W("scalar", "sV")
            A(lambda a: a.activation(out=yc[:], in_=yc[:], func=AF.Silu))
            emit_table_sins("sA")
            if l == 0 and c == 0:
                dump("dyc", yc.ap()[:], [("sA", val("sA"))])
                dump("dct", ct.ap()[:], [("sA", val("sA"))])
                dump("dst", st.ap()[:], [("sA", val("sA"))])
            W("vector", "sA")
            V(lambda v: v.tensor_mul(out=wre[:], in0=ct[:], in1=yc[:]))
            V(lambda v: v.tensor_mul(out=wim[:], in0=st[:], in1=yc[:]))
            V(lambda v: v.tensor_scalar(out=wim[:], in0=wim[:], scalar1=-1.0,
                                        scalar2=None, op0=ALU.mult))
            rhoap = pp.ap()[:, 0 + pc:1 + pc]
            for seg in range(2):
                s0, s1 = seg * 512, (seg + 1) * 512
                V(lambda v, c=c, s0=s0, s1=s1, seg=seg, rhoap=rhoap:
                  v.tensor_tensor_scan(out=u0.ap()[:, c * T + s0:c * T + s1],
                                       data0=rhoap.broadcast_to([128, 512]),
                                       data1=wre.ap()[:, s0:s1],
                                       initial=(zero128.ap()[:] if seg == 0 else
                                                u0.ap()[:, c * T + 511:c * T + 512]),
                                       op0=ALU.mult, op1=ALU.add))
                V(lambda v, c=c, s0=s0, s1=s1, seg=seg, rhoap=rhoap:
                  v.tensor_tensor_scan(out=v0.ap()[:, c * T + s0:c * T + s1],
                                       data0=rhoap.broadcast_to([128, 512]),
                                       data1=wim.ap()[:, s0:s1],
                                       initial=(zero128.ap()[:] if seg == 0 else
                                                v0.ap()[:, c * T + 511:c * T + 512]),
                                       op0=ALU.mult, op1=ALU.add))
            V(lambda v, c=c: v.tensor_copy(out=finals[:, c:c + 1],
                                           in_=u0.ap()[:, c * T + 1023:c * T + 1024]))
            V(lambda v, c=c: v.tensor_copy(out=finals[:, 4 + c:5 + c],
                                           in_=v0.ap()[:, c * T + 1023:c * T + 1024]),
              "sV")
        scans_sv = val("sV")
        if l == 0:
            dump("du0", u0.ap()[:], [("sV", scans_sv)])

        W("gpsimd", "sV", scans_sv)
        dma("gpsimd", contrib_d.ap(), finals[:], "sW")
        W("gpsimd", "sW", val("sW"))
        tgt = incd("sC")
        E("gpsimd", lambda gp, tgt=tgt: gp.collective_compute(
            "AllGather", ALU.bypass, replica_groups=[[0, 1, 2, 3, 4, 5, 6, 7]],
            ins=[contrib_d.ap().opt()], outs=[gathered_d.ap().opt()]).then_inc(*tgt))
        W("gpsimd", "sC", val("sC"))
        dma("gpsimd", gath.ap().rearrange("p (c j) -> p c j", c=8),
            gathered_d.ap().rearrange("(c p) j -> p c j", p=128), "sW")
        gath_w = val("sW")
        if l == 0:
            dump("dgath", gath.ap()[:], [("sW", gath_w)])

        W("vector", "sW", gath_w)
        W("scalar", "sW", gath_w)
        for c in range(4):
            pc = l * 4 + c
            lnr = pp.ap()[:, 16 + pc:17 + pc]
            emit_table_fracs(pc)
            V(lambda v, lnr=lnr: v.tensor_scalar(out=tlb[:], in0=iota[:], scalar1=lnr,
                                                 scalar2=lnr, op0=ALU.mult, op1=ALU.add),
              "sV")
            W("scalar", "sV")
            for j in range(8):
                A(lambda a, j=j, lnr=lnr: a.activation(out=gcoef[:, j:j + 1], in_=lnr,
                                                       func=AF.Exp,
                                                       scale=meta.ap()[:, 1 + j:2 + j]))
            A(lambda a, lnr=lnr: a.activation(out=statc[:, 12:13], in_=lnr, func=AF.Exp,
                                              scale=meta.ap()[:, 9:10]))
            emit_table_sins()
            A(lambda a: a.activation(out=rp[:], in_=tlb[:], func=AF.Exp), "sA")
            W("vector", "sA")
            V(lambda v, c=c: v.tensor_mul(out=msk8[:], in0=gcoef[:],
                                          in1=gathv[:, :, c]))
            V(lambda v, c=c: v.tensor_mul(out=mskc[:], in0=gcoef[:],
                                          in1=gathv[:, :, 4 + c]))
            junk()
            V(lambda v: v.reduce_sum(out=acar[:, 0:1], in_=msk8.ap()[:], axis=AX))
            V(lambda v: v.reduce_sum(out=acar[:, 1:2], in_=mskc.ap()[:], axis=AX))
            junk()
            V(lambda v, pc=pc: v.scalar_tensor_tensor(out=acar[:, 0:1],
                                                      in0=statc[:, 12:13],
                                                      scalar=pp.ap()[:, 48 + pc:49 + pc],
                                                      in1=acar.ap()[:, 0:1],
                                                      op0=ALU.mult, op1=ALU.add))
            V(lambda v, pc=pc: v.scalar_tensor_tensor(out=acar[:, 1:2],
                                                      in0=statc[:, 12:13],
                                                      scalar=pp.ap()[:, 64 + pc:65 + pc],
                                                      in1=acar.ap()[:, 1:2],
                                                      op0=ALU.mult, op1=ALU.add))
            junk()
            V(lambda v, c=c: v.scalar_tensor_tensor(out=wre[:], in0=rp[:],
                                                    scalar=acar[:, 0:1],
                                                    in1=u0.ap()[:, c * T:(c + 1) * T],
                                                    op0=ALU.mult, op1=ALU.add))
            V(lambda v, c=c: v.scalar_tensor_tensor(out=wim[:], in0=rp[:],
                                                    scalar=acar[:, 1:2],
                                                    in1=v0.ap()[:, c * T:(c + 1) * T],
                                                    op0=ALU.mult, op1=ALU.add))
            V(lambda v: v.tensor_mul(out=wre[:], in0=ct[:], in1=wre[:]))
            V(lambda v: v.tensor_mul(out=wim[:], in0=st[:], in1=wim[:]))
            V(lambda v: v.tensor_sub(out=tlb[:], in0=wre[:], in1=wim[:]))
            V(lambda v, lb=lb, c=c: v.tensor_scalar(
                out=outk[:], in0=tlb[:],
                scalar1=chvec.ap()[:, lb + 16 + c:lb + 17 + c],
                scalar2=None, op0=ALU.mult), "sV")
            if l == 0 and c == 0:
                dump("doutk", outk.ap()[:], [("sV", val("sV"))])
            W("tensor", "sV", val("sV"))
            for grp in range(2):
                bank = 3 + grp
                W("tensor", f"bf{bank}", val(f"bf{bank}"))
                for i4 in range(4):
                    i = grp * 4 + i4
                    TE(lambda t, c=c, i=i, i4=i4, bank=bank:
                       t.transpose(pbbf[bank][:, i4 * 128:(i4 + 1) * 128],
                                   outk.ap()[:, i * 128:(i + 1) * 128],
                                   ident.ap()[:]), f"bd{bank}" if i4 == 3 else None)
                W("vector", f"bd{bank}", val(f"bd{bank}"))
                V(lambda v, c=c, grp=grp, bank=bank:
                  v.tensor_add(out=hv[:, grp * 4:grp * 4 + 4, c * 128:(c + 1) * 128],
                               in0=hv[:, grp * 4:grp * 4 + 4, c * 128:(c + 1) * 128],
                               in1=pbbf[bank].rearrange("p (i j) -> p i j", i=8)[:, 0:4, :]),
                  f"bf{bank}")
        hupd_sv = val("sV")
        dump(f"dh{l + 1}", h.ap()[:], [("sV", hupd_sv), ("bf3", val("bf3")),
                                       ("bf4", val("bf4"))])

        if w2_done[0] is not None:
            W("gpsimd", *w2_done[0])
        dma("gpsimd", b2r[:], b2r_d[:, l * DIM:(l + 1) * DIM], "sW")
        sw_b2r = val("sW")
        W("vector", "sV", hupd_sv)
        emit_ln_to_fm(z2fmv)
        z2_bf3, z2_bf4 = val("bf3"), val("bf4")
        W("vector", "bf3", z2_bf3)
        W("vector", "bf4", z2_bf4)
        for dd in range(ND):
            V(lambda v, dd=dd, lb=lb: v.tensor_scalar(
                out=z2fmv[:, 0:NT, dd * 128:(dd + 1) * 128],
                in0=z2fmv[:, 0:NT, dd * 128:(dd + 1) * 128],
                scalar1=chvec.ap()[:, lb + 8 + dd:lb + 9 + dd],
                scalar2=chvec.ap()[:, lb + 12 + dd:lb + 13 + dd],
                op0=ALU.mult, op1=ALU.add),
              "sV" if dd == ND - 1 else None)
        z2g_sv = val("sV")
        W("tensor", "sV", z2g_sv)

        for half in range(2):
            for f8 in range(8):
                fi = half * 8 + f8
                slot = fi % 4
                kglob = l * NF + fi
                if kglob >= 4:
                    W("gpsimd", *w1_done[kglob - 4])
                dma("gpsimd", w1sb[slot][:], w1T_d[l, fi], "sW")
                sw_f = val("sW")
                W("tensor", "sW", sw_f)
                for tc in range(2):
                    bank = 5 + tc
                    W("tensor", f"bf{bank}", val(f"bf{bank}"))
                    for d in range(ND):
                        TE(lambda t, slot=slot, d=d, tc=tc, bank=bank:
                           t.matmul(pb[bank][:],
                                    w1sb[slot].ap()[:, d * 128:(d + 1) * 128],
                                    z2fmv[:, 4 * tc:4 * tc + 4, d * 128:(d + 1) * 128],
                                    start=(d == 0), stop=(d == ND - 1)),
                           f"bd{bank}" if d == ND - 1 else None)
                    W("scalar", f"bd{bank}", val(f"bd{bank}"))
                    A(lambda a, f8=f8, tc=tc, bank=bank, fi=fi, l=l:
                      a.activation(out=z1s[:, f8 * T + tc * 512:f8 * T + (tc + 1) * 512],
                                   in_=pb[bank][:], func=AF.Silu,
                                   bias=b1c.ap()[:, l * NF + fi:l * NF + fi + 1]),
                      f"bf{bank}")
                w1_done.append(("bd6", val("bd6")))
            W("tensor", "sW", sw_w2_cur[0])
            W("tensor", "sW", sw_b2r)
            W("tensor", "bf5", val("bf5"))
            W("tensor", "bf6", val("bf6"))
            for i in range(NT):
                wb = 2 if i % 2 == 0 else 7
                W("tensor", f"bf{wb}", val(f"bf{wb}"))
                for f8 in range(8):
                    last = (half == 0 and f8 == 7)
                    TE(lambda t, i=i, f8=f8, half=half, last=last, wb=wb:
                       t.matmul(pb[wb][:],
                                z1s.ap()[:, f8 * T + i * 128:f8 * T + (i + 1) * 128],
                                w2sb.ap()[:, (half * 8 + f8) * DIM:(half * 8 + f8 + 1) * DIM],
                                start=(f8 == 0), stop=last),
                       f"bd{wb}" if last else None)
                if half == 1:
                    TE(lambda t, wb=wb: t.matmul(pb[wb][:], onesr.ap()[0:1, 0:128],
                                                 b2r.ap()[:], start=False, stop=True),
                       f"bd{wb}")
                    if i == NT - 1:
                        w2_done[0] = (f"bd{wb}", val(f"bd{wb}"))
                W("vector", f"bd{wb}", val(f"bd{wb}"))
                V(lambda v, i=i, wb=wb: v.tensor_add(out=hv[:, i, :], in0=hv[:, i, :],
                                                     in1=pb[wb][:]), f"bf{wb}")
        if l + 1 < NL:
            W("gpsimd", *w2_done[0])
            dma("gpsimd", w2sb[:], w2T_d[l + 1], "sW")
            sw_w2_cur[0] = val("sW")

    z3fmv = z2fmv
    emit_ln_to_fm(z3fmv)
    W("vector", "bf3", val("bf3"))
    W("vector", "bf4", val("bf4"))
    for dd in range(ND):
        V(lambda v, dd=dd: v.tensor_scalar(
            out=z3fmv[:, 0:NT, dd * 128:(dd + 1) * 128],
            in0=z3fmv[:, 0:NT, dd * 128:(dd + 1) * 128],
            scalar1=chvec.ap()[:, 160 + dd:161 + dd],
            scalar2=chvec.ap()[:, 164 + dd:165 + dd],
            op0=ALU.mult, op1=ALU.add),
          "sV" if dd == ND - 1 else None)
    z3g_sv = val("sV")
    W("tensor", "sV", z3g_sv)
    xbuf = yc
    obuf = scr[0]

    wout_done = []
    for vi in range(NO):
        slot = vi % 4
        if vi < 4:
            W("gpsimd", *w1_done[(NL - 1) * NF + 12 + vi])
        else:
            W("gpsimd", *wout_done[vi - 4])
        dma("gpsimd", w1sb[slot][:], woutT_d[vi], "sW")
        sw_v = val("sW")
        W("tensor", "sW", sw_v)
        for tc in range(2):
            bank = tc
            W("scalar", "sV", val("sV"))
            dma("scalar", xbuf.ap()[:, tc * 512:(tc + 1) * 512],
                x_fm_d[vi * 128:(vi + 1) * 128, tc * 512:(tc + 1) * 512], "sS")
            sw_x = val("sS")
            W("tensor", f"bf{bank}", val(f"bf{bank}"))
            for d in range(ND):
                TE(lambda t, slot=slot, d=d, tc=tc, bank=bank:
                   t.matmul(pb[bank][:], w1sb[slot].ap()[:, d * 128:(d + 1) * 128],
                            z3fmv[:, 4 * tc:4 * tc + 4, d * 128:(d + 1) * 128],
                            start=(d == 0), stop=(d == ND - 1)),
                   f"bd{bank}" if d == ND - 1 else None)
            if tc == 1:
                wout_done.append((f"bd{bank}", val(f"bd{bank}")))
            W("vector", f"bd{bank}", val(f"bd{bank}"))
            W("vector", "sS", sw_x)
            W("vector", "sO", max(0, val("sO") - 16))
            V(lambda v, vi=vi, tc=tc, bank=bank:
              v.scalar_tensor_tensor(out=obuf.ap()[:, tc * 512:(tc + 1) * 512],
                                     in0=pb[bank][:], scalar=boutc.ap()[:, vi:vi + 1],
                                     in1=g.ap()[:, vi * T + tc * 512:vi * T + (tc + 1) * 512],
                                     op0=ALU.add, op1=ALU.mult), f"bf{bank}")
            V(lambda v, tc=tc: v.tensor_add(out=obuf.ap()[:, tc * 512:(tc + 1) * 512],
                                            in0=obuf.ap()[:, tc * 512:(tc + 1) * 512],
                                            in1=xbuf.ap()[:, tc * 512:(tc + 1) * 512]),
              "sV")
            W("scalar", "sV", val("sV"))
            dma("scalar", outT_d[vi * 128:(vi + 1) * 128, tc * 512:(tc + 1) * 512],
                obuf.ap()[:, tc * 512:(tc + 1) * 512], "sO")

    W("gpsimd", "sO", val("sO"))

    with nc.Block() as block:
        @block.sync
        def _(sync):
            for f in ops["sync"]:
                f(sync)

        @block.scalar
        def _(scalar):
            for f in ops["scalar"]:
                f(scalar)

        @block.vector
        def _(vector):
            for f in ops["vector"]:
                f(vector)

        @block.tensor
        def _(tensor):
            for f in ops["tensor"]:
                f(tensor)

        @block.gpsimd
        def _(gpsimd):
            for f in ops["gpsimd"]:
                f(gpsimd)

    es.close()
    return nc


_NC_CACHE = {}


def _host_layouts(inputs):
    f32 = np.float32
    W1 = inputs["L0_attn_W1"].astype(f32)
    W2 = inputs["L0_attn_W2"].astype(f32)
    Win = inputs["L0_attn_Win"].astype(f32)
    Wout = inputs["L0_attn_Wout"].astype(f32)
    baseW = inputs["L0_ffn_base_W"].astype(f32)
    routW = inputs["L0_ffn_router_W"].astype(f32)
    Am = inputs["L0_ffn_A"].astype(f32)
    Bm = inputs["L0_ffn_B"].astype(f32)

    c = np.ascontiguousarray
    out = {}
    WinT = Win.T.reshape(2, 16, 128, 512)
    out["winT"] = c(WinT.transpose(0, 2, 1, 3).reshape(2, 128, 16 * 512))
    BT = baseW.T.reshape(NV, 128, NO, 128)
    out["baseT"] = c(BT.transpose(2, 1, 0, 3).reshape(NO, 128, NV * 128))
    AT = Am.reshape(128, D).T.reshape(2, 16, 128, 128)
    out["atT"] = c(AT.transpose(0, 2, 1, 3).reshape(2, 128, 16 * 128))
    out["b2T"] = c(Bm.transpose(0, 2, 1).reshape(128, D))
    RT = np.zeros((D, 16), f32)
    RT[:, :15] = routW.T
    out["routerT"] = c(RT.reshape(NV, 128, 16).transpose(1, 0, 2).reshape(128, NV * 16))
    W1T = W1.transpose(0, 2, 1).reshape(NL, ND, 128, NF, 128)
    out["w1T"] = c(W1T.transpose(0, 3, 2, 1, 4).reshape(NL, NF, 128, ND * 128))
    W2T = W2.transpose(0, 2, 1).reshape(NL, NF, 128, DIM)
    out["w2T"] = c(W2T.transpose(0, 2, 1, 3).reshape(NL, 128, NF * DIM))
    WoT = Wout.T.reshape(ND, 128, NO, 128)
    out["woutT"] = c(WoT.transpose(2, 1, 0, 3).reshape(NO, 128, ND * 128))
    out["binr"] = c(inputs["L0_attn_bin"].astype(f32).reshape(1, DIM))
    out["b1c"] = c(inputs["L0_attn_b1"].astype(f32).reshape(NL, NF, 128)
                   .transpose(2, 0, 1).reshape(128, NL * NF))
    out["b2r"] = c(inputs["L0_attn_b2"].astype(f32).reshape(1, NL * DIM))
    out["basebc"] = c(inputs["L0_ffn_base_b"].astype(f32).reshape(NO, 128).T)
    out["boutc"] = c(inputs["L0_attn_bout"].astype(f32).reshape(NO, 128).T)
    rb = np.full((1, 16), -1e30, f32)
    rb[0, :15] = inputs["L0_ffn_router_b"].astype(f32)
    out["routbr"] = rb
    em = np.full((1, 16), -1.0, f32)
    em[0, :15] = inputs["L0_ffn_expert_map"].astype(f32)
    out["emr"] = em
    ch = np.zeros((128, 176), f32)
    packs = [("ln1w", inputs["L0_attn_ln1_w"]), ("ln1b", inputs["L0_attn_ln1_b"]),
             ("ln2w", inputs["L0_attn_ln2_w"]), ("ln2b", inputs["L0_attn_ln2_b"]),
             ("scl", inputs["L0_attn_sc_lin"]), ("pre", inputs["L0_attn_phz_re"]),
             ("pim", inputs["L0_attn_phz_im"]), ("lcr", inputs["L0_attn_lc_re"]),
             ("lci", inputs["L0_attn_lc_im"])]
    offs = {"ln1w": 0, "ln1b": 4, "ln2w": 8, "ln2b": 12, "scl": 16,
            "pre": 20, "pim": 24, "lcr": 28, "lci": 32}
    for l in range(NL):
        for name, arr in packs:
            a4 = np.asarray(arr, f32)[l].reshape(4, 128).T
            ch[:, l * 40 + offs[name]:l * 40 + offs[name] + 4] = a4
    ch[:, 160:164] = np.asarray(inputs["L0_attn_lnl_w"], f32).reshape(4, 128).T
    ch[:, 164:168] = np.asarray(inputs["L0_attn_lnl_b"], f32).reshape(4, 128).T
    out["chvec"] = c(ch)
    out["iota"] = c(np.broadcast_to(np.arange(T, dtype=f32)[None, :], (128, T)))
    cst = np.zeros((128, 8), f32)
    cst[:, 0] = 1e-5
    cst[:, 1] = PIH
    out["consts"] = cst
    out["e16"] = (2.0 * (np.arange(128)[None, :] // 16 ==
                         np.arange(8)[:, None])).astype(BF)
    out["identity"] = np.eye(128).astype(BF)
    return out


def kernel(**inputs):
    shared = _host_layouts(inputs)
    x = np.asarray(inputs["L0_attn_x"], np.float32)
    in_maps = []
    for core in range(8):
        b, q = core // 4, core % 4
        xs = np.ascontiguousarray(x[b, q * T:(q + 1) * T, :])
        xfs = np.ascontiguousarray(xs.T)
        mt = np.zeros((128, 16), np.float32)
        mt[:, 0] = 1024.0 * q
        for j in range(8):
            jb, jq = j // 4, j % 4
            if jb == b and jq < q:
                mt[:, 1 + j] = 1024.0 * (q - 1 - jq)
            else:
                mt[:, 1 + j] = 1e30
        mt[:, 9] = 1024.0 * q
        m = dict(shared)
        m["x_tm"] = xs
        m["x_fm"] = xfs
        m["meta"] = mt
        in_maps.append(m)

    if "nc" not in _NC_CACHE:
        _NC_CACHE["nc"] = build_program()
    nc = _NC_CACHE["nc"]
    res = run_bass_kernel_spmd(nc, in_maps, core_ids=list(range(8)))
    _NC_CACHE["res"] = res
    out = np.empty((2, 4096, D), np.float32)
    for core in range(8):
        b, q = core // 4, core % 4
        out[b, q * T:(q + 1) * T, :] = np.asarray(res.results[core]["outT"]).T
    return out


if __name__ == "__main__":
    nc = build_program()
    print("build ok:", nc.n_instructions(), "instructions")


# version 11
# speedup vs baseline: 1.1483x; 1.0696x over previous
import numpy as np
import ml_dtypes
import concourse.bass as bass
import concourse.mybir as mybir
from concourse.bass_utils import run_bass_kernel_spmd
from contextlib import ExitStack

F32 = mybir.dt.float32
BF16 = mybir.dt.bfloat16
I32 = mybir.dt.int32
AF = mybir.ActivationFunctionType
ALU = mybir.AluOpType
AX = mybir.AxisListType.X
BF = ml_dtypes.bfloat16

D, DIM, DFF, T = 4096, 512, 2048, 1024
NT, NV, NO, NF, ND, NL = 8, 32, 32, 16, 4, 4
DEBUG_DUMPS = False
MAGIC = 12582912.0
RHO_MAX = 0.99999994
PIH = float(np.pi / 2)
TWO_PI = float(2 * np.pi)


SCHED = []


def build_program():
    nc = bass.Bass("TRN2", target_bir_lowering=False)
    es = ExitStack()
    SCHED.clear()

    dt = nc.dram_tensor
    x_tm_d = dt("x_tm", [T, D], F32, kind="ExternalInput").ap()
    x_fm_d = dt("x_fm", [D, T], F32, kind="ExternalInput").ap()
    winT_d = dt("winT", [2, 128, 16 * DIM], F32, kind="ExternalInput").ap()
    baseT_d = dt("baseT", [NO, 128, NV * 128], F32, kind="ExternalInput").ap()
    atT_d = dt("atT", [2, 128, 16 * 128], F32, kind="ExternalInput").ap()
    b2T_d = dt("b2T", [128, NO * 128], F32, kind="ExternalInput").ap()
    routerT_d = dt("routerT", [128, NV * 16], F32, kind="ExternalInput").ap()
    w1T_d = dt("w1T", [NL, NF, 128, ND * 128], F32, kind="ExternalInput").ap()
    w2T_d = dt("w2T", [NL, 128, NF * DIM], F32, kind="ExternalInput").ap()
    woutT_d = dt("woutT", [NO, 128, ND * 128], F32, kind="ExternalInput").ap()
    binr_d = dt("binr", [1, DIM], F32, kind="ExternalInput").ap()
    b1c_d = dt("b1c", [128, NL * NF], F32, kind="ExternalInput").ap()
    b2r_d = dt("b2r", [1, NL * DIM], F32, kind="ExternalInput").ap()
    basebc_d = dt("basebc", [128, NO], F32, kind="ExternalInput").ap()
    boutc_d = dt("boutc", [128, NO], F32, kind="ExternalInput").ap()
    routbr_d = dt("routbr", [1, 16], F32, kind="ExternalInput").ap()
    emr_d = dt("emr", [1, 16], F32, kind="ExternalInput").ap()
    chvec_d = dt("chvec", [128, 176], F32, kind="ExternalInput").ap()
    iota_d = dt("iota", [128, T], F32, kind="ExternalInput").ap()
    consts_d = dt("consts", [128, 8], F32, kind="ExternalInput").ap()
    meta_d = dt("meta", [128, 16], F32, kind="ExternalInput").ap()
    e16_d = dt("e16", [8, 128], BF16, kind="ExternalInput").ap()
    ident_d = dt("identity", [128, 128], BF16, kind="ExternalInput").ap()
    outT_d = dt("outT", [D, T], F32, kind="ExternalOutput").ap()
    contrib_d = dt("contrib", [128, 8], F32, kind="Internal")
    gathered_d = dt("gathered", [8 * 128, 8], F32, kind="Internal")
    dbg_d = {}
    if DEBUG_DUMPS:
        for nm, shp, dty in [("dxn", [128, NT * D], BF16), ("dh0", [128, NT * DIM], F32),
                             ("dlog", [128, NT * 16], F32), ("dw", [128, NT * 8], BF16),
                             ("dhw", [128, T], BF16), ("dg", [128, NO * T], BF16),
                             ("dpp", [128, 96], F32), ("dgath", [128, 64], F32),
                             ("dh1", [128, NT * DIM], F32), ("dh2", [128, NT * DIM], F32),
                             ("dh3", [128, NT * DIM], F32), ("dh4", [128, NT * DIM], F32),
                             ("du0", [128, 4 * T], F32), ("dyc", [128, T], F32),
                             ("dct", [128, T], F32), ("dst", [128, T], F32),
                             ("dmb", [128, 128], BF16), ("desc", [128, 16], F32),
                             ("dcnt", [128, 16], F32),
                             ("doutk", [128, T], BF16), ("dz1s", [128, 8 * T], BF16)]:
            dbg_d[nm] = dt(nm, shp, dty, kind="ExternalOutput").ap()

    ops = {k: [] for k in ("sync", "scalar", "vector", "tensor", "gpsimd")}
    sems = {}

    def mksem(name):
        sems[name] = [es.enter_context(nc.semaphore(name)), 0]

    for s in ("sX", "sW", "sS", "sO", "sO2", "sO3", "sV", "sA", "sT", "sG", "sC",
              "sWf", "sW2f"):
        mksem(s)
    for i in range(8):
        mksem(f"bd{i}")
        mksem(f"bf{i}")

    def incd(name, n=1):
        sems[name][1] += n
        return (sems[name][0], n)

    def val(name):
        return sems[name][1]

    def E(eng, fn):
        ops[eng].append(fn)

    def W(eng, name, v=None):
        h = sems[name][0]
        vv = sems[name][1] if v is None else v
        if vv > 0:
            SCHED.append((eng, 'w', name, vv))
            E(eng, lambda e, h=h, vv=vv: e.wait_ge(h, vv))

    def dma(eng, out, in_, sem="sW"):
        tgt = incd(sem, 16)
        SCHED.append((eng, 'i', sem, 16))
        E(eng, lambda e, out=out, in_=in_, tgt=tgt:
          e.dma_start(out=out, in_=in_).then_inc(*tgt))

    def mk(engine_letter):
        def emitter(fn, sem=None, sem2=None):
            tgt = incd(sem) if sem else None
            if sem:
                SCHED.append((engine_letter, 'i', sem, 1))
            def run(e, fn=fn, tgt=tgt):
                ins = fn(e)
                if tgt:
                    ins = ins.then_inc(*tgt)
                return ins
            E(engine_letter, run)
            if sem2:
                tgt2 = incd(sem2)
                SCHED.append((engine_letter, 'i', sem2, 1))
                E(engine_letter, lambda e, tgt2=tgt2: e.nop().then_inc(*tgt2))
        return emitter

    _mkV = mk("vector")
    AUTOJ = [False]

    def junk():
        E("vector", lambda v: v.tensor_scalar(out=vjunk[:], in0=vjunk[:],
                                              scalar1=1.0, scalar2=None,
                                              op0=ALU.mult))

    def V(fn, sem=None, sem2=None):
        _mkV(fn, sem, sem2)
        if AUTOJ[0]:
            junk()

    A = mk("scalar")
    TE = mk("tensor")
    GP = mk("gpsimd")

    def dump(nm, src_ap, waits=()):
        if not DEBUG_DUMPS:
            return
        for s, v in waits:
            W("scalar", s, v)
        dma("scalar", dbg_d[nm], src_ap, "sS")
        W("vector", "sS", val("sS"))
        W("tensor", "sS", val("sS"))

    sb = nc.alloc_sbuf_tensor
    consts = sb("consts_t", [128, 8], F32)
    chvec = sb("chvec_t", [128, 176], F32)
    meta = sb("meta_t", [128, 16], F32)
    e16 = sb("e16_t", [8, 128], BF16)
    onesr = sb("onesr_t", [1, 512], BF16)
    ones128f = sb("ones128f_t", [1, 128], F32)
    ident = sb("ident_t", [128, 128], BF16)
    binr = sb("binr_t", [1, DIM], BF16)
    b1c = sb("b1c_t", [128, NL * NF], F32)
    b2r = sb("b2r_t", [1, DIM], BF16)
    basebc = sb("basebc_t", [128, NO], F32)
    boutc = sb("boutc_t", [128, NO], F32)
    routbr = sb("routbr_t", [1, 16], BF16)
    emr = sb("emr_t", [1, 16], F32)
    mb = sb("mb_t", [128, 8 * 16], BF16)
    pp = sb("pp_t", [128, 96], F32)
    h = sb("h_t", [128, NT * DIM], F32)
    g = sb("g_t", [128, NO * T], BF16)
    finals = sb("finals_t", [128, 8], F32)
    gath = sb("gath_t", [128, 64], F32)
    gcoef = sb("gcoef_t", [128, 8], F32)
    acar = sb("acar_t", [128, 2], F32)
    statc = sb("statc_t", [128, 16], F32)
    zero128 = sb("zero128_t", [128, 1], F32)
    vjunk = sb("vjunk_t", [128, 16], F32)
    msk8 = sb("msk8_t", [128, 8], F32)
    mskc = sb("mskc_t", [128, 8], F32)
    lnst = sb("lnst_t", [128, 32], F32)

    pbank = [nc.alloc_psum_tensor(f"pb{i}", [128, 512], F32) for i in range(8)]
    pb = [b.ap() for b in pbank]
    pbbf = [b.ap().bitcast(BF16) for b in pbank]

    hv = h.ap().rearrange("p (i r) -> p i r", i=NT)

    dma("sync", consts[:], consts_d, "sX")
    dma("sync", chvec[:], chvec_d, "sX")
    dma("sync", meta[:], meta_d, "sX")
    dma("sync", b1c[:], b1c_d, "sX")
    dma("sync", basebc[:], basebc_d, "sX")
    dma("sync", boutc[:], boutc_d, "sX")
    dma("sync", emr[:], emr_d, "sX")
    dma("sync", ident[:], ident_d, "sX")
    dma("sync", e16[:], e16_d, "sX")
    dma("gpsimd", binr[:], binr_d, "sW")
    dma("gpsimd", routbr[:], routbr_d, "sW")
    sx_small = val("sX")

    ctxXN = ExitStack()
    xn = ctxXN.enter_context(nc.sbuf_tensor("xn", [128, NT * D], BF16))
    xnv = xn.ap().rearrange("p (i r) -> p i r", i=NT)

    ctxA = ExitStack()
    sbA = lambda *a: ctxA.enter_context(nc.sbuf_tensor(*a))
    prW = sbA("prW", [128, 16], F32)
    prX = sbA("prX", [128, 16], F32)
    prY = sbA("prY", [128, 16], F32)
    prZ = sbA("prZ", [128, 16], F32)
    prS = sbA("prS", [128, 16], F32)
    prM = sbA("prM", [128, 16], I32)
    ones16 = sbA("ones16", [128, 16], F32)
    zeros16 = sbA("zeros16", [128, 16], F32)
    rcol = sbA("rcol", [128, 16], F32)
    prT = sbA("prT", [128, 16], F32)
    prU = sbA("prU", [128, 16], F32)

    W("vector", "sX", sx_small)
    V(lambda v: v.memset(vjunk[:], 1.0))
    AUTOJ[0] = True
    V(lambda v: v.memset(onesr[:], 1.0))
    V(lambda v: v.memset(ones128f[:], 1.0))
    V(lambda v: v.memset(zero128[:], 0.0))
    V(lambda v: v.memset(ones16[:], 1.0))
    V(lambda v: v.memset(zeros16[:], 0.0))

    def chv(q, l):
        off = {"ln1w": 0, "ln1b": 4, "ln2w": 8, "ln2b": 12, "scl": 16,
               "pre": 20, "pim": 24, "lcr": 28, "lci": 32}[q] + l * 40
        return chvec.ap()[:, off:off + 4]

    for l in range(NL):
        V(lambda v, l=l: v.tensor_copy(out=prW[:, l * 4:l * 4 + 4], in_=chv("pre", l)))
        V(lambda v, l=l: v.tensor_copy(out=prX[:, l * 4:l * 4 + 4], in_=chv("pim", l)))
    V(lambda v: v.tensor_mul(out=prY[:], in0=prW[:], in1=prW[:]))
    V(lambda v: v.tensor_mul(out=prZ[:], in0=prX[:], in1=prX[:]))
    V(lambda v: v.tensor_add(out=prY[:], in0=prY[:], in1=prZ[:]))
    V(lambda v: v.tensor_scalar(out=prM[:], in0=prY[:], scalar1=1e-30,
                                scalar2=None, op0=ALU.is_gt), "sV")
    W("scalar", "sV")
    A(lambda a: a.activation(out=rcol[:], in_=prY[:], func=AF.Sqrt), "sA")
    W("vector", "sA")
    V(lambda v: v.tensor_scalar(out=prZ[:], in0=rcol[:], scalar1=1e-30,
                                scalar2=None, op0=ALU.max))
    V(lambda v: v.reciprocal(out=prZ[:], in_=prZ[:]))
    V(lambda v: v.tensor_mul(out=prZ[:], in0=prY[:], in1=prZ[:]))
    V(lambda v: v.tensor_add(out=rcol[:], in0=rcol[:], in1=prZ[:]))
    V(lambda v: v.tensor_scalar(out=rcol[:], in0=rcol[:], scalar1=0.5,
                                scalar2=None, op0=ALU.mult), "sV")
    W("scalar", "sV")
    A(lambda a: a.activation(out=prS[:], in_=rcol[:], func=AF.Tanh), "sA")
    W("vector", "sA")
    V(lambda v: v.tensor_scalar(out=pp[:, 0:16], in0=prS[:], scalar1=RHO_MAX,
                                scalar2=None, op0=ALU.min))
    V(lambda v: v.tensor_scalar(out=prZ[:], in0=pp.ap()[:, 0:16], scalar1=1e-38,
                                scalar2=None, op0=ALU.max), "sV")
    W("scalar", "sV")
    A(lambda a: a.activation(out=pp[:, 16:32], in_=prZ[:], func=AF.Ln), "sA")
    W("vector", "sA")
    V(lambda v: v.tensor_scalar(out=prZ[:], in0=rcol[:], scalar1=1e-30,
                                scalar2=None, op0=ALU.max))
    V(lambda v: v.reciprocal(out=prZ[:], in_=prZ[:]))
    cos1, sin1 = prY, prS
    V(lambda v: v.tensor_mul(out=cos1[:], in0=prW[:], in1=prZ[:]))
    V(lambda v: v.tensor_mul(out=sin1[:], in0=prX[:], in1=prZ[:]))
    V(lambda v: v.select(out=cos1[:], mask=prM[:], on_true=cos1[:], on_false=ones16[:]))
    V(lambda v: v.select(out=sin1[:], mask=prM[:], on_true=sin1[:], on_false=zeros16[:]))
    V(lambda v: v.tensor_scalar(out=prZ[:], in0=prW[:], scalar1=-1.0,
                                scalar2=None, op0=ALU.mult))
    V(lambda v: v.tensor_max(out=prZ[:], in0=prZ[:], in1=prW[:]))
    V(lambda v: v.tensor_add(out=prZ[:], in0=rcol[:], in1=prZ[:]))
    V(lambda v: v.tensor_scalar(out=prZ[:], in0=prZ[:], scalar1=1e-30,
                                scalar2=None, op0=ALU.max))
    V(lambda v: v.reciprocal(out=prZ[:], in_=prZ[:]))
    V(lambda v: v.tensor_mul(out=prZ[:], in0=prX[:], in1=prZ[:]), "sV")
    W("scalar", "sV")
    A(lambda a: a.activation(out=prZ[:], in_=prZ[:], func=AF.Arctan), "sA")
    W("vector", "sA")
    V(lambda v: v.tensor_scalar(out=prT[:], in0=prZ[:],
                                scalar1=float(1.0 / np.pi), scalar2=None,
                                op0=ALU.mult))
    V(lambda v: v.tensor_scalar(out=prU[:], in0=prX[:], scalar1=0.0,
                                scalar2=None, op0=ALU.is_ge))
    V(lambda v: v.tensor_scalar(out=prU[:], in0=prU[:], scalar1=1.0,
                                scalar2=-0.5, op0=ALU.mult, op1=ALU.add))
    V(lambda v: v.tensor_sub(out=prU[:], in0=prU[:], in1=prT[:]))
    V(lambda v: v.tensor_scalar(out=prM[:], in0=prW[:], scalar1=0.0,
                                scalar2=None, op0=ALU.is_lt))
    V(lambda v: v.select(out=pp[:, 32:48], mask=prM[:], on_true=prU[:],
                         on_false=prT[:]))
    for l in range(NL):
        V(lambda v, l=l: v.tensor_copy(out=prW[:, l * 4:l * 4 + 4], in_=chv("lcr", l)))
        V(lambda v, l=l: v.tensor_copy(out=prX[:, l * 4:l * 4 + 4], in_=chv("lci", l)))
    V(lambda v: v.tensor_mul(out=pp[:, 48:64], in0=prW[:], in1=cos1[:]))
    V(lambda v: v.tensor_mul(out=prZ[:], in0=prX[:], in1=sin1[:]))
    V(lambda v: v.tensor_sub(out=pp[:, 48:64], in0=pp.ap()[:, 48:64], in1=prZ[:]))
    V(lambda v: v.tensor_mul(out=pp[:, 64:80], in0=prW[:], in1=sin1[:]))
    V(lambda v: v.tensor_mul(out=prZ[:], in0=prX[:], in1=cos1[:]))
    V(lambda v: v.tensor_add(out=pp[:, 64:80], in0=pp.ap()[:, 64:80], in1=prZ[:]), "sV")

    W("tensor", "sX", sx_small)
    W("tensor", "sV")
    TE(lambda t: t.matmul(pb[7][0:128, 0:16], ones128f.ap()[:], emr.ap()[:],
                          start=True, stop=True), "bd7")
    W("vector", "bd7")
    for e in range(8):
        V(lambda v, e=e: v.tensor_scalar(out=mb[:, e * 16:e * 16 + 16],
                                         in0=pb[7][0:128, 0:16],
                                         scalar1=float(e), scalar2=None,
                                         op0=ALU.is_equal),
          "bf7" if e == 7 else None)
    AUTOJ[0] = False

    xt = [sbA(f"xt{i}", [128, D], F32) for i in range(2)]
    sqs = sbA("sqs", [128, D], F32)
    xnt = sbA("xnt", [128, D], BF16)

    xload = [None] * NT
    xnsv = [None] * NT
    xq = ["sync", "scalar", "gpsimd", "sync", "scalar", "gpsimd", "sync", "scalar"]
    xqs = {"sync": "sX", "scalar": "sS", "gpsimd": "sW"}

    def emit_xload(j):
        q = xq[j]
        if j >= 2:
            W(q, "sV", xnsv[j - 2])
        dma(q, xt[j % 2][:], x_tm_d[j * 128:(j + 1) * 128, :], xqs[q])
        xload[j] = (xqs[q], val(xqs[q]))

    emit_xload(0)
    emit_xload(1)
    for i in range(NT):
        W("vector", *xload[i])
        V(lambda v, i=i: v.reduce_sum(out=statc[:, 0:1], in_=xt[i % 2][:], axis=AX), "sV")
        W("scalar", "sV")
        A(lambda a, i=i: a.activation(out=sqs[:], in_=xt[i % 2][:], func=AF.Square,
                                      accum_out=statc[:, 1:2]), "sA")
        W("vector", "sA")
        V(lambda v: v.tensor_scalar(out=statc[:, 2:3], in0=statc[:, 0:1],
                                    scalar1=float(1.0 / D), scalar2=None, op0=ALU.mult))
        V(lambda v: v.tensor_scalar(out=statc[:, 3:4], in0=statc[:, 1:2],
                                    scalar1=float(1.0 / D), scalar2=None, op0=ALU.mult))
        junk()
        V(lambda v: v.scalar_tensor_tensor(out=statc[:, 4:5], in0=statc[:, 2:3],
                                           scalar=statc[:, 2:3], in1=statc[:, 3:4],
                                           op0=ALU.mult, op1=ALU.subtract), "sV")
        W("scalar", "sV")
        A(lambda a: a.activation(out=statc[:, 5:6], in_=statc[:, 4:5], func=AF.Ln,
                                 scale=-1.0, bias=consts.ap()[:, 0:1]))
        A(lambda a: a.activation(out=statc[:, 6:7], in_=statc[:, 5:6], func=AF.Exp,
                                 scale=-0.5), "sA")
        W("vector", "sA")
        if i >= 1:
            W("vector", "bd4", val("bd4"))
        V(lambda v, i=i: v.tensor_scalar(out=xnt[:], in0=xt[i % 2][:],
                                         scalar1=statc[:, 2:3], scalar2=statc[:, 6:7],
                                         op0=ALU.subtract, op1=ALU.mult), "sV")
        xnsv[i] = val("sV")
        if i + 2 < NT:
            emit_xload(i + 2)
        W("tensor", "sV")
        for gi in range(4):
            bank = 3 + (gi % 2)
            W("tensor", f"bf{bank}", val(f"bf{bank}"))
            for c8 in range(8):
                c = gi * 8 + c8
                TE(lambda t, c=c, c8=c8, bank=bank:
                   t.transpose(pbbf[bank][:, c8 * 128:(c8 + 1) * 128],
                               xnt.ap()[:, c * 128:(c + 1) * 128], ident.ap()[:]),
                   f"bd{bank}" if c8 == 7 else None)
            W("scalar", f"bd{bank}", val(f"bd{bank}"))
            A(lambda a, i=i, gi=gi, bank=bank:
              a.copy(out=xn.ap()[:, i * D + gi * 1024:i * D + (gi + 1) * 1024],
                     in_=pbbf[bank][:]), f"bf{bank}")
    xn_bf3, xn_bf4 = val("bf3"), val("bf4")
    dump("dxn", xn.ap()[:], [("bf3", xn_bf3), ("bf4", xn_bf4)])
    dump("dpp", pp.ap()[:], [("sV", val("sV"))])
    xn_apply_sv = val("sV")

    def xn_lhs(i, c):
        return xn.ap()[:, i * D + c * 128:i * D + c * 128 + 128]

    def xn_rhs(c, tc):
        return xnv[:, 4 * tc:4 * tc + 4, c * 128:(c + 1) * 128]

    ctxA.close()
    ctxB = ExitStack()
    sbB = lambda *a: ctxB.enter_context(nc.sbuf_tensor(*a))
    winT_sb = sbB("winT_sb", [128, 16 * DIM], BF16)
    atT_sb = sbB("atT_sb", [128, 16 * 128], BF16)
    b2T_sb = sbB("b2T_sb", [128, NO * 128], BF16)
    routT_sb = sbB("routT_sb", [128, NV * 16], BF16)
    bstream = [sbB(f"bstream{i}", [128, NV * 128], BF16) for i in range(2)]
    logits = sbB("logits", [128, NT * 16], F32)
    wsel = sbB("wsel", [128, NT * 8], BF16)
    cnt = sbB("cnt", [128, 16], F32)
    mskB = sbB("mskB", [128, 16], F32)
    cmpa = sbB("cmpa", [128, 256], F32)
    escb = sbB("escb", [128, 16], F32)
    wfm = sbB("wfm", [8, T], BF16)
    wexp_sb = sbB("wexp_sb", [128, T], BF16)
    hwb = sbB("hwb", [128, T], BF16)

    W("gpsimd", "sV", xn_apply_sv)
    W("gpsimd", "sA", val("sA"))
    dma("gpsimd", routT_sb[:], routerT_d, "sW")
    dma("gpsimd", b2T_sb[:], b2T_d, "sW")
    dma("gpsimd", atT_sb[:], atT_d[0], "sW")
    sw_rt = val("sW")

    W("tensor", "sW", sw_rt)
    W("tensor", "bf3", xn_bf3)
    W("tensor", "bf4", xn_bf4)
    for i in range(NT):
        W("tensor", "bf7", val("bf7"))
        for c in range(NV):
            TE(lambda t, i=i, c=c: t.matmul(pb[7][:, 0:16], xn_lhs(i, c),
                                            routT_sb.ap()[:, c * 16:(c + 1) * 16],
                                            start=(c == 0), stop=False))
        TE(lambda t: t.matmul(pb[7][:, 0:16], onesr.ap()[0:1, 0:128], routbr.ap()[:],
                              start=False, stop=True), "bd7")
        W("vector", "bd7", val("bd7"))
        V(lambda v, i=i: v.tensor_copy(out=logits[:, i * 16:(i + 1) * 16],
                                       in_=pb[7][:, 0:16]), "bf7")
    dump("dmb", mb.ap()[:], [("sV", val("sV"))])
    for i in range(NT):
        lg = logits.ap()[:, i * 16:(i + 1) * 16]
        V(lambda v, lg=lg: v.reduce_max(out=statc[:, 8:9], in_=lg, axis=AX,
                                        negate=True), "sV")
        W("scalar", "sV")
        A(lambda a, lg=lg: a.activation(out=escb[:], in_=lg, func=AF.Exp,
                                        bias=statc[:, 8:9], accum_out=statc[:, 9:10]), "sA")
        W("vector", "sA")
        V(lambda v: v.reciprocal(out=statc[:, 10:11], in_=statc[:, 9:10]))
        for j in range(15):
            V(lambda v, j=j: v.tensor_scalar(out=cmpa[:, j * 16:(j + 1) * 16],
                                             in0=escb[:],
                                             scalar1=escb[:, j:j + 1],
                                             scalar2=None, op0=ALU.is_gt))
        junk()
        for j in range(15):
            V(lambda v, j=j: v.reduce_sum(out=cnt[:, j:j + 1],
                                          in_=cmpa.ap()[:, j * 16:(j + 1) * 16],
                                          axis=AX))
        V(lambda v: v.memset(cnt[:, 15:16], 100.0))
        junk()
        if i == 0:
            V(lambda v: v.engine_nop(), "sV")
            dump("desc", escb.ap()[:], [("sV", val("sV"))])
            dump("dcnt", cnt.ap()[:], [("sV", val("sV"))])
        V(lambda v: v.tensor_scalar(out=mskB[:], in0=cnt[:], scalar1=7.5,
                                    scalar2=None, op0=ALU.is_lt))
        junk()
        V(lambda v: v.tensor_mul(out=escb[:], in0=escb[:], in1=mskB[:]))
        junk()
        for e in range(8):
            V(lambda v, e=e: v.tensor_mul(out=cmpa[:, e * 16:(e + 1) * 16],
                                          in0=escb[:],
                                          in1=mb.ap()[:, e * 16:e * 16 + 16]))
        junk()
        for e in range(8):
            V(lambda v, e=e: v.reduce_sum(out=msk8[:, e:e + 1],
                                          in_=cmpa.ap()[:, e * 16:(e + 1) * 16],
                                          axis=AX))
        junk()
        V(lambda v, i=i: v.tensor_scalar(out=wsel[:, i * 8:i * 8 + 8],
                                         in0=msk8.ap()[:],
                                         scalar1=statc[:, 10:11], scalar2=None,
                                         op0=ALU.mult), "sV")
    w_done = val("sV")
    dump("dlog", logits.ap()[:], [("sV", w_done)])
    dump("dw", wsel.ap()[:], [("sV", w_done)])
    W("tensor", "sV", w_done)
    for i in range(NT):
        W("tensor", "bf2", val("bf2"))
        TE(lambda t, i=i: t.transpose(pbbf[2][0:8, 0:128],
                                      wsel.ap()[:, i * 8:(i + 1) * 8], ident.ap()[:]),
           "bd2")
        W("vector", "bd2", val("bd2"))
        V(lambda v, i=i: v.tensor_copy(out=wfm[:, i * 128:(i + 1) * 128],
                                       in_=pbbf[2][0:8, 0:128]), "bf2")
    wfm_sv = val("sV")
    wfm_bf2 = val("bf2")
    for tc in range(2):
        W("tensor", "bf6", val("bf6"))
        for half in range(2):
            if not (tc == 0 and half == 0):
                W("gpsimd", "sT", val("sT"))
                dma("gpsimd", atT_sb[:], atT_d[half], "sW")
                W("tensor", "sW", val("sW"))
            for c16 in range(16):
                cc = half * 16 + c16
                TE(lambda t, c16=c16, cc=cc, tc=tc:
                   t.matmul(pb[6][:], atT_sb.ap()[:, c16 * 128:(c16 + 1) * 128],
                            xn_rhs(cc, tc), start=(cc == 0), stop=(cc == NV - 1)),
                   "sT" if c16 == 15 else None)
        hA_done = val("sT")
        W("tensor", "sV", wfm_sv)
        W("tensor", "bf2", wfm_bf2)
        W("tensor", "bf5", val("bf5"))
        TE(lambda t, tc=tc: t.matmul(pb[5][:], e16.ap()[:],
                                     wfm.ap()[0:8, tc * 512:(tc + 1) * 512],
                                     start=True, stop=True), "bd5")
        W("scalar", "bd5", val("bd5"))
        A(lambda a, tc=tc: a.copy(out=wexp_sb[:, tc * 512:(tc + 1) * 512],
                                  in_=pb[5][:]), "bf5")
        W("vector", "sT", hA_done)
        W("vector", "bf5", val("bf5"))
        V(lambda v, tc=tc: v.tensor_mul(out=hwb[:, tc * 512:(tc + 1) * 512],
                                        in0=pb[6][:],
                                        in1=wexp_sb.ap()[:, tc * 512:(tc + 1) * 512]),
          "bf6")
    hw_sv = val("sV")
    hw_bf6 = val("bf6")
    dump("dhw", hwb.ap()[:], [("bf6", hw_bf6)])

    win_done = [("sT", 0)]
    for half in range(2):
        if half == 1:
            W("gpsimd", *win_done[0])
        dma("gpsimd", winT_sb[:], winT_d[half], "sW")
        W("tensor", "sW", val("sW"))
        for i in range(NT):
            W("tensor", "bf2", val("bf2"))
            for c16 in range(16):
                last = (half == 1 and c16 == 15)
                TE(lambda t, i=i, c16=c16, half=half, last=last:
                   t.matmul(pb[2][:], xn_lhs(i, half * 16 + c16),
                            winT_sb.ap()[:, c16 * DIM:(c16 + 1) * DIM],
                            start=(c16 == 0), stop=last),
                   "bd2" if last else None)
            if half == 0:
                TE(lambda t: t.matmul(pb[2][:], onesr.ap()[0:1, 0:128], binr.ap()[:],
                                      start=False, stop=True),
                   "bd2")
                if i == NT - 1:
                    win_done[0] = ("bd2", val("bd2"))
                W("vector", "bd2", val("bd2"))
                V(lambda v, i=i: v.tensor_copy(out=hv[:, i, :], in_=pb[2][:]), "bf2")
            else:
                W("vector", "bd2", val("bd2"))
                V(lambda v, i=i: v.tensor_add(out=hv[:, i, :], in0=hv[:, i, :],
                                              in1=pb[2][:]), "bf2")
    h_sv = val("sV")
    dump("dh0", h.ap()[:], [("bf2", val("bf2"))])

    W("tensor", "bf6", hw_bf6)
    sw_base0 = val("sW")
    base_done = []
    for oi in range(NO):
        buf = oi % 2
        if oi >= 2:
            W("gpsimd", *base_done[oi - 2])
        dma("gpsimd", bstream[buf][:], baseT_d[oi], "sW")
        W("tensor", "sW", sw_base0 + 16 * (oi + 1))
        for tc in range(2):
            bank = tc
            W("tensor", f"bf{bank}", val(f"bf{bank}"))
            for c in range(NV):
                TE(lambda t, buf=buf, c=c, tc=tc, bank=bank:
                   t.matmul(pb[bank][:], bstream[buf].ap()[:, c * 128:(c + 1) * 128],
                            xn_rhs(c, tc), start=(c == 0), stop=False))
            TE(lambda t, oi=oi, tc=tc, bank=bank:
               t.matmul(pb[bank][:], b2T_sb.ap()[:, oi * 128:(oi + 1) * 128],
                        hwb.ap()[:, tc * 512:(tc + 1) * 512], start=False, stop=True),
               f"bd{bank}")
            if tc == 1:
                base_done.append((f"bd{bank}", val(f"bd{bank}")))
            W("scalar", f"bd{bank}", val(f"bd{bank}"))
            A(lambda a, oi=oi, tc=tc, bank=bank:
              a.activation(out=g[:, oi * T + tc * 512:oi * T + (tc + 1) * 512],
                           in_=pb[bank][:], func=AF.Silu,
                           bias=basebc.ap()[:, oi:oi + 1]), f"bf{bank}")
    g_bf0, g_bf1 = val("bf0"), val("bf1")
    dump("dg", g.ap()[:], [("bf0", g_bf0), ("bf1", g_bf1)])

    ctxB.close()
    ctxXN.close()
    ctxC = ExitStack()
    sbC = lambda *a: ctxC.enter_context(nc.sbuf_tensor(*a))
    iota = sbC("iota_sb", [128, T], F32)
    zt = sbC("zt", [128, DIM], BF16)
    zfm = sbC("zfm", [128, NT * DIM], BF16)
    z2fm = sbC("z2fm", [128, NT * DIM], BF16)
    yc = sbC("yc", [128, T], F32)
    scr = [sbC(f"scr{i}", [128, T], F32) for i in range(3)]
    u0 = sbC("u0", [128, 4 * T], F32)
    v0 = sbC("v0", [128, 4 * T], F32)
    rp = sbC("rp", [128, T], BF16)
    z1s = sbC("z1s", [128, 8 * T], BF16)
    w1sb = [sbC(f"w1sb{i}", [128, ND * 128], BF16) for i in range(4)]
    w2sb = sbC("w2sb", [128, NF * DIM], BF16)
    outk = sbC("outk", [128, T], BF16)

    ctk = z1s.ap().bitcast(F32)
    stk = w2sb.ap().bitcast(F32)

    def ct_c(c):
        return ctk[:, c * T:(c + 1) * T]

    def st_c(c):
        return stk[:, c * T:(c + 1) * T]

    sqs2ap = scr[1].ap()[:, 0:DIM]
    zfmv = zfm.ap().rearrange("p (i r) -> p i r", i=NT)
    z2fmv = z2fm.ap().rearrange("p (i r) -> p i r", i=NT)
    gathv = gath.ap().rearrange("p (cc j) -> p cc j", cc=8)

    W("gpsimd", "bf0", g_bf0)
    W("gpsimd", "bf1", g_bf1)
    W("gpsimd", "sV", val("sV"))
    dma("gpsimd", iota[:], iota_d, "sW")

    tlb, wre, wim = scr[0], scr[1], scr[2]
    w1_done = []
    w2_done = [None]

    def emit_ln_to_fm(dstv):
        inv_n = float(1.0 / DIM)
        for i in range(NT):
            V(lambda v, i=i: v.reduce_sum(out=lnst[:, i:i + 1], in_=hv[:, i, :],
                                          axis=AX), "sV" if i == NT - 1 else None)
        W("scalar", "sV")
        for i in range(NT):
            A(lambda a, i=i: a.activation(out=sqs2ap, in_=hv[:, i, :], func=AF.Square,
                                          accum_out=lnst[:, 8 + i:9 + i]),
              "sA" if i == NT - 1 else None)
        W("vector", "sA")
        V(lambda v: v.tensor_scalar(out=lnst[:, 16:24], in0=lnst.ap()[:, 0:8],
                                    scalar1=inv_n, scalar2=None, op0=ALU.mult))
        V(lambda v: v.tensor_scalar(out=lnst[:, 8:16], in0=lnst.ap()[:, 8:16],
                                    scalar1=inv_n, scalar2=None, op0=ALU.mult))
        junk()
        V(lambda v: v.tensor_mul(out=lnst[:, 0:8], in0=lnst.ap()[:, 16:24],
                                 in1=lnst.ap()[:, 16:24]))
        junk()
        V(lambda v: v.tensor_sub(out=lnst[:, 0:8], in0=lnst.ap()[:, 0:8],
                                 in1=lnst.ap()[:, 8:16]), "sV")
        W("scalar", "sV")
        A(lambda a: a.activation(out=lnst[:, 8:16], in_=lnst.ap()[:, 0:8], func=AF.Ln,
                                 scale=-1.0, bias=consts.ap()[:, 0:1]))
        A(lambda a: a.activation(out=lnst[:, 24:32], in_=lnst.ap()[:, 8:16],
                                 func=AF.Exp, scale=-0.5), "sA")
        W("vector", "sA")
        for i in range(NT):
            W("vector", f"bd{3 + ((i + 1) % 2)}", val(f"bd{3 + ((i + 1) % 2)}"))
            V(lambda v, i=i: v.tensor_scalar(out=zt[:], in0=hv[:, i, :],
                                             scalar1=lnst[:, 16 + i:17 + i],
                                             scalar2=lnst[:, 24 + i:25 + i],
                                             op0=ALU.subtract, op1=ALU.mult), "sV")
            bank = 3 + (i % 2)
            W("tensor", "sV")
            W("tensor", f"bf{bank}", val(f"bf{bank}"))
            for c in range(ND):
                TE(lambda t, c=c, bank=bank:
                   t.transpose(pbbf[bank][:, c * 128:(c + 1) * 128],
                               zt.ap()[:, c * 128:(c + 1) * 128], ident.ap()[:]),
                   f"bd{bank}" if c == ND - 1 else None)
            W("scalar", f"bd{bank}", val(f"bd{bank}"))
            A(lambda a, i=i, bank=bank, dstv=dstv:
              a.copy(out=dstv[:, i, :], in_=pbbf[bank][:, 0:512]), f"bf{bank}")

    def emit_table_fracs(pc, sem=None):
        V(lambda v, pc=pc: v.tensor_scalar(out=statc[:, 11:12],
                                           in0=pp.ap()[:, 32 + pc:33 + pc],
                                           scalar1=meta.ap()[:, 0:1], scalar2=None,
                                           op0=ALU.mult))
        junk()
        V(lambda v, pc=pc: v.tensor_scalar(out=tlb[:], in0=iota[:],
                                           scalar1=pp.ap()[:, 32 + pc:33 + pc],
                                           scalar2=statc[:, 11:12],
                                           op0=ALU.mult, op1=ALU.add))
        V(lambda v: v.tensor_scalar(out=wre[:], in0=tlb[:], scalar1=MAGIC,
                                    scalar2=-MAGIC, op0=ALU.add, op1=ALU.add))
        V(lambda v: v.tensor_sub(out=wre[:], in0=tlb[:], in1=wre[:]))
        V(lambda v: v.tensor_scalar(out=wim[:], in0=tlb[:], scalar1=0.25,
                                    scalar2=MAGIC, op0=ALU.add, op1=ALU.add))
        V(lambda v: v.tensor_scalar(out=wim[:], in0=wim[:], scalar1=-MAGIC,
                                    scalar2=None, op0=ALU.add))
        V(lambda v: v.tensor_sub(out=wim[:], in0=tlb[:], in1=wim[:]), sem)

    def emit_table_sins(c, sem=None):
        A(lambda a, c=c: a.activation(out=st_c(c), in_=wre[:], func=AF.Sin,
                                      scale=TWO_PI))
        A(lambda a, c=c: a.activation(out=ct_c(c), in_=wim[:], func=AF.Sin,
                                      scale=TWO_PI, bias=consts.ap()[:, 1:2]), sem)

    sw_w2_cur = [0]
    W("scalar", "bd0", val("bd0"))
    W("scalar", "bd1", val("bd1"))
    for l in range(NL):
        lb = l * 40
        emit_ln_to_fm(zfmv)

        W("vector", "bf3", val("bf3"))
        W("vector", "bf4", val("bf4"))
        for c in range(4):
            pc = l * 4 + c
            V(lambda v, c=c, lb=lb: v.tensor_scalar(
                out=yc[:], in0=zfmv[:, 0:NT, c * 128:(c + 1) * 128],
                scalar1=chvec.ap()[:, lb + 0 + c:lb + 1 + c],
                scalar2=chvec.ap()[:, lb + 4 + c:lb + 5 + c],
                op0=ALU.mult, op1=ALU.add))
            emit_table_fracs(pc, "sV")
            W("scalar", "sV")
            A(lambda a: a.activation(out=yc[:], in_=yc[:], func=AF.Silu))
            emit_table_sins(c, "sA")
            if l == 0 and c == 0:
                dump("dyc", yc.ap()[:], [("sA", val("sA"))])
                dump("dct", ct_c(0), [("sA", val("sA"))])
                dump("dst", st_c(0), [("sA", val("sA"))])
            W("vector", "sA")
            V(lambda v, c=c: v.tensor_mul(out=wre[:], in0=ct_c(c), in1=yc[:]))
            V(lambda v, c=c: v.tensor_mul(out=wim[:], in0=st_c(c), in1=yc[:]))
            V(lambda v: v.tensor_scalar(out=wim[:], in0=wim[:], scalar1=-1.0,
                                        scalar2=None, op0=ALU.mult))
            rhoap = pp.ap()[:, 0 + pc:1 + pc]
            for seg in range(2):
                s0, s1 = seg * 512, (seg + 1) * 512
                V(lambda v, c=c, s0=s0, s1=s1, seg=seg, rhoap=rhoap:
                  v.tensor_tensor_scan(out=u0.ap()[:, c * T + s0:c * T + s1],
                                       data0=rhoap.broadcast_to([128, 512]),
                                       data1=wre.ap()[:, s0:s1],
                                       initial=(zero128.ap()[:] if seg == 0 else
                                                u0.ap()[:, c * T + 511:c * T + 512]),
                                       op0=ALU.mult, op1=ALU.add))
                V(lambda v, c=c, s0=s0, s1=s1, seg=seg, rhoap=rhoap:
                  v.tensor_tensor_scan(out=v0.ap()[:, c * T + s0:c * T + s1],
                                       data0=rhoap.broadcast_to([128, 512]),
                                       data1=wim.ap()[:, s0:s1],
                                       initial=(zero128.ap()[:] if seg == 0 else
                                                v0.ap()[:, c * T + 511:c * T + 512]),
                                       op0=ALU.mult, op1=ALU.add))
            V(lambda v, c=c: v.tensor_copy(out=finals[:, c:c + 1],
                                           in_=u0.ap()[:, c * T + 1023:c * T + 1024]))
            V(lambda v, c=c: v.tensor_copy(out=finals[:, 4 + c:5 + c],
                                           in_=v0.ap()[:, c * T + 1023:c * T + 1024]),
              "sV")
        scans_sv = val("sV")
        if l == 0:
            dump("du0", u0.ap()[:], [("sV", scans_sv)])

        W("gpsimd", "sV", scans_sv)
        dma("gpsimd", contrib_d.ap(), finals[:], "sW")
        W("gpsimd", "sW", val("sW"))
        tgt = incd("sC")
        SCHED.append(("gpsimd", 'i', "sC", 1))
        E("gpsimd", lambda gp, tgt=tgt: gp.collective_compute(
            "AllGather", ALU.bypass, replica_groups=[[0, 1, 2, 3, 4, 5, 6, 7]],
            ins=[contrib_d.ap().opt()], outs=[gathered_d.ap().opt()]).then_inc(*tgt))
        W("gpsimd", "sC", val("sC"))
        dma("gpsimd", gath.ap().rearrange("p (c j) -> p c j", c=8),
            gathered_d.ap().rearrange("(c p) j -> p c j", p=128), "sW")
        gath_w = val("sW")
        if l == 0:
            dump("dgath", gath.ap()[:], [("sW", gath_w)])

        for c in range(4):
            pc = l * 4 + c
            lnr = pp.ap()[:, 16 + pc:17 + pc]
            V(lambda v, lnr=lnr: v.tensor_scalar(out=tlb[:], in0=iota[:], scalar1=lnr,
                                                 scalar2=lnr, op0=ALU.mult, op1=ALU.add),
              "sV")
            W("scalar", "sV")
            for j in range(8):
                A(lambda a, j=j, lnr=lnr: a.activation(out=gcoef[:, j:j + 1], in_=lnr,
                                                       func=AF.Exp,
                                                       scale=meta.ap()[:, 1 + j:2 + j]))
            A(lambda a, lnr=lnr: a.activation(out=statc[:, 12:13], in_=lnr, func=AF.Exp,
                                              scale=meta.ap()[:, 9:10]))
            A(lambda a: a.activation(out=rp[:], in_=tlb[:], func=AF.Exp), "sA")
            if c == 0:
                W("vector", "sW", gath_w)
            W("vector", "sA")
            V(lambda v, c=c: v.tensor_mul(out=msk8[:], in0=gcoef[:],
                                          in1=gathv[:, :, c]))
            V(lambda v, c=c: v.tensor_mul(out=mskc[:], in0=gcoef[:],
                                          in1=gathv[:, :, 4 + c]))
            junk()
            V(lambda v: v.reduce_sum(out=acar[:, 0:1], in_=msk8.ap()[:], axis=AX))
            V(lambda v: v.reduce_sum(out=acar[:, 1:2], in_=mskc.ap()[:], axis=AX))
            junk()
            V(lambda v, pc=pc: v.scalar_tensor_tensor(out=acar[:, 0:1],
                                                      in0=statc[:, 12:13],
                                                      scalar=pp.ap()[:, 48 + pc:49 + pc],
                                                      in1=acar.ap()[:, 0:1],
                                                      op0=ALU.mult, op1=ALU.add))
            V(lambda v, pc=pc: v.scalar_tensor_tensor(out=acar[:, 1:2],
                                                      in0=statc[:, 12:13],
                                                      scalar=pp.ap()[:, 64 + pc:65 + pc],
                                                      in1=acar.ap()[:, 1:2],
                                                      op0=ALU.mult, op1=ALU.add))
            junk()
            V(lambda v, c=c: v.scalar_tensor_tensor(out=wre[:], in0=rp[:],
                                                    scalar=acar[:, 0:1],
                                                    in1=u0.ap()[:, c * T:(c + 1) * T],
                                                    op0=ALU.mult, op1=ALU.add))
            V(lambda v, c=c: v.scalar_tensor_tensor(out=wim[:], in0=rp[:],
                                                    scalar=acar[:, 1:2],
                                                    in1=v0.ap()[:, c * T:(c + 1) * T],
                                                    op0=ALU.mult, op1=ALU.add))
            V(lambda v, c=c: v.tensor_mul(out=wre[:], in0=ct_c(c), in1=wre[:]))
            V(lambda v, c=c: v.tensor_mul(out=wim[:], in0=st_c(c), in1=wim[:]))
            V(lambda v: v.tensor_sub(out=tlb[:], in0=wre[:], in1=wim[:]))
            V(lambda v, lb=lb, c=c: v.tensor_scalar(
                out=outk[:], in0=tlb[:],
                scalar1=chvec.ap()[:, lb + 16 + c:lb + 17 + c],
                scalar2=None, op0=ALU.mult), "sV")
            if l == 0 and c == 0:
                dump("doutk", outk.ap()[:], [("sV", val("sV"))])
            W("tensor", "sV", val("sV"))
            for grp in range(2):
                bank = 3 + grp
                W("tensor", f"bf{bank}", val(f"bf{bank}"))
                for i4 in range(4):
                    i = grp * 4 + i4
                    TE(lambda t, c=c, i=i, i4=i4, bank=bank:
                       t.transpose(pbbf[bank][:, i4 * 128:(i4 + 1) * 128],
                                   outk.ap()[:, i * 128:(i + 1) * 128],
                                   ident.ap()[:]), f"bd{bank}" if i4 == 3 else None)
                W("vector", f"bd{bank}", val(f"bd{bank}"))
                V(lambda v, c=c, grp=grp, bank=bank:
                  v.tensor_add(out=hv[:, grp * 4:grp * 4 + 4, c * 128:(c + 1) * 128],
                               in0=hv[:, grp * 4:grp * 4 + 4, c * 128:(c + 1) * 128],
                               in1=pbbf[bank].rearrange("p (i j) -> p i j", i=8)[:, 0:4, :]),
                  f"bf{bank}")
        hupd_sv = val("sV")
        dump(f"dh{l + 1}", h.ap()[:], [("sV", hupd_sv), ("bf3", val("bf3")),
                                       ("bf4", val("bf4"))])

        if w2_done[0] is not None:
            W("gpsimd", *w2_done[0])
        dma("gpsimd", b2r[:], b2r_d[:, l * DIM:(l + 1) * DIM], "sW")
        sw_b2r = val("sW")
        W("vector", "sV", hupd_sv)
        emit_ln_to_fm(z2fmv)
        z2_bf3, z2_bf4 = val("bf3"), val("bf4")
        W("vector", "bf3", z2_bf3)
        W("vector", "bf4", z2_bf4)
        for dd in range(ND):
            V(lambda v, dd=dd, lb=lb: v.tensor_scalar(
                out=z2fmv[:, 0:NT, dd * 128:(dd + 1) * 128],
                in0=z2fmv[:, 0:NT, dd * 128:(dd + 1) * 128],
                scalar1=chvec.ap()[:, lb + 8 + dd:lb + 9 + dd],
                scalar2=chvec.ap()[:, lb + 12 + dd:lb + 13 + dd],
                op0=ALU.mult, op1=ALU.add),
              "sV" if dd == ND - 1 else None)
        z2g_sv = val("sV")
        W("tensor", "sV", z2g_sv)

        for half in range(2):
            for f8 in range(8):
                fi = half * 8 + f8
                slot = fi % 4
                kglob = l * NF + fi
                if kglob >= 4:
                    W("gpsimd", *w1_done[kglob - 4])
                dma("gpsimd", w1sb[slot][:], w1T_d[l, fi], "sW")
                sw_f = val("sW")
                if fi == 2:
                    W("gpsimd", "sV", hupd_sv)
                    if w2_done[0] is not None:
                        W("gpsimd", *w2_done[0])
                    dma("gpsimd", w2sb[:], w2T_d[l], "sW")
                    sw_w2_cur[0] = val("sW")
                W("tensor", "sW", sw_f)
                for tc in range(2):
                    bank = 5 + tc
                    W("tensor", f"bf{bank}", val(f"bf{bank}"))
                    for d in range(ND):
                        TE(lambda t, slot=slot, d=d, tc=tc, bank=bank:
                           t.matmul(pb[bank][:],
                                    w1sb[slot].ap()[:, d * 128:(d + 1) * 128],
                                    z2fmv[:, 4 * tc:4 * tc + 4, d * 128:(d + 1) * 128],
                                    start=(d == 0), stop=(d == ND - 1)),
                           f"bd{bank}" if d == ND - 1 else None)
                    W("scalar", f"bd{bank}", val(f"bd{bank}"))
                    A(lambda a, f8=f8, tc=tc, bank=bank, fi=fi, l=l:
                      a.activation(out=z1s[:, f8 * T + tc * 512:f8 * T + (tc + 1) * 512],
                                   in_=pb[bank][:], func=AF.Silu,
                                   bias=b1c.ap()[:, l * NF + fi:l * NF + fi + 1]),
                      f"bf{bank}")
                w1_done.append(("bd6", val("bd6")))
            W("tensor", "sW", sw_w2_cur[0])
            W("tensor", "sW", sw_b2r)
            W("tensor", "bf5", val("bf5"))
            W("tensor", "bf6", val("bf6"))
            for i in range(NT):
                wb = 2 if i % 2 == 0 else 7
                W("tensor", f"bf{wb}", val(f"bf{wb}"))
                for f8 in range(8):
                    last = (half == 0 and f8 == 7)
                    TE(lambda t, i=i, f8=f8, half=half, last=last, wb=wb:
                       t.matmul(pb[wb][:],
                                z1s.ap()[:, f8 * T + i * 128:f8 * T + (i + 1) * 128],
                                w2sb.ap()[:, (half * 8 + f8) * DIM:(half * 8 + f8 + 1) * DIM],
                                start=(f8 == 0), stop=last),
                       f"bd{wb}" if last else None)
                if half == 1:
                    TE(lambda t, wb=wb: t.matmul(pb[wb][:], onesr.ap()[0:1, 0:128],
                                                 b2r.ap()[:], start=False, stop=True),
                       f"bd{wb}")
                    if i == NT - 1:
                        w2_done[0] = (f"bd{wb}", val(f"bd{wb}"))
                W("vector", f"bd{wb}", val(f"bd{wb}"))
                V(lambda v, i=i, wb=wb: v.tensor_add(out=hv[:, i, :], in0=hv[:, i, :],
                                                     in1=pb[wb][:]), f"bf{wb}")

    z3fmv = z2fmv
    hupd_L3 = hupd_sv

    def xslot(s, tc=None):
        s8 = s % 8
        buf = u0 if s8 < 4 else v0
        base = (s8 % 4) * T
        if tc is None:
            return buf.ap()[:, base:base + T]
        return buf.ap()[:, base + tc * 512:base + (tc + 1) * 512]

    xfl = [None] * NO
    comb_sv = [None] * NO
    st_done = [[] for _ in range(NO)]
    stq_sem = {"scalar": "sO", "sync": "sO2", "gpsimd": "sO3"}

    def emit_xfm_load(vi):
        q = "sync" if vi % 2 == 0 else "scalar"
        if vi >= 8:
            W(q, "sV", comb_sv[vi - 8])
        sem = xqs[q]
        dma(q, xslot(vi), x_fm_d[vi * 128:(vi + 1) * 128, :], sem)
        xfl[vi] = (sem, val(sem))

    W("sync", "sV", hupd_L3)
    W("scalar", "sV", hupd_L3)
    for vi in range(8):
        emit_xfm_load(vi)

    emit_ln_to_fm(z3fmv)
    W("vector", "bf3", val("bf3"))
    W("vector", "bf4", val("bf4"))
    for dd in range(ND):
        V(lambda v, dd=dd: v.tensor_scalar(
            out=z3fmv[:, 0:NT, dd * 128:(dd + 1) * 128],
            in0=z3fmv[:, 0:NT, dd * 128:(dd + 1) * 128],
            scalar1=chvec.ap()[:, 160 + dd:161 + dd],
            scalar2=chvec.ap()[:, 164 + dd:165 + dd],
            op0=ALU.mult, op1=ALU.add),
          "sV" if dd == ND - 1 else None)
    z3g_sv = val("sV")
    W("tensor", "sV", z3g_sv)

    wout_done = []
    for vi in range(NO):
        slot = vi % 4
        if vi < 4:
            W("gpsimd", *w1_done[(NL - 1) * NF + 12 + vi])
        else:
            W("gpsimd", *wout_done[vi - 4])
        dma("gpsimd", w1sb[slot][:], woutT_d[vi], "sW")
        sw_v = val("sW")
        W("tensor", "sW", sw_v)
        obuf = scr[vi % 2]
        for tc in range(2):
            bank = tc
            W("tensor", f"bf{bank}", val(f"bf{bank}"))
            for d in range(ND):
                TE(lambda t, slot=slot, d=d, tc=tc, bank=bank:
                   t.matmul(pb[bank][:], w1sb[slot].ap()[:, d * 128:(d + 1) * 128],
                            z3fmv[:, 4 * tc:4 * tc + 4, d * 128:(d + 1) * 128],
                            start=(d == 0), stop=(d == ND - 1)),
                   f"bd{bank}" if d == ND - 1 else None)
            if tc == 1:
                wout_done.append((f"bd{bank}", val(f"bd{bank}")))
            W("vector", f"bd{bank}", val(f"bd{bank}"))
            if tc == 0:
                W("vector", *xfl[vi])
                for sv in st_done[vi - 2] if vi >= 2 else []:
                    W("vector", *sv)
            V(lambda v, vi=vi, tc=tc, bank=bank, obuf=obuf:
              v.scalar_tensor_tensor(out=obuf.ap()[:, tc * 512:(tc + 1) * 512],
                                     in0=pb[bank][:], scalar=boutc.ap()[:, vi:vi + 1],
                                     in1=g.ap()[:, vi * T + tc * 512:vi * T + (tc + 1) * 512],
                                     op0=ALU.add, op1=ALU.mult), f"bf{bank}")
            V(lambda v, vi=vi, tc=tc, obuf=obuf:
              v.tensor_add(out=obuf.ap()[:, tc * 512:(tc + 1) * 512],
                           in0=obuf.ap()[:, tc * 512:(tc + 1) * 512],
                           in1=xslot(vi, tc)),
              "sV")
            if tc == 1:
                comb_sv[vi] = val("sV")
                if vi + 8 < NO:
                    emit_xfm_load(vi + 8)
            rq = ["sync", "gpsimd"][(vi * 2 + tc) % 2]
            W(rq, "sV", val("sV"))
            dma(rq, outT_d[vi * 128:(vi + 1) * 128, tc * 512:(tc + 1) * 512],
                obuf.ap()[:, tc * 512:(tc + 1) * 512], stq_sem[rq])
            st_done[vi].append((stq_sem[rq], val(stq_sem[rq])))

    W("gpsimd", "sO", val("sO"))
    W("gpsimd", "sO2", val("sO2"))
    W("gpsimd", "sO3", val("sO3"))

    with nc.Block() as block:
        @block.sync
        def _(sync):
            for f in ops["sync"]:
                f(sync)

        @block.scalar
        def _(scalar):
            for f in ops["scalar"]:
                f(scalar)

        @block.vector
        def _(vector):
            for f in ops["vector"]:
                f(vector)

        @block.tensor
        def _(tensor):
            for f in ops["tensor"]:
                f(tensor)

        @block.gpsimd
        def _(gpsimd):
            for f in ops["gpsimd"]:
                f(gpsimd)

    es.close()
    return nc


_NC_CACHE = {}


def _host_layouts(inputs):
    f32 = np.float32
    W1 = inputs["L0_attn_W1"].astype(f32)
    W2 = inputs["L0_attn_W2"].astype(f32)
    Win = inputs["L0_attn_Win"].astype(f32)
    Wout = inputs["L0_attn_Wout"].astype(f32)
    baseW = inputs["L0_ffn_base_W"].astype(f32)
    routW = inputs["L0_ffn_router_W"].astype(f32)
    Am = inputs["L0_ffn_A"].astype(f32)
    Bm = inputs["L0_ffn_B"].astype(f32)

    c = np.ascontiguousarray
    out = {}
    WinT = Win.T.reshape(2, 16, 128, 512)
    out["winT"] = c(WinT.transpose(0, 2, 1, 3).reshape(2, 128, 16 * 512))
    BT = baseW.T.reshape(NV, 128, NO, 128)
    out["baseT"] = c(BT.transpose(2, 1, 0, 3).reshape(NO, 128, NV * 128))
    AT = Am.reshape(128, D).T.reshape(2, 16, 128, 128)
    out["atT"] = c(AT.transpose(0, 2, 1, 3).reshape(2, 128, 16 * 128))
    out["b2T"] = c(Bm.transpose(0, 2, 1).reshape(128, D))
    RT = np.zeros((D, 16), f32)
    RT[:, :15] = routW.T
    out["routerT"] = c(RT.reshape(NV, 128, 16).transpose(1, 0, 2).reshape(128, NV * 16))
    W1T = W1.transpose(0, 2, 1).reshape(NL, ND, 128, NF, 128)
    out["w1T"] = c(W1T.transpose(0, 3, 2, 1, 4).reshape(NL, NF, 128, ND * 128))
    W2T = W2.transpose(0, 2, 1).reshape(NL, NF, 128, DIM)
    out["w2T"] = c(W2T.transpose(0, 2, 1, 3).reshape(NL, 128, NF * DIM))
    WoT = Wout.T.reshape(ND, 128, NO, 128)
    out["woutT"] = c(WoT.transpose(2, 1, 0, 3).reshape(NO, 128, ND * 128))
    out["binr"] = c(inputs["L0_attn_bin"].astype(f32).reshape(1, DIM))
    out["b1c"] = c(inputs["L0_attn_b1"].astype(f32).reshape(NL, NF, 128)
                   .transpose(2, 0, 1).reshape(128, NL * NF))
    out["b2r"] = c(inputs["L0_attn_b2"].astype(f32).reshape(1, NL * DIM))
    out["basebc"] = c(inputs["L0_ffn_base_b"].astype(f32).reshape(NO, 128).T)
    out["boutc"] = c(inputs["L0_attn_bout"].astype(f32).reshape(NO, 128).T)
    rb = np.full((1, 16), -1e30, f32)
    rb[0, :15] = inputs["L0_ffn_router_b"].astype(f32)
    out["routbr"] = rb
    em = np.full((1, 16), -1.0, f32)
    em[0, :15] = inputs["L0_ffn_expert_map"].astype(f32)
    out["emr"] = em
    ch = np.zeros((128, 176), f32)
    packs = [("ln1w", inputs["L0_attn_ln1_w"]), ("ln1b", inputs["L0_attn_ln1_b"]),
             ("ln2w", inputs["L0_attn_ln2_w"]), ("ln2b", inputs["L0_attn_ln2_b"]),
             ("scl", inputs["L0_attn_sc_lin"]), ("pre", inputs["L0_attn_phz_re"]),
             ("pim", inputs["L0_attn_phz_im"]), ("lcr", inputs["L0_attn_lc_re"]),
             ("lci", inputs["L0_attn_lc_im"])]
    offs = {"ln1w": 0, "ln1b": 4, "ln2w": 8, "ln2b": 12, "scl": 16,
            "pre": 20, "pim": 24, "lcr": 28, "lci": 32}
    for l in range(NL):
        for name, arr in packs:
            a4 = np.asarray(arr, f32)[l].reshape(4, 128).T
            ch[:, l * 40 + offs[name]:l * 40 + offs[name] + 4] = a4
    ch[:, 160:164] = np.asarray(inputs["L0_attn_lnl_w"], f32).reshape(4, 128).T
    ch[:, 164:168] = np.asarray(inputs["L0_attn_lnl_b"], f32).reshape(4, 128).T
    out["chvec"] = c(ch)
    out["iota"] = c(np.broadcast_to(np.arange(T, dtype=f32)[None, :], (128, T)))
    cst = np.zeros((128, 8), f32)
    cst[:, 0] = 1e-5
    cst[:, 1] = PIH
    out["consts"] = cst
    out["e16"] = (2.0 * (np.arange(128)[None, :] // 16 ==
                         np.arange(8)[:, None])).astype(BF)
    out["identity"] = np.eye(128).astype(BF)
    return out


def kernel(**inputs):
    shared = _host_layouts(inputs)
    x = np.asarray(inputs["L0_attn_x"], np.float32)
    in_maps = []
    for core in range(8):
        b, q = core // 4, core % 4
        xs = np.ascontiguousarray(x[b, q * T:(q + 1) * T, :])
        xfs = np.ascontiguousarray(xs.T)
        mt = np.zeros((128, 16), np.float32)
        mt[:, 0] = 1024.0 * q
        for j in range(8):
            jb, jq = j // 4, j % 4
            if jb == b and jq < q:
                mt[:, 1 + j] = 1024.0 * (q - 1 - jq)
            else:
                mt[:, 1 + j] = 1e30
        mt[:, 9] = 1024.0 * q
        m = dict(shared)
        m["x_tm"] = xs
        m["x_fm"] = xfs
        m["meta"] = mt
        in_maps.append(m)

    if "nc" not in _NC_CACHE:
        _NC_CACHE["nc"] = build_program()
    nc = _NC_CACHE["nc"]
    res = run_bass_kernel_spmd(nc, in_maps, core_ids=list(range(8)))
    _NC_CACHE["res"] = res
    out = np.empty((2, 4096, D), np.float32)
    for core in range(8):
        b, q = core // 4, core % 4
        out[b, q * T:(q + 1) * T, :] = np.asarray(res.results[core]["outT"]).T
    return out


def check_sched():
    from collections import defaultdict
    streams = defaultdict(list)
    for ev in SCHED:
        streams[ev[0]].append(ev)
    pcs = {e: 0 for e in streams}
    semv = defaultdict(int)
    progress = True
    while progress:
        progress = False
        for e, lst in streams.items():
            while pcs[e] < len(lst):
                ev = lst[pcs[e]]
                if ev[1] == 'w':
                    if semv[ev[2]] >= ev[3]:
                        pcs[e] += 1
                        progress = True
                    else:
                        break
                else:
                    semv[ev[2]] += ev[3]
                    pcs[e] += 1
                    progress = True
    stuck = {e: (pcs[e], streams[e][pcs[e]]) for e in streams
             if pcs[e] < len(streams[e])}
    return stuck, semv


if __name__ == "__main__":
    nc = build_program()
    print("build ok:", nc.n_instructions(), "instructions")
    stuck, semv = check_sched()
    if stuck:
        print("DEADLOCK: blocked engines:")
        for e, (pc, ev) in stuck.items():
            print(f"  {e} at sched-op {pc}: waits {ev[2]} >= {ev[3]} (now {semv[ev[2]]})")
    else:
        print("schedule OK (no deadlock)")
